# version 03
# baseline (speedup 1.0000x reference)
import math
from collections import defaultdict
from contextlib import ExitStack

import numpy as np
import ml_dtypes

import concourse.bass as bass
import concourse.mybir as mybir
from concourse.bass_utils import run_bass_kernel_spmd

F32 = mybir.dt.float32
BF16 = mybir.dt.bfloat16
I8 = mybir.dt.int8
I32 = mybir.dt.int32
AF = mybir.ActivationFunctionType
ALU = mybir.AluOpType
AX = mybir.AxisListType

D = 2048
TOK = 1024
NT = 8
NI = D // 128
H = 16
DFF = 8192
NO = DFF // 128
ISQ = 1.0 / math.sqrt(128.0)
N_SUB = 7
L0F_CUT = 5
L0F_SKIP_AG = False
L0F_SKIP_KV = False
L0F_SKIP_Q = False
L0F_LOAD_ONLY = False
L0F_SKIP_GATE = False
LNT_MODE = 0
L1F_CUT = 3
L1FA_MODE = 0
SKIP_PRELOAD = False
SKIP_INIT = False

ENG = {"t": "tensor", "v": "vector", "s": "scalar", "g": "gpsimd", "y": "sync"}


class Q:

    def __init__(self, nc, stk):
        self.nc = nc
        self.buf = {k: [] for k in ENG}
        self.meta = []
        self.cnt = defaultdict(int)
        self.sems = {}
        self.waited = {k: defaultdict(int) for k in ENG}
        self.stk = stk

    def sem(self, name):
        self.sems[name] = self.stk.enter_context(self.nc.semaphore(name))

    def op(self, key, f, inc=None, wait=None, tag="", lock=False):
        if lock:
            wait = list(wait or []) + [(key, self.cnt[key])]
            if inc is None:
                inc = (key, 1)
            else:
                assert inc[0] == key
        waits = []
        for w in (wait or []):
            sname, val = w if isinstance(w, tuple) else (w, None)
            if val is None:
                val = self.cnt[sname]
            if val > self.waited[key][sname]:
                waits.append((sname, val))
                self.waited[key][sname] = val
        incs = None
        if inc:
            sname, k = inc
            self.cnt[sname] += k
            incs = (sname, k)
        self.meta.append((key, waits, incs, tag))

        def fn(eng, waits=waits, incs=incs, f=f):
            for sname, val in waits:
                eng.wait_ge(self.sems[sname], val)
            instr = f(eng)
            if incs is not None:
                instr.then_inc(self.sems[incs[0]], incs[1])
        self.buf[key].append(fn)

    def flush(self):
        bufs = {k: v for k, v in self.buf.items() if v}
        self.buf = {k: [] for k in ENG}
        if not bufs:
            return
        with self.nc.Block() as blk:
            for key, fns in bufs.items():
                getattr(blk, ENG[key])(lambda eng, fns=fns: [f(eng) for f in fns])

    def audit(self):
        qs = {k: [] for k in ENG}
        for key, waits, incs, tag in self.meta:
            qs[key].append((waits, incs, tag))
        pos = {k: 0 for k in ENG}
        cnt = defaultdict(int)
        progress = True
        while progress:
            progress = False
            for k in ENG:
                while pos[k] < len(qs[k]):
                    waits, incs, tag = qs[k][pos[k]]
                    if any(cnt[s] < v for s, v in waits):
                        break
                    if incs:
                        cnt[incs[0]] += incs[1]
                    pos[k] += 1
                    progress = True
        stuck = {k: (len(qs[k]) - pos[k], qs[k][pos[k]]) for k in ENG if pos[k] < len(qs[k])}
        if stuck:
            raise RuntimeError(f"deadlock audit failed: {stuck}")


def build(n_sub=None):
    n_sub = N_SUB if n_sub is None else n_sub
    nc = bass.Bass("TRN2", target_bir_lowering=False, detect_race_conditions=False)

    xin = nc.dram_tensor("xin", [TOK, D], F32, kind="ExternalInput")
    wg = nc.dram_tensor("wg", [D, 16], F32, kind="ExternalInput")
    wqkv = nc.dram_tensor("wqkv", [D, 3 * D], F32, kind="ExternalInput")
    bt = nc.dram_tensor("bt", [D, 1024], F32, kind="ExternalInput")
    ct = nc.dram_tensor("ct", [1024, D], F32, kind="ExternalInput")
    ddiag = nc.dram_tensor("ddiag", [4, 128, 4, 512], F32, kind="ExternalInput")
    wkt = nc.dram_tensor("wkt", [D, D], F32, kind="ExternalInput")
    wqt = nc.dram_tensor("wqt", [D, D], F32, kind="ExternalInput")
    wvt = nc.dram_tensor("wvt", [D, D], F32, kind="ExternalInput")
    wot = nc.dram_tensor("wot", [D, D], F32, kind="ExternalInput")
    w1t = nc.dram_tensor("w1t", [D, DFF], F32, kind="ExternalInput")
    w2t = nc.dram_tensor("w2t", [DFF, D], F32, kind="ExternalInput")
    g1b_h = nc.dram_tensor("g1b", [128, D], F32, kind="ExternalInput")
    g2b_h = nc.dram_tensor("g2b", [128, DFF], F32, kind="ExternalInput")
    id_h = nc.dram_tensor("identf", [128, 128], F32, kind="ExternalInput")
    idb_h = nc.dram_tensor("identb", [128, 128], BF16, kind="ExternalInput")
    mka_h = nc.dram_tensor("maska", [128, 128], F32, kind="ExternalInput")
    mkb_h = nc.dram_tensor("maskb", [128, 128], F32, kind="ExternalInput")
    cst_h = nc.dram_tensor("cst", [128, 8], F32, kind="ExternalInput")
    out_h = nc.dram_tensor("out", [TOK, D], F32, kind="ExternalOutput")
    dbg_h = nc.dram_tensor("dbg", [2048, 1024], F32, kind="ExternalOutput") \
        if globals().get("DEBUG_OUT", False) else None

    xres = nc.dram_tensor("xres", [TOK, D], F32)
    l0kv_in = [nc.dram_tensor(f"l0kvi{c}", [TOK, 256], F32) for c in range(2)]
    l0kv_out = [nc.dram_tensor(f"l0kvo{c}", [2 * TOK, 256], F32) for c in range(2)]
    kg_in = nc.dram_tensor("kgi", [D, TOK], F32)
    kg_out = nc.dram_tensor("kgo", [2 * D, TOK], F32)
    vg_in = nc.dram_tensor("vgi", [TOK, D], F32)
    vg_out = nc.dram_tensor("vgo", [2 * TOK, D], F32)
    w1q_h = nc.dram_tensor("w1qd", [D, DFF], I8)
    w2q_h = nc.dram_tensor("w2qd", [DFF, D], I8)
    hbuf = nc.dram_tensor("hbuf", [TOK, DFF], F32)

    stk = ExitStack()
    with stk:
        q = Q(nc, stk)
        for s in ["dy", "dg", "cc", "t", "v", "s", "g"]:
            q.sem(s)
        REPG = [[0, 1], [2, 3], [4, 5], [6, 7]]
        sbp = lambda name, shape, dt=F32: stk.enter_context(nc.sbuf_tensor(name, shape, dt))

        ident = sbp("ident_s", [128, 128])
        identb = sbp("identb_s", [128, 128], BF16)
        maska = sbp("maska_s", [128, 128])
        maskb = sbp("maskb_s", [128, 128])
        cst = sbp("cst_s", [128, 8])
        onesc = sbp("onesc", [128, 1])
        onesr = sbp("onesr", [1, 128])
        swcl = sbp("swcl", [128, 4])

        if not SKIP_INIT:
            for t_sb, t_h in [(ident, id_h), (identb, idb_h), (maska, mka_h),
                              (maskb, mkb_h), (cst, cst_h)]:
                q.op("y", lambda y, a=t_sb, b=t_h: y.dma_start(out=a[:], in_=b.ap()), inc=("dy", 16))
            q.op("v", lambda v: v.memset(onesc[:], 1.0), inc=("v", 1))
            q.op("v", lambda v: v.memset(onesr[:], 1.0), wait=[("dy", None)], inc=("v", 1))

        def ln_transpose(xsrc, x_lnT, sp):
            with ExitStack() as ps:
                e = ps.enter_context
                xt = [e(nc.sbuf_tensor(f"{sp}_xt{i}", [128, D], F32)) for i in range(2)]
                lnb = e(nc.sbuf_tensor(f"{sp}_lnb", [128, D], F32))
                sq = e(nc.sbuf_tensor(f"{sp}_sq", [128, D], F32))
                st = e(nc.sbuf_tensor(f"{sp}_st", [128, 12], F32))
                pst = [e(nc.psum_tensor(f"{sp}_pt{i}", [128, 4, 128], F32)) for i in range(2)]
                if not SKIP_PRELOAD:
                    q.op("s", lambda s: s.preload_act_table(AF.Sqrt), inc=("s", 1))
                for j in range(globals().get("LNT_NJ", NT)):
                    b = xt[j % 2]
                    q.op("y", lambda y, b=b, j=j: y.dma_start(
                        out=b[:], in_=xsrc.ap()[j * 128:(j + 1) * 128, :]),
                        inc=("dy", 16), wait=[("v", None), ("s", None), ("t", None)],
                        tag=f"{sp} lnload {j}")
                    dyv = q.cnt["dy"]
                    if LNT_MODE == 2:
                        sv = q.cnt["s"]
                        for i0 in range(0, NI, 4):
                            pp = pst[(i0 // 4) % 2]
                            for i in range(i0, i0 + 4):
                                q.op("t", lambda t, pp=pp, i=i, i0=i0, b=b: t.transpose(
                                    pp[:, i - i0, :], b[:, i * 128:(i + 1) * 128], ident[:]),
                                    wait=[("dy", dyv), ("v", None)] if i == i0 else None,
                                    inc=("t", 1) if i == i0 + 3 else None)
                            q.op("v", lambda v, pp=pp, i0=i0, j=j: v.tensor_copy(
                                x_lnT[:, i0:i0 + 4, j * 128:(j + 1) * 128], pp[:]),
                                wait=[("t", None)], inc=("v", 1))
                        continue
                    q.op("v", lambda v, b=b: v.reduce_sum(st[:, 0:1], b[:], AX.X),
                         wait=[("dy", dyv)], lock=True)
                    q.op("s", lambda s, b=b: s.activation(sq[:], b[:], AF.Square,
                                                          accum_out=st[:, 1:2]),
                         wait=[("dy", dyv)], inc=("s", 1))
                    q.op("v", lambda v: v.tensor_scalar(st[:, 2:3], st[:, 0:1], 1.0 / D, None, ALU.mult),
                         wait=[("s", None)], lock=True)
                    q.op("v", lambda v: v.tensor_scalar(st[:, 3:4], st[:, 1:2], 1.0 / D, None, ALU.mult),
                         lock=True)
                    q.op("v", lambda v: v.tensor_tensor(st[:, 4:5], st[:, 2:3], st[:, 2:3], ALU.mult),
                         lock=True)
                    q.op("v", lambda v: v.tensor_tensor(st[:, 5:6], st[:, 3:4], st[:, 4:5], ALU.subtract),
                         lock=True)
                    q.op("v", lambda v: v.tensor_scalar(st[:, 5:6], st[:, 5:6], 1e-5, None, ALU.add),
                         lock=True)
                    q.op("s", lambda s: s.activation(st[:, 6:7], st[:, 5:6], AF.Sqrt),
                         wait=[("v", None)], inc=("s", 1))
                    q.op("v", lambda v: v.reciprocal(st[:, 7:8], st[:, 6:7]), wait=[("s", None)],
                         lock=True)
                    q.op("v", lambda v: v.scalar_tensor_tensor(
                        st[:, 8:9], st[:, 2:3], -1.0, st[:, 7:8], ALU.mult, ALU.mult),
                        lock=True)
                    q.op("s", lambda s, b=b: s.activation(lnb[:], b[:], AF.Identity,
                                                          bias=st[:, 8:9], scale=st[:, 7:8]),
                         wait=[("v", None), ("t", None)], inc=("s", 1))
                    sv = q.cnt["s"]
                    if LNT_MODE == 1:
                        q.op("v", lambda v, j=j: v.memset(
                            x_lnT[:, :, j * 128:(j + 1) * 128], 0.01),
                            wait=[("s", sv)], inc=("v", 1))
                        continue
                    for i0 in range(0, NI, 4):
                        pp = pst[(i0 // 4) % 2]
                        for i in range(i0, i0 + 4):
                            q.op("t", lambda t, pp=pp, i=i, i0=i0: t.transpose(
                                pp[:, i - i0, :], lnb[:, i * 128:(i + 1) * 128], ident[:]),
                                wait=[("s", sv), ("v", None)] if i == i0 else None,
                                inc=("t", 1) if i == i0 + 3 else None)
                        q.op("v", lambda v, pp=pp, i0=i0, j=j: v.tensor_copy(
                            x_lnT[:, i0:i0 + 4, j * 128:(j + 1) * 128], pp[:]),
                            wait=[("t", None)], inc=("v", 1))
                q.flush()

        def make_attn_bufs(sp, e):
            return {
                "s0": e(nc.psum_tensor(f"{sp}_s0", [128, 1024], F32)),
                "s1": e(nc.psum_tensor(f"{sp}_s1", [128, 1024], F32)),
                "ptp": [e(nc.psum_tensor(f"{sp}_ptp{i}", [128, 4, 128], F32)) for i in range(2)],
                "psy": e(nc.psum_tensor(f"{sp}_psy", [128, 128], F32)),
                "psb": e(nc.sbuf_tensor(f"{sp}_Pp", [128, 2, 8, 128], F32)),
                "ptb": e(nc.sbuf_tensor(f"{sp}_PT", [128, 16, 128], F32)),
                "sta": e(nc.sbuf_tensor(f"{sp}_sta", [128, 10], F32)),
            }

        def attn_unit(bufs, q_ap, kfm, vtok, ns, y_emit, scale_p, xwaits=None):
            s0, s1 = bufs["s0"], bufs["s1"]
            ptp, psy = bufs["ptp"], bufs["psy"]
            psb, ptb, sta = bufs["psb"], bufs["ptb"], bufs["sta"]
            for r, sps in ((0, s0), (1, s1)):
                for i0 in range(0, ns, 4):
                    w = min(4, ns - i0)
                    wl = ([("s", None), ("v", None)] + (xwaits or [])) if (r == 0 and i0 == 0) else None
                    lastsc = (r == 1 and i0 + 4 >= ns)
                    q.op("t", lambda t, sps=sps, r=r, i0=i0, w=w: t.matmul(
                        sps[:, i0 * 128:(i0 + w) * 128], q_ap,
                        kfm[:, 8 * r + i0:8 * r + i0 + w, :]),
                        wait=wl, inc=("t", 1) if lastsc else None)
            tv = q.cnt["t"]
            q.op("v", lambda v: v.tensor_tensor(
                s0[:, (ns - 1) * 128:ns * 128], s0[:, (ns - 1) * 128:ns * 128],
                maska[:], ALU.add), wait=[("t", tv), ("s", None)], lock=True)
            q.op("v", lambda v: v.tensor_tensor(
                s1[:, (ns - 1) * 128:ns * 128], s1[:, (ns - 1) * 128:ns * 128],
                maskb[:], ALU.add), lock=True)
            q.op("v", lambda v: v.reduce_max(sta[:, 0:1], s0[:, 0:ns * 128], AX.X, negate=True),
                 lock=True)
            q.op("v", lambda v: v.reduce_max(sta[:, 1:2], s1[:, 0:ns * 128], AX.X, negate=True),
                 lock=True)
            q.op("v", lambda v: v.tensor_tensor(sta[:, 0:1], sta[:, 0:1], sta[:, 1:2], ALU.min),
                 lock=True)
            q.op("v", lambda v: v.tensor_scalar(sta[:, 2:3], sta[:, 0:1], ISQ, None, ALU.mult),
                 lock=True)
            vv = q.cnt["v"]
            pieces = []
            for r, sps in ((0, s0), (1, s1)):
                for i0 in range(0, ns, 4):
                    w = min(4, ns - i0)
                    col = 4 + len(pieces)
                    pieces.append(col)
                    q.op("s", lambda s, sps=sps, r=r, i0=i0, w=w, col=col: s.activation(
                        psb[:, r, i0:i0 + w, :],
                        sps[:, i0 * 128:(i0 + w) * 128].rearrange("p (a b) -> p a b", a=w),
                        AF.Exp, bias=sta[:, 2:3], scale=cst[:, 0:1],
                        accum_out=sta[:, col:col + 1]),
                        wait=[("v", vv), ("t", None)])
            q.op("s", lambda s: s.copy(sta[:, 3:4], sta[:, 4:5]), inc=("s", 1))
            sv = q.cnt["s"]
            for col in pieces[1:]:
                q.op("v", lambda v, col=col: v.tensor_tensor(
                    sta[:, 3:4], sta[:, 3:4], sta[:, col:col + 1], ALU.add),
                    wait=[("s", sv)], lock=True)
            q.op("v", lambda v: v.reciprocal(sta[:, 3:4], sta[:, 3:4]), lock=True)
            if scale_p:
                for r in range(2):
                    q.op("v", lambda v, r=r: v.tensor_scalar(
                        psb[:, r, 0:ns, :], psb[:, r, 0:ns, :], sta[:, 3:4], None, ALU.mult),
                        lock=True)
            pv = q.cnt["v"]
            nsl = 2 * ns
            for b0 in range(0, nsl, 4):
                pp = ptp[(b0 // 4) % 2]
                w = min(4, nsl - b0)
                for kk in range(w):
                    slot = b0 + kk
                    r, i = slot // ns, slot % ns
                    q.op("t", lambda t, pp=pp, kk=kk, r=r, i=i: t.transpose(
                        pp[:, kk, :], psb[:, r, i, :], ident[:]),
                        wait=([("v", pv)] if (b0 == 0 and kk == 0)
                              else ([("v", None)] if kk == 0 else None)),
                        inc=("t", 1) if kk == w - 1 else None)
                q.op("v", lambda v, pp=pp, b0=b0, w=w: v.tensor_copy(
                    ptb[:, b0:b0 + w, :], pp[:, 0:w, :]),
                    wait=[("t", None)], inc=("v", 1))
            ptv = q.cnt["v"]
            for slot in range(nsl):
                r, i = slot // ns, slot % ns
                vslot = 8 * r + i
                first, last = slot == 0, slot == nsl - 1
                if scale_p:
                    q.op("t", lambda t, vslot=vslot, slot=slot, first=first, last=last: t.matmul(
                        psy[:], vtok[:, vslot, :], ptb[:, slot, :], start=first, stop=last),
                        wait=[("v", ptv)] if first else None, inc=("t", 1) if last else None)
                else:
                    q.op("t", lambda t, vslot=vslot, slot=slot, first=first, last=last: t.matmul(
                        psy[:], ptb[:, slot, :], vtok[:, vslot, :], start=first, stop=last),
                        wait=[("v", ptv)] if first else None, inc=("t", 1) if last else None)
            y_emit(psy, sta[:, 3:4])

        def l0f(c, first):
            sp = f"l0f{c}"
            xsrc = xin if first else xres
            with ExitStack() as ps:
                e = ps.enter_context
                q_all = e(nc.sbuf_tensor(f"{sp}_qall", [128, H, NT, 128], F32))
                kfm = e(nc.sbuf_tensor(f"{sp}_kfm", [128, 16, 128], F32))
                vtok = e(nc.sbuf_tensor(f"{sp}_vtok", [128, 16, 128], F32))
                pa = ExitStack()
                e1 = pa.enter_context
                x_lnT = e1(nc.sbuf_tensor(f"{sp}_xlnT", [128, NI, TOK], F32))
                ln_transpose(xsrc, x_lnT, sp)
                q.op("s", lambda s: s.preload_act_table(AF.Exp), inc=("s", 1))
                if dbg_h is not None and c == 0 and globals().get("DBG_WHAT", "xlnT") == "xlnT":
                    q.op("y", lambda y: y.dma_start(
                        out=dbg_h.ap().rearrange("(c p) n -> p c n", p=128), in_=x_lnT[:]),
                        wait=[("v", None)], inc=("dy", 16), tag="dbg xlnT")
                if L0F_CUT < 2:
                    pa.close()
                    return

                kvsh = e1(nc.sbuf_tensor(f"{sp}_kvsh", [128, NT, 256], F32))
                zind = e1(nc.sbuf_tensor(f"{sp}_zind", [128, NT, 16], F32))
                zgt = e1(nc.sbuf_tensor(f"{sp}_zgt", [128, NT, 16], F32))
                zst = e1(nc.sbuf_tensor(f"{sp}_zst", [128, NT, 4], F32))
                if not globals().get("SKIP_KVSH_MS", False):
                    q.op("v", lambda v: v.memset(kvsh[:], 0.0), inc=("v", 1))

                with ExitStack() as p2:
                    e2 = p2.enter_context
                    wgt = e2(nc.sbuf_tensor(f"{sp}_wgt", [128, NI, 16], F32))
                    psz = e2(nc.psum_tensor(f"{sp}_psz", [128, 16], F32))
                    zsc = e2(nc.sbuf_tensor(f"{sp}_zsc", [128, 16], F32))
                    if not globals().get("SKIP_WGT", False):
                        q.op("y", lambda y: y.dma_start(
                            out=wgt[:], in_=wg.ap().rearrange("(c p) n -> p c n", p=128)),
                            inc=("dy", 16))
                    dyv = q.cnt["dy"]
                    for j in range(NT) if not L0F_SKIP_GATE else []:
                        for i in range(NI):
                            q.op("t", lambda t, i=i, j=j: t.matmul(
                                psz[:], x_lnT[:, i, j * 128:(j + 1) * 128], wgt[:, i, :],
                                start=(i == 0), stop=(i == NI - 1)),
                                wait=[("dy", dyv), ("v", None)] if i == 0 else None,
                                inc=("t", 1) if i == NI - 1 else None,
                                tag=f"{sp} z {j}")
                        q.op("v", lambda v, j=j: v.tensor_copy(zind[:, j, :], psz[:]),
                             wait=[("t", None)], lock=True)
                        q.op("v", lambda v, j=j: v.reduce_max(zst[:, j, 0:1], zind[:, j, :],
                                                              AX.X, negate=True), lock=True)
                        q.op("v", lambda v, j=j: v.tensor_scalar(
                            zst[:, j, 1:2], zst[:, j, 0:1], -1.0, None, ALU.mult), lock=True)
                        q.op("s", lambda s, j=j: s.activation(
                            zsc[:], zind[:, j, :], AF.Exp, bias=zst[:, j, 0:1],
                            scale=cst[:, 1:2],
                            accum_out=zst[:, j, 2:3]), wait=[("v", None)], inc=("s", 1))
                        q.op("v", lambda v, j=j: v.tensor_scalar(
                            zind[:, j, :], zind[:, j, :], zst[:, j, 1:2], None, ALU.is_ge),
                            wait=[("s", None)], lock=True)
                        q.op("v", lambda v, j=j: v.reciprocal(zst[:, j, 3:4], zst[:, j, 2:3]),
                            lock=True)
                        q.op("v", lambda v, j=j: v.tensor_scalar(
                            zgt[:, j, :], zind[:, j, :], zst[:, j, 3:4], None, ALU.mult),
                            lock=True)
                    q.flush()
                if L0F_CUT < 3:
                    pa.close()
                    return

                wsl = e1(nc.sbuf_tensor(f"{sp}_wsl", [128, NI, 512], F32))
                with ExitStack() as p2:
                    e2 = p2.enter_context
                    pskv = [e2(nc.psum_tensor(f"{sp}_pskv{i}", [128, 512], F32)) for i in range(2)]
                    oglist = ([] if L0F_SKIP_KV else list(range(4, 12))) + \
                             ([] if L0F_SKIP_Q else list(range(0, 4)))
                    oglist = oglist[:globals().get("L0F_NOG", 99)]
                    for ogi, og in enumerate(oglist):
                        tv = q.cnt["t"]
                        q.op("y", lambda y, og=og: y.dma_start(
                            out=wsl[:], in_=wqkv.ap()[:, og * 512:(og + 1) * 512]
                            .rearrange("(c p) n -> p c n", p=128)),
                            inc=("dy", 16), wait=[("t", tv)], tag=f"{sp} wsl {og}")
                        dyv = q.cnt["dy"]
                        if L0F_LOAD_ONLY:
                            continue
                        if og >= 4:
                            sel = zind if og < 8 else zgt
                            dst0 = 0 if og < 8 else 128
                            for j in range(NT):
                                pp = pskv[j % 2]
                                for i in range(NI):
                                    q.op("t", lambda t, pp=pp, i=i, j=j: t.matmul(
                                        pp[:], x_lnT[:, i, j * 128:(j + 1) * 128], wsl[:, i, :],
                                        start=(i == 0), stop=(i == NI - 1)),
                                        wait=[("dy", dyv), ("v", None)] if i == 0 else None,
                                        inc=("t", 1) if i == NI - 1 else None,
                                        tag=f"{sp} kv {og}.{j}")
                                tv2 = q.cnt["t"]
                                for hh in range(4):
                                    head = (og - (4 if og < 8 else 8)) * 4 + hh
                                    q.op("v", lambda v, pp=pp, hh=hh, head=head, j=j, dst0=dst0,
                                         sel=sel: v.scalar_tensor_tensor(
                                             kvsh[:, j, dst0:dst0 + 128],
                                             pp[:, hh * 128:(hh + 1) * 128],
                                             sel[:, j, head:head + 1],
                                             kvsh[:, j, dst0:dst0 + 128],
                                             ALU.mult, ALU.add),
                                         wait=[("t", tv2)] if hh == 0 else None,
                                         lock=True)
                        else:
                            ngrp = globals().get("L0F_NGRP", 8)
                            for oc in range(4):
                                head = og * 4 + oc
                                for jh in range(2):
                                    if oc * 2 + jh >= ngrp:
                                        continue
                                    pp = pskv[(oc * 2 + jh) % 2]
                                    for i in range(NI):
                                        q.op("t", lambda t, pp=pp, i=i, oc=oc, jh=jh: t.matmul(
                                            pp[:], wsl[:, i, oc * 128:(oc + 1) * 128],
                                            x_lnT[:, i, jh * 512:(jh + 1) * 512],
                                            start=(i == 0), stop=(i == NI - 1)),
                                            wait=[("dy", dyv), ("v", None)] if i == 0 else None,
                                            inc=("t", 1) if i == NI - 1 else None,
                                            tag=f"{sp} q {head}.{jh}")
                                    if not globals().get("SKIP_QCOPY", False):
                                        q.op("v", lambda v, pp=pp, head=head, jh=jh: v.tensor_copy(
                                            q_all[:, head, jh * 4:(jh + 1) * 4, :],
                                            pp[:].rearrange("p (a b) -> p a b", a=4)),
                                            wait=[("t", None)], inc=("v", 1))
                        if og == 4 and dbg_h is not None and c == 0 and \
                                globals().get("DBG_WHAT") == "gate":
                            q.op("y", lambda y: y.dma_start(
                                out=dbg_h.ap()[0:128, 0:128].rearrange("p (a b) -> p a b", a=8),
                                in_=zind[:]), wait=[("v", None)], inc=("dy", 16))
                            q.op("y", lambda y: y.dma_start(
                                out=dbg_h.ap()[0:128, 128:256].rearrange("p (a b) -> p a b", a=8),
                                in_=zgt[:]), inc=("dy", 16))
                            q.op("y", lambda y: y.dma_start(
                                out=dbg_h.ap()[0:128, 256:288].rearrange("p (a b) -> p a b", a=8),
                                in_=zst[:]), inc=("dy", 16))
                        if og == 11 and dbg_h is not None and c == 0 and \
                                globals().get("DBG_WHAT") == "kvsh":
                            q.op("y", lambda y: y.dma_start(
                                out=dbg_h.ap()[0:1024, 0:256].rearrange("(a p) n -> p a n", p=128),
                                in_=kvsh[:]), wait=[("v", None)], inc=("dy", 16), tag="dbg kvsh")
                        if og == 11 and not L0F_SKIP_AG:
                            q.op("y", lambda y: y.dma_start(
                                out=l0kv_in[c].ap().rearrange("(a p) n -> p a n", p=128),
                                in_=kvsh[:]), inc=("dy", 16), wait=[("v", None)])
                            dyb = q.cnt["dy"]
                            q.op("g", lambda g: g.collective_compute(
                                "AllGather", ALU.bypass, replica_groups=REPG,
                                ins=[l0kv_in[c].ap().opt()], outs=[l0kv_out[c].ap().opt()]),
                                inc=("cc", 1), wait=[("dy", dyb)], tag=f"{sp} AG")
                    q.flush()

                pa.close()
                if L0F_CUT < 4:
                    return
                with ExitStack() as p2:
                    e2 = p2.enter_context
                    ktok = e2(nc.sbuf_tensor(f"{sp}_ktok", [128, 16, 128], F32))
                    ptk = [e2(nc.psum_tensor(f"{sp}_ptk{i}", [128, 4, 128], F32)) for i in range(2)]
                    ccv = q.cnt["cc"]
                    q.op("y", lambda y: y.dma_start(
                        out=ktok[:], in_=l0kv_out[c].ap()[:, 0:128]
                        .rearrange("(a p) n -> p a n", p=128)), inc=("dy", 16),
                        wait=[("cc", ccv), ("t", None), ("v", None), ("s", None)])
                    q.op("y", lambda y: y.dma_start(
                        out=vtok[:], in_=l0kv_out[c].ap()[:, 128:256]
                        .rearrange("(a p) n -> p a n", p=128)), inc=("dy", 16))
                    dyv = q.cnt["dy"]
                    for b0 in range(0, 16, 4):
                        pp = ptk[(b0 // 4) % 2]
                        for kk in range(4):
                            q.op("t", lambda t, pp=pp, kk=kk, b0=b0: t.transpose(
                                pp[:, kk, :], ktok[:, b0 + kk, :], ident[:]),
                                wait=[("dy", dyv), ("v", None)] if kk == 0 else None,
                                inc=("t", 1) if kk == 3 else None)
                        q.op("v", lambda v, pp=pp, b0=b0: v.tensor_copy(
                            kfm[:, b0:b0 + 4, :], pp[:]), wait=[("t", None)], inc=("v", 1))
                    q.flush()

                if L0F_CUT < 5:
                    return
                ytile = e(nc.sbuf_tensor(f"{sp}_yt", [128, D], F32))
                xt = e(nc.sbuf_tensor(f"{sp}_xtr", [128, D], F32))
                with ExitStack() as p2:
                    bufs = make_attn_bufs(sp, p2.enter_context)
                    for j in range(NT):
                        for h in range(H):
                            def y_emit(psy, rme, h=h, j=j):
                                q.op("v", lambda v, rme=rme: v.tensor_scalar(
                                    swcl[:, 2:3], rme, 0.5, None, ALU.mult),
                                    wait=[("t", None)], lock=True)
                                q.op("v", lambda v, h=h, psy=psy: v.tensor_scalar(
                                    ytile[:, h * 128:(h + 1) * 128], psy[:], swcl[:, 2:3],
                                    None, ALU.mult), lock=True)
                                if h == H - 1:
                                    q.op("y", lambda y, j=j: y.dma_start(
                                        out=xt[:], in_=xsrc.ap()[j * 128:(j + 1) * 128, :]),
                                        inc=("dy", 16))
                                    q.op("v", lambda v: v.tensor_tensor(
                                        xt[:], xt[:], ytile[:], ALU.add),
                                        wait=[("dy", None)], inc=("v", 1))
                                    q.op("y", lambda y, j=j: y.dma_start(
                                        out=xres.ap()[j * 128:(j + 1) * 128, :], in_=xt[:]),
                                        wait=[("v", None)], inc=("dy", 16), tag=f"{sp} xst {j}")
                            attn_unit(bufs, q_all[:, h, j, :], kfm, vtok, j + 1,
                                      y_emit, scale_p=False)
                    q.flush()

        def l0a():
            sp = "l0a"
            with ExitStack() as ps:
                e = ps.enter_context
                x_lnT = e(nc.sbuf_tensor(f"{sp}_xlnT", [128, NI, TOK], F32))
                ln_transpose(xres, x_lnT, sp)
                q.op("s", lambda s: s.preload_act_table(AF.Tanh), inc=("s", 1))
                stfm = e(nc.sbuf_tensor(f"{sp}_stfm", [128, 8, TOK], F32))
                with ExitStack() as p2:
                    e2 = p2.enter_context
                    wsl = e2(nc.sbuf_tensor(f"{sp}_wsl", [128, NI, 512], F32))
                    pp2 = [e2(nc.psum_tensor(f"{sp}_ps{i}", [128, 512], F32)) for i in range(2)]
                    for og in range(2):
                        tv = q.cnt["t"]
                        q.op("y", lambda y, og=og: y.dma_start(
                            out=wsl[:], in_=bt.ap()[:, og * 512:(og + 1) * 512]
                            .rearrange("(c p) n -> p c n", p=128)),
                            inc=("dy", 16), wait=[("t", tv)])
                        dyv = q.cnt["dy"]
                        for nn in range(4):
                            for jh in range(2):
                                pp = pp2[(nn * 2 + jh) % 2]
                                for i in range(NI):
                                    q.op("t", lambda t, pp=pp, i=i, nn=nn, jh=jh: t.matmul(
                                        pp[:], wsl[:, i, nn * 128:(nn + 1) * 128],
                                        x_lnT[:, i, jh * 512:(jh + 1) * 512],
                                        start=(i == 0), stop=(i == NI - 1)),
                                        wait=[("dy", dyv), ("s", None)] if i == 0 else None,
                                        inc=("t", 1) if i == NI - 1 else None,
                                        tag=f"{sp} B {og}.{nn}.{jh}")
                                q.op("s", lambda s, pp=pp, og=og, nn=nn, jh=jh: s.activation(
                                    stfm[:, og * 4 + nn, jh * 512:(jh + 1) * 512], pp[:],
                                    AF.Tanh), wait=[("t", None)], inc=("s", 1))
                    q.flush()
                with ExitStack() as p2:
                    e2 = p2.enter_context
                    csl = e2(nc.sbuf_tensor(f"{sp}_csl", [128, 8, 512], F32))
                    dsl = e2(nc.sbuf_tensor(f"{sp}_dsl", [128, 4, 512], F32))
                    xts = [e2(nc.sbuf_tensor(f"{sp}_xts{j}", [128, D], F32)) for j in range(NT)]
                    pp2 = [e2(nc.psum_tensor(f"{sp}_pc{i}", [128, 512], F32)) for i in range(2)]
                    for j in range(NT):
                        q.op("y", lambda y, j=j: y.dma_start(
                            out=xts[j][:], in_=xres.ap()[j * 128:(j + 1) * 128, :]),
                            inc=("dy", 16), wait=[("s", None), ("t", None), ("v", None)],
                            tag=f"{sp} xload {j}")
                    for og in range(4):
                        tv = q.cnt["t"]
                        q.op("y", lambda y, og=og: y.dma_start(
                            out=csl[:], in_=ct.ap()[:, og * 512:(og + 1) * 512]
                            .rearrange("(c p) n -> p c n", p=128)),
                            inc=("dy", 16), wait=[("t", tv)])
                        q.op("y", lambda y, og=og: y.dma_start(
                            out=dsl[:], in_=ddiag.ap()[og]), inc=("dy", 16))
                        dyv = q.cnt["dy"]
                        for j in range(NT):
                            pp = pp2[j % 2]
                            for n8 in range(8):
                                q.op("t", lambda t, pp=pp, n8=n8, j=j: t.matmul(
                                    pp[:], stfm[:, n8, j * 128:(j + 1) * 128], csl[:, n8, :],
                                    start=(n8 == 0), stop=False),
                                    wait=[("dy", dyv), ("v", None)] if n8 == 0 else None,
                                    tag=f"{sp} C {j}.{og}")
                            for ii in range(4):
                                i = og * 4 + ii
                                q.op("t", lambda t, pp=pp, i=i, ii=ii, j=j: t.matmul(
                                    pp[:], x_lnT[:, i, j * 128:(j + 1) * 128], dsl[:, ii, :],
                                    start=False, stop=(ii == 3)),
                                    inc=("t", 1) if ii == 3 else None)
                            q.op("v", lambda v, pp=pp, og=og, j=j: v.tensor_tensor(
                                xts[j][:, og * 512:(og + 1) * 512],
                                xts[j][:, og * 512:(og + 1) * 512], pp[:], ALU.add),
                                wait=[("t", None)], inc=("v", 1))
                    for j in range(NT):
                        q.op("y", lambda y, j=j: y.dma_start(
                            out=xres.ap()[j * 128:(j + 1) * 128, :], in_=xts[j][:]),
                            wait=[("v", None)], inc=("dy", 16), tag=f"{sp} xst {j}")
                    q.flush()

        def wq():
            sp = "wq"
            with ExitStack() as ps:
                e = ps.enter_context
                wbuf = [e(nc.sbuf_tensor(f"{sp}_wb{i}", [128, DFF], F32)) for i in range(2)]
                acc = e(nc.sbuf_tensor(f"{sp}_acc", [128, 4], F32))
                ssc = e(nc.sbuf_tensor(f"{sp}_ssc", [1, 4], F32))
                psm = e(nc.psum_tensor(f"{sp}_psm", [1, 2], F32))
                psb = e(nc.psum_tensor(f"{sp}_psb", [128, 2], F32))
                u32 = e(nc.sbuf_tensor(f"{sp}_u32", [128, 2048], I32))
                uf = e(nc.sbuf_tensor(f"{sp}_uf", [128, 2048], F32))
                ti8 = e(nc.sbuf_tensor(f"{sp}_i8", [128, DFF], I8))
                q.op("v", lambda v: v.memset(acc[:], 0.0), inc=("v", 1))
                lastuse = [q.cnt["v"], q.cnt["v"]]
                for wi, (wsrc, nch, w_fd) in enumerate([(w1t, NI, DFF), (w2t, NO, D)]):
                    for chk in range(nch):
                        bi = chk % 2
                        b = wbuf[bi]
                        q.op("y", lambda y, b=b, wsrc=wsrc, chk=chk, w_fd=w_fd: y.dma_start(
                            out=b[:, 0:w_fd], in_=wsrc.ap()[chk * 128:(chk + 1) * 128, :]),
                            inc=("dy", 16),
                            wait=[("v", lastuse[bi]), ("t", None), ("s", None)],
                            tag=f"{sp} A {wi}.{chk}")
                        q.op("v", lambda v, b=b, w_fd=w_fd: v.tensor_reduce(
                            acc[:, 2:3], b[:, 0:w_fd], AX.X, ALU.add, apply_absolute_value=True),
                            wait=[("dy", None)], lock=True)
                        q.op("v", lambda v, wi=wi: v.tensor_tensor(
                            acc[:, wi:wi + 1], acc[:, wi:wi + 1], acc[:, 2:3], ALU.add),
                            lock=True)
                        lastuse[bi] = q.cnt["v"]
                q.op("t", lambda t: t.matmul(psm[:], onesc[:], acc[:, 0:2]),
                     wait=[("v", None)], inc=("t", 1))
                q.op("v", lambda v: v.tensor_scalar(ssc[:, 0:2], psm[:], 1.0 / (D * DFF), None,
                                                    ALU.mult), wait=[("t", None)], lock=True)
                q.op("v", lambda v: v.tensor_scalar(ssc[:, 0:2], ssc[:, 0:2], 1e-5, None, ALU.max),
                     lock=True)
                q.op("v", lambda v: v.reciprocal(ssc[:, 2:4], ssc[:, 0:2]), lock=True)
                q.op("t", lambda t: t.matmul(psb[:], onesr[:], ssc[:, 0:2]),
                     wait=[("v", None)], inc=("t", 1))
                q.op("v", lambda v: v.tensor_copy(swcl[:, 0:2], psb[:]),
                     wait=[("t", None)], inc=("v", 1))
                q.op("t", lambda t: t.matmul(psb[:], onesr[:], ssc[:, 2:4]),
                     wait=[("v", None)], inc=("t", 1))
                q.op("v", lambda v: v.tensor_copy(acc[:, 0:2], psb[:]),
                     wait=[("t", None)], inc=("v", 1))
                lastv = [q.cnt["v"], q.cnt["v"]]
                lastg = [q.cnt["g"], q.cnt["g"]]
                for wi, (wsrc, wdst, nch, w_fd) in enumerate(
                        [(w1t, w1q_h, NI, DFF), (w2t, w2q_h, NO, D)]):
                    for chk in range(nch):
                        bi = chk % 2
                        b = wbuf[bi]
                        eng = "v" if chk % 2 == 0 else "g"
                        q.op("y", lambda y, b=b, wsrc=wsrc, chk=chk, w_fd=w_fd: y.dma_start(
                            out=b[:, 0:w_fd], in_=wsrc.ap()[chk * 128:(chk + 1) * 128, :]),
                            inc=("dy", 16),
                            wait=[("v", lastv[bi]), ("g", lastg[bi])], tag=f"{sp} B {wi}.{chk}")
                        dyv = q.cnt["dy"]
                        for pc in range(0, w_fd, 2048):
                            pw = min(2048, w_fd - pc)
                            q.op(eng, lambda x, b=b, pc=pc, pw=pw, wi=wi: x.tensor_scalar(
                                uf[:, 0:pw], b[:, pc:pc + pw], acc[:, wi:wi + 1], None, ALU.mult),
                                wait=[("dy", dyv)] + ([("g", None)] if eng == "v" else [("v", None)]))
                            q.op(eng, lambda x, pw=pw: x.tensor_copy(u32[:, 0:pw], uf[:, 0:pw]))
                            q.op(eng, lambda x, pw=pw: x.tensor_scalar(
                                u32[:, 0:pw], u32[:, 0:pw], -1, 1, ALU.max, ALU.min))
                            q.op(eng, lambda x, pc=pc, pw=pw: x.tensor_copy(
                                ti8[:, pc:pc + pw], u32[:, 0:pw]), inc=(eng, 1))
                        if eng == "v":
                            lastv[bi] = q.cnt["v"]
                        else:
                            lastg[bi] = q.cnt["g"]
                        q.op("y", lambda y, wdst=wdst, chk=chk, w_fd=w_fd: y.dma_start(
                            out=wdst.ap()[chk * 128:(chk + 1) * 128, :], in_=ti8[:, 0:w_fd]),
                            wait=[(eng, None)], inc=("dy", 16))
                q.flush()

        def l1f(c):
            sp = f"l1f{c}"
            with ExitStack() as ps:
                e = ps.enter_context
                combs = e(nc.sbuf_tensor(f"{sp}_cmb", [128, NT, 2], F32))
                msp = e(nc.sbuf_tensor(f"{sp}_msp", [128, NT, 16], F32))
                pab = ExitStack()
                xqT = pab.enter_context(nc.sbuf_tensor(f"{sp}_xqT", [128, NI, TOK], BF16))
                with ExitStack() as p2:
                    e2 = p2.enter_context
                    g1s = e2(nc.sbuf_tensor(f"{sp}_g1", [128, D], F32))
                    xt = e2(nc.sbuf_tensor(f"{sp}_xta", [128, D], F32))
                    lnb = e2(nc.sbuf_tensor(f"{sp}_lnb", [128, D], F32))
                    sq = e2(nc.sbuf_tensor(f"{sp}_sq", [128, D], F32))
                    kb = e2(nc.sbuf_tensor(f"{sp}_kb", [128, D], BF16))
                    k32 = e2(nc.sbuf_tensor(f"{sp}_k32", [128, D], I32))
                    st = e2(nc.sbuf_tensor(f"{sp}_sta", [128, 16], F32))
                    ptb = [e2(nc.psum_tensor(f"{sp}_ptb{i}", [128, 4, 128], F32)) for i in range(2)]
                    q.op("y", lambda y: y.dma_start(out=g1s[:], in_=g1b_h.ap()), inc=("dy", 16))
                    q.op("s", lambda s: s.preload_act_table(AF.Sqrt), inc=("s", 1))
                    for j in range(NT):
                        q.op("y", lambda y, j=j: y.dma_start(
                            out=xt[:], in_=xres.ap()[j * 128:(j + 1) * 128, :]),
                            inc=("dy", 16), wait=[("v", None), ("s", None), ("t", None)],
                            tag=f"{sp} a {j}")
                        dyv = q.cnt["dy"]
                        q.op("v", lambda v: v.reduce_sum(st[:, 0:1], xt[:], AX.X),
                             wait=[("dy", dyv)], lock=True)
                        q.op("s", lambda s: s.activation(sq[:], xt[:], AF.Square,
                                                         accum_out=st[:, 1:2]),
                             wait=[("dy", dyv)], inc=("s", 1))
                        q.op("v", lambda v: v.tensor_scalar(st[:, 2:3], st[:, 0:1], 1.0 / D, None, ALU.mult),
                             wait=[("s", None)], lock=True)
                        q.op("v", lambda v: v.tensor_scalar(st[:, 3:4], st[:, 1:2], 1.0 / D, None, ALU.mult),
                             lock=True)
                        q.op("v", lambda v: v.tensor_tensor(st[:, 4:5], st[:, 2:3], st[:, 2:3], ALU.mult),
                             lock=True)
                        q.op("v", lambda v: v.tensor_tensor(st[:, 5:6], st[:, 3:4], st[:, 4:5], ALU.subtract),
                             lock=True)
                        q.op("v", lambda v: v.tensor_scalar(st[:, 5:6], st[:, 5:6], 1e-5, None, ALU.add),
                             lock=True)
                        q.op("s", lambda s: s.activation(st[:, 6:7], st[:, 5:6], AF.Sqrt),
                             wait=[("v", None)], inc=("s", 1))
                        q.op("v", lambda v: v.reciprocal(st[:, 7:8], st[:, 6:7]), wait=[("s", None)],
                             lock=True)
                        q.op("v", lambda v: v.scalar_tensor_tensor(
                            st[:, 8:9], st[:, 2:3], -1.0, st[:, 7:8], ALU.mult, ALU.mult),
                            lock=True)
                        q.op("s", lambda s: s.activation(lnb[:], xt[:], AF.Identity,
                                                         bias=st[:, 8:9], scale=st[:, 7:8]),
                             wait=[("v", None)], inc=("s", 1))
                        q.op("s", lambda s: s.activation(sq[:], lnb[:], AF.Square,
                                                         accum_out=st[:, 9:10]), inc=("s", 1))
                        q.op("v", lambda v: v.tensor_scalar(st[:, 10:11], st[:, 9:10], 1.0 / D,
                                                            None, ALU.mult), wait=[("s", None)],
                             lock=True)
                        q.op("v", lambda v: v.tensor_scalar(st[:, 10:11], st[:, 10:11], 1e-6,
                                                            None, ALU.add), lock=True)
                        q.op("s", lambda s: s.activation(st[:, 11:12], st[:, 10:11], AF.Sqrt),
                             wait=[("v", None)], inc=("s", 1))
                        q.op("v", lambda v: v.reciprocal(st[:, 12:13], st[:, 11:12]),
                             wait=[("s", None)], lock=True)
                        q.op("v", lambda v: v.scalar_tensor_tensor(
                            lnb[:], lnb[:], st[:, 12:13], g1s[:], ALU.mult, ALU.mult), lock=True)
                        q.op("v", lambda v: v.tensor_reduce(st[:, 13:14], lnb[:], AX.X, ALU.max,
                                                            apply_absolute_value=True), lock=True)
                        q.op("v", lambda v: v.tensor_scalar(st[:, 13:14], st[:, 13:14], 1e-5,
                                                            None, ALU.max), lock=True)
                        q.op("v", lambda v: v.reciprocal(st[:, 14:15], st[:, 13:14]), lock=True)
                        q.op("v", lambda v: v.tensor_scalar(st[:, 14:15], st[:, 14:15], 127.0,
                                                            None, ALU.mult), lock=True)
                        q.op("v", lambda v, j=j: v.tensor_scalar(
                            combs[:, j, 0:1], st[:, 13:14], 1.0 / 127.0, None, ALU.mult), lock=True)
                        q.op("v", lambda v, j=j: v.tensor_tensor(
                            combs[:, j, 0:1], combs[:, j, 0:1], swcl[:, 0:1], ALU.mult), lock=True)
                        if L1FA_MODE >= 2:
                            q.op("v", lambda v, j=j: v.memset(
                                xqT[:, :, j * 128:(j + 1) * 128], 1.0), inc=("v", 1))
                            continue
                        q.op("v", lambda v: v.tensor_scalar(lnb[:], lnb[:], st[:, 14:15], None,
                                                            ALU.mult), lock=True)
                        q.op("v", lambda v: v.tensor_copy(k32[:], lnb[:]))
                        q.op("v", lambda v: v.tensor_copy(kb[:], k32[:]), inc=("v", 1))
                        vv = q.cnt["v"]
                        if L1FA_MODE >= 1:
                            q.op("v", lambda v, j=j: v.memset(
                                xqT[:, :, j * 128:(j + 1) * 128], 1.0),
                                wait=[("v", vv)], inc=("v", 1))
                            continue
                        for i0 in range(0, NI, 4):
                            pp = ptb[(i0 // 4) % 2]
                            for i in range(i0, i0 + 4):
                                q.op("t", lambda t, pp=pp, i=i, i0=i0: t.matmul(
                                    pp[:, i - i0, :], kb[:, i * 128:(i + 1) * 128], identb[:],
                                    start=True, stop=True),
                                    wait=([("v", None)] if i == i0 else None),
                                    inc=("t", 1) if i == i0 + 3 else None)
                            q.op("v", lambda v, pp=pp, i0=i0, j=j: v.tensor_copy(
                                xqT[:, i0:i0 + 4, j * 128:(j + 1) * 128], pp[:]),
                                wait=[("t", None)], inc=("v", 1))
                    q.flush()
                if L1F_CUT < 2:
                    pab.close()
                    return

                with ExitStack() as p2:
                    e2 = p2.enter_context
                    w1s = [e2(nc.sbuf_tensor(f"{sp}_w1s{i}", [128, NI, 512], BF16)) for i in range(2)]
                    hsb = [e2(nc.sbuf_tensor(f"{sp}_hsb{i}", [128, 512], F32)) for i in range(2)]
                    hscr = e2(nc.sbuf_tensor(f"{sp}_hscr", [128, 512], F32))
                    g2s = e2(nc.sbuf_tensor(f"{sp}_g2", [128, DFF], F32))
                    pph = [e2(nc.psum_tensor(f"{sp}_pph{i}", [128, 512], F32)) for i in range(2)]
                    q.op("y", lambda y: y.dma_start(out=g2s[:], in_=g2b_h.ap()),
                         inc=("dy", 16), wait=[("t", None), ("v", None)])
                    wlast = [q.cnt["t"], q.cnt["t"]]
                    hlast = [q.cnt["dy"], q.cnt["dy"]]
                    for og in range(16):
                        wb = w1s[og % 2]
                        q.op("g", lambda g, og=og, wb=wb: g.dma_start(
                            out=wb[:], in_=w1q_h.ap()[:, og * 512:(og + 1) * 512]
                            .rearrange("(c p) n -> p c n", p=128)),
                            inc=("dg", 16), wait=[("t", wlast[og % 2])], tag=f"{sp} w1 {og}")
                        dgv = q.cnt["dg"]
                        for j in range(NT):
                            pp = pph[j % 2]
                            hb = hsb[j % 2]
                            for i in range(NI):
                                q.op("t", lambda t, pp=pp, wb=wb, i=i, j=j: t.matmul(
                                    pp[:], xqT[:, i, j * 128:(j + 1) * 128], wb[:, i, :],
                                    start=(i == 0), stop=(i == NI - 1)),
                                    wait=[("dg", dgv), ("s", None)] if i == 0 else None,
                                    inc=("t", 1) if i == NI - 1 else None,
                                    tag=f"{sp} h {og}.{j}")
                            q.op("s", lambda s, pp=pp, hb=hb, j=j: s.activation(
                                hb[:], pp[:], AF.Relu, scale=combs[:, j, 0:1]),
                                wait=[("t", None), ("dy", hlast[j % 2])], inc=("s", 1))
                            q.op("s", lambda s, hb=hb: s.activation(hb[:], hb[:], AF.Square),
                                 inc=("s", 1))
                            q.op("s", lambda s, hb=hb, j=j, og=og: s.activation(
                                hscr[:], hb[:], AF.Square, accum_out=msp[:, j, og:og + 1]),
                                inc=("s", 1))
                            q.op("v", lambda v, hb=hb, og=og: v.tensor_tensor(
                                hb[:], hb[:], g2s[:, og * 512:(og + 1) * 512], ALU.mult),
                                wait=[("s", None)], inc=("v", 1))
                            q.op("y", lambda y, hb=hb, j=j, og=og: y.dma_start(
                                out=hbuf.ap()[j * 128:(j + 1) * 128, og * 512:(og + 1) * 512],
                                in_=hb[:]), wait=[("v", None)], inc=("dy", 16))
                            hlast[j % 2] = q.cnt["dy"]
                        wlast[og % 2] = q.cnt["t"]
                    q.flush()
                pab.close()
                if L1F_CUT < 3:
                    return

                with ExitStack() as p2:
                    e2 = p2.enter_context
                    hqT = e2(nc.sbuf_tensor(f"{sp}_hqT", [128, NO, 4, 128], BF16))
                    hti = e2(nc.sbuf_tensor(f"{sp}_ht", [128, DFF], F32))
                    kb2 = e2(nc.sbuf_tensor(f"{sp}_kb2", [128, 2048], BF16))
                    k322 = e2(nc.sbuf_tensor(f"{sp}_k322", [128, 2048], I32))
                    st2 = e2(nc.sbuf_tensor(f"{sp}_st2", [128, 8], F32))
                    w2s = e2(nc.sbuf_tensor(f"{sp}_w2s", [128, NO, 512], BF16))
                    xtl = [e2(nc.sbuf_tensor(f"{sp}_xtl{i}", [128, D], F32)) for i in range(4)]
                    ptb2 = [e2(nc.psum_tensor(f"{sp}_pt2{i}", [128, 4, 128], F32)) for i in range(2)]
                    ppo = [e2(nc.psum_tensor(f"{sp}_ppo{i}", [128, 512], F32)) for i in range(2)]
                    for half in range(2):
                        jl = [half * 4 + jj for jj in range(4)]
                        for jj, j in enumerate(jl):
                            q.op("y", lambda y, j=j: y.dma_start(
                                out=hti[:], in_=xres.ap()[0:0 + 128, :]) if False else y.dma_start(
                                out=hti[:], in_=hbuf.ap()[j * 128:(j + 1) * 128, :]),
                                inc=("dy", 16), wait=[("v", None), ("t", None)],
                                tag=f"{sp} hload {j}")
                            dyv = q.cnt["dy"]
                            q.op("v", lambda v, j=j: v.reduce_sum(st2[:, 0:1], msp[:, j, :], AX.X),
                                 wait=[("dy", dyv)], lock=True)
                            q.op("v", lambda v: v.tensor_scalar(st2[:, 0:1], st2[:, 0:1],
                                                                1.0 / DFF, None, ALU.mult), lock=True)
                            q.op("v", lambda v: v.tensor_scalar(st2[:, 0:1], st2[:, 0:1], 1e-6,
                                                                None, ALU.add), lock=True)
                            q.op("s", lambda s: s.activation(st2[:, 1:2], st2[:, 0:1], AF.Sqrt),
                                 wait=[("v", None)], inc=("s", 1))
                            q.op("v", lambda v: v.reciprocal(st2[:, 2:3], st2[:, 1:2]),
                                 wait=[("s", None)], lock=True)
                            q.op("v", lambda v: v.tensor_scalar(
                                hti[:], hti[:], st2[:, 2:3], None, ALU.mult), lock=True)
                            q.op("v", lambda v: v.tensor_reduce(st2[:, 3:4], hti[:], AX.X,
                                                                ALU.max, apply_absolute_value=True),
                                lock=True)
                            q.op("v", lambda v: v.tensor_scalar(st2[:, 3:4], st2[:, 3:4], 1e-5,
                                                                None, ALU.max), lock=True)
                            q.op("v", lambda v: v.reciprocal(st2[:, 4:5], st2[:, 3:4]), lock=True)
                            q.op("v", lambda v: v.tensor_scalar(st2[:, 4:5], st2[:, 4:5], 127.0,
                                                                None, ALU.mult), lock=True)
                            q.op("v", lambda v, j=j: v.tensor_scalar(
                                combs[:, j, 1:2], st2[:, 3:4], 0.5 / 127.0, None, ALU.mult), lock=True)
                            q.op("v", lambda v, j=j: v.tensor_tensor(
                                combs[:, j, 1:2], combs[:, j, 1:2], swcl[:, 1:2], ALU.mult), lock=True)
                            q.op("v", lambda v: v.tensor_scalar(hti[:], hti[:], st2[:, 4:5],
                                                                None, ALU.mult), lock=True)
                            for pc in range(0, DFF, 2048):
                                q.op("v", lambda v, pc=pc: v.tensor_copy(k322[:], hti[:, pc:pc + 2048]))
                                q.op("v", lambda v, pc=pc: v.tensor_copy(kb2[:], k322[:]),
                                     inc=("v", 1))
                                vv = q.cnt["v"]
                                for o0 in range(0, 16, 4):
                                    pp = ptb2[(o0 // 4) % 2]
                                    for oc in range(o0, o0 + 4):
                                        q.op("t", lambda t, pp=pp, oc=oc, o0=o0: t.matmul(
                                            pp[:, oc - o0, :], kb2[:, oc * 128:(oc + 1) * 128],
                                            identb[:], start=True, stop=True),
                                            wait=([("v", None)] if oc == o0 else None),
                                            inc=("t", 1) if oc == o0 + 3 else None)
                                    q.op("v", lambda v, pp=pp, pc=pc, o0=o0, jj=jj: v.tensor_copy(
                                        hqT[:, pc // 128 + o0:pc // 128 + o0 + 4, jj, :], pp[:]),
                                        wait=[("t", None)], inc=("v", 1))
                            q.op("y", lambda y, jj=jj, j=j: y.dma_start(
                                out=xtl[jj][:], in_=xres.ap()[j * 128:(j + 1) * 128, :]),
                                inc=("dy", 16))
                        wlast = q.cnt["t"]
                        for dmc in range(4):
                            q.op("g", lambda g, dmc=dmc: g.dma_start(
                                out=w2s[:], in_=w2q_h.ap()[:, dmc * 512:(dmc + 1) * 512]
                                .rearrange("(c p) n -> p c n", p=128)),
                                inc=("dg", 16), wait=[("t", None)], tag=f"{sp} w2 {half}.{dmc}")
                            dgv = q.cnt["dg"]
                            for jj, j in enumerate(jl):
                                pp = ppo[jj % 2]
                                for oc in range(NO):
                                    q.op("t", lambda t, pp=pp, oc=oc, jj=jj: t.matmul(
                                        pp[:], hqT[:, oc, jj, :], w2s[:, oc, :],
                                        start=(oc == 0), stop=(oc == NO - 1)),
                                        wait=[("dg", dgv), ("v", None)] if oc == 0 else None,
                                        inc=("t", 1) if oc == NO - 1 else None,
                                        tag=f"{sp} y2 {half}.{dmc}.{jj}")
                                q.op("v", lambda v, pp=pp, jj=jj, j=j, dmc=dmc: v.scalar_tensor_tensor(
                                    xtl[jj][:, dmc * 512:(dmc + 1) * 512], pp[:],
                                    combs[:, j, 1:2], xtl[jj][:, dmc * 512:(dmc + 1) * 512],
                                    ALU.mult, ALU.add),
                                    wait=[("t", None), ("dy", None)], inc=("v", 1))
                        for jj, j in enumerate(jl):
                            q.op("y", lambda y, jj=jj, j=j: y.dma_start(
                                out=xres.ap()[j * 128:(j + 1) * 128, :], in_=xtl[jj][:]),
                                wait=[("v", None)], inc=("dy", 16), tag=f"{sp} xst {j}")
                    q.flush()

        def l1a():
            sp = "l1a"
            with ExitStack() as ps:
                e = ps.enter_context
                x_lnT = e(nc.sbuf_tensor(f"{sp}_xlnT", [128, NI, TOK], F32))
                ln_transpose(xres, x_lnT, sp)
                q.op("s", lambda s: s.preload_act_table(AF.Exp), inc=("s", 1))
                with ExitStack() as p2:
                    e2 = p2.enter_context
                    wsl = e2(nc.sbuf_tensor(f"{sp}_wsl", [128, NI, 512], F32))
                    kvb = e2(nc.sbuf_tensor(f"{sp}_kvb", [128, 1024], F32))
                    pp2 = [e2(nc.psum_tensor(f"{sp}_pk{i}", [128, 512], F32)) for i in range(2)]
                    for og in range(4):
                        tv = q.cnt["t"]
                        q.op("y", lambda y, og=og: y.dma_start(
                            out=wsl[:], in_=wkt.ap()[:, og * 512:(og + 1) * 512]
                            .rearrange("(c p) n -> p c n", p=128)),
                            inc=("dy", 16), wait=[("t", tv)], tag=f"{sp} wk {og}")
                        dyv = q.cnt["dy"]
                        for hc in range(4):
                            for jh in range(2):
                                pp = pp2[jh]
                                for i in range(NI):
                                    q.op("t", lambda t, pp=pp, i=i, hc=hc, jh=jh: t.matmul(
                                        pp[:], wsl[:, i, hc * 128:(hc + 1) * 128],
                                        x_lnT[:, i, jh * 512:(jh + 1) * 512],
                                        start=(i == 0), stop=(i == NI - 1)),
                                        wait=[("dy", dyv), ("v", None)] if i == 0 else None,
                                        inc=("t", 1) if i == NI - 1 else None,
                                        tag=f"{sp} K {og}.{hc}.{jh}")
                                q.op("v", lambda v, pp=pp, jh=jh: v.tensor_copy(
                                    kvb[:, jh * 512:(jh + 1) * 512], pp[:]),
                                    wait=[("t", None)], inc=("v", 1))
                            gh = og * 4 + hc
                            q.op("y", lambda y, gh=gh: y.dma_start(
                                out=kg_in.ap()[gh * 128:(gh + 1) * 128, :], in_=kvb[:]),
                                wait=[("v", None)], inc=("dy", 16))
                    for og in range(4):
                        tv = q.cnt["t"]
                        q.op("y", lambda y, og=og: y.dma_start(
                            out=wsl[:], in_=wvt.ap()[:, og * 512:(og + 1) * 512]
                            .rearrange("(c p) n -> p c n", p=128)),
                            inc=("dy", 16), wait=[("t", tv), ("dy", None)], tag=f"{sp} wv {og}")
                        dyv = q.cnt["dy"]
                        for j in range(NT):
                            pp = pp2[j % 2]
                            for i in range(NI):
                                q.op("t", lambda t, pp=pp, i=i, j=j: t.matmul(
                                    pp[:], x_lnT[:, i, j * 128:(j + 1) * 128], wsl[:, i, :],
                                    start=(i == 0), stop=(i == NI - 1)),
                                    wait=[("dy", dyv), ("v", None)] if i == 0 else None,
                                    inc=("t", 1) if i == NI - 1 else None,
                                    tag=f"{sp} V {og}.{j}")
                            q.op("v", lambda v, pp=pp: v.tensor_copy(kvb[:, 0:512], pp[:]),
                                 wait=[("t", None)], inc=("v", 1))
                            q.op("y", lambda y, j=j, og=og: y.dma_start(
                                out=vg_in.ap()[j * 128:(j + 1) * 128, og * 512:(og + 1) * 512],
                                in_=kvb[:, 0:512]), wait=[("v", None)], inc=("dy", 16))
                    dyb = q.cnt["dy"]
                    q.op("g", lambda g: g.collective_compute(
                        "AllGather", ALU.bypass, replica_groups=REPG,
                        ins=[kg_in.ap().opt()], outs=[kg_out.ap().opt()]),
                        inc=("cc", 1), wait=[("dy", dyb)], tag=f"{sp} AGK")
                    q.op("g", lambda g: g.collective_compute(
                        "AllGather", ALU.bypass, replica_groups=REPG,
                        ins=[vg_in.ap().opt()], outs=[vg_out.ap().opt()]),
                        inc=("cc", 1), tag=f"{sp} AGV")
                    q.flush()
                ccv = q.cnt["cc"]

                qh = e(nc.sbuf_tensor(f"{sp}_qh", [128, H, 4, 128], F32))
                yfm = e(nc.sbuf_tensor(f"{sp}_yfm", [128, H, 4, 128], F32))
                for half in range(2):
                    jl = [half * 4 + jj for jj in range(4)]
                    with ExitStack() as p2:
                        e2 = p2.enter_context
                        wslq = e2(nc.sbuf_tensor(f"{sp}_wsq{half}", [128, NI, 512], F32))
                        ppq = [e2(nc.psum_tensor(f"{sp}_pq{half}{i}", [128, 512], F32))
                               for i in range(2)]
                        for og in range(4):
                            tv = q.cnt["t"]
                            q.op("y", lambda y, og=og: y.dma_start(
                                out=wslq[:], in_=wqt.ap()[:, og * 512:(og + 1) * 512]
                                .rearrange("(c p) n -> p c n", p=128)),
                                inc=("dy", 16), wait=[("t", tv)], tag=f"{sp} wq{half} {og}")
                            dyv = q.cnt["dy"]
                            for hc in range(4):
                                h = og * 4 + hc
                                pp = ppq[hc % 2]
                                for i in range(NI):
                                    q.op("t", lambda t, pp=pp, i=i, hc=hc, half=half: t.matmul(
                                        pp[:], wslq[:, i, hc * 128:(hc + 1) * 128],
                                        x_lnT[:, i, half * 512:(half + 1) * 512],
                                        start=(i == 0), stop=(i == NI - 1)),
                                        wait=[("dy", dyv), ("v", None)] if i == 0 else None,
                                        inc=("t", 1) if i == NI - 1 else None,
                                        tag=f"{sp} Q {half}.{h}")
                                q.op("v", lambda v, pp=pp, h=h: v.tensor_copy(
                                    qh[:, h, :, :], pp[:].rearrange("p (a b) -> p a b", a=4)),
                                    wait=[("t", None)], inc=("v", 1))
                        q.flush()

                    with ExitStack() as p2:
                        e2 = p2.enter_context
                        kfh = [e2(nc.sbuf_tensor(f"{sp}_kfh{half}{i}", [128, 16, 128], F32))
                               for i in range(2)]
                        vth = [e2(nc.sbuf_tensor(f"{sp}_vth{half}{i}", [128, 16, 128], F32))
                               for i in range(2)]
                        bufs = make_attn_bufs(f"{sp}h{half}", e2)
                        kvt_last = [0, 0]
                        dy_kv = [0] * (H + 1)

                        def loadkv(h, w):
                            buf = h % 2
                            q.op("y", lambda y, h=h, buf=buf: y.dma_start(
                                out=kfh[buf][:, 0:8, :],
                                in_=kg_out.ap()[h * 128:(h + 1) * 128, :]
                                .rearrange("p (a b) -> p a b", a=8)),
                                inc=("dy", 16), wait=w, tag=f"{sp} kv {half}.{h}")
                            q.op("y", lambda y, h=h, buf=buf: y.dma_start(
                                out=kfh[buf][:, 8:16, :],
                                in_=kg_out.ap()[D + h * 128:D + (h + 1) * 128, :]
                                .rearrange("p (a b) -> p a b", a=8)), inc=("dy", 16))
                            q.op("y", lambda y, h=h, buf=buf: y.dma_start(
                                out=vth[buf][:], in_=vg_out.ap()[:, h * 128:(h + 1) * 128]
                                .rearrange("(a p) n -> p a n", p=128)), inc=("dy", 16))
                            dy_kv[h] = q.cnt["dy"]

                        loadkv(0, [("cc", ccv), ("t", None), ("v", None)])
                        for h in range(H):
                            if h + 1 < H:
                                loadkv(h + 1, [("t", kvt_last[(h + 1) % 2])])
                            for jj, j in enumerate(jl):
                                def y_emit(psy, rme, h=h, jj=jj):
                                    q.op("v", lambda v, h=h, jj=jj, psy=psy: v.tensor_copy(
                                        yfm[:, h, jj, :], psy[:]),
                                        wait=[("t", None)], inc=("v", 1))
                                attn_unit(bufs, qh[:, h, jj, :], kfh[h % 2], vth[h % 2],
                                          j + 1, y_emit, scale_p=True,
                                          xwaits=[("dy", dy_kv[h])])
                            kvt_last[h % 2] = q.cnt["t"]
                        q.flush()

                    with ExitStack() as p2:
                        e2 = p2.enter_context
                        wslo = e2(nc.sbuf_tensor(f"{sp}_wso{half}", [128, NI, 512], F32))
                        xtl = [e2(nc.sbuf_tensor(f"{sp}_xo{half}{i}", [128, D], F32))
                               for i in range(4)]
                        ppo = [e2(nc.psum_tensor(f"{sp}_po{half}{i}", [128, 512], F32))
                               for i in range(2)]
                        for jj, j in enumerate(jl):
                            q.op("y", lambda y, jj=jj, j=j: y.dma_start(
                                out=xtl[jj][:], in_=xres.ap()[j * 128:(j + 1) * 128, :]),
                                inc=("dy", 16), wait=[("t", None), ("v", None)])
                        for og in range(4):
                            tv = q.cnt["t"]
                            q.op("y", lambda y, og=og: y.dma_start(
                                out=wslo[:], in_=wot.ap()[:, og * 512:(og + 1) * 512]
                                .rearrange("(c p) n -> p c n", p=128)),
                                inc=("dy", 16), wait=[("t", tv)], tag=f"{sp} wo{half} {og}")
                            dyv = q.cnt["dy"]
                            for jj, j in enumerate(jl):
                                pp = ppo[jj % 2]
                                for hh in range(H):
                                    q.op("t", lambda t, pp=pp, hh=hh, jj=jj: t.matmul(
                                        pp[:], yfm[:, hh, jj, :], wslo[:, hh, :],
                                        start=(hh == 0), stop=(hh == H - 1)),
                                        wait=[("dy", dyv), ("v", None)] if hh == 0 else None,
                                        inc=("t", 1) if hh == H - 1 else None,
                                        tag=f"{sp} WO {half}.{og}.{jj}")
                                q.op("v", lambda v, pp=pp, jj=jj, og=og: v.tensor_tensor(
                                    xtl[jj][:, og * 512:(og + 1) * 512],
                                    xtl[jj][:, og * 512:(og + 1) * 512], pp[:], ALU.add),
                                    wait=[("t", None), ("dy", None)], inc=("v", 1))
                        for jj, j in enumerate(jl):
                            q.op("y", lambda y, jj=jj, j=j: y.dma_start(
                                out=xres.ap()[j * 128:(j + 1) * 128, :], in_=xtl[jj][:]),
                                wait=[("v", None)], inc=("dy", 16), tag=f"{sp} xst {j}")
                        q.flush()

        if n_sub >= 1:
            l0f(0, first=True)
        if n_sub >= 2:
            l0a()
        if n_sub >= 3:
            l0f(1, first=False)
        if n_sub >= 4:
            wq()
        if n_sub >= 5:
            l1f(0)
        if n_sub >= 6:
            l1a()
        if n_sub >= 7:
            l1f(1)

        src = xres if (n_sub >= 1 and L0F_CUT >= 5) else xin
        q.op("y", lambda y: y.dma_start(out=out_h.ap(), in_=src.ap()),
             wait=[("v", None), ("s", None), ("t", None), ("dy", None), ("dg", None),
                   ("cc", None), ("g", None)], inc=("dy", 16), tag="final out")
        fin = q.cnt["dy"]
        q.op("y", lambda y: y.nop(), wait=[("dy", fin)], tag="final wait")
        q.flush()
        q.audit()
    return nc


def _prep(inputs):
    f32 = np.float32
    x = np.asarray(inputs["L0_attn_x"], f32)
    Bm = np.asarray(inputs["L0_attn_B"], f32)
    Cm = np.asarray(inputs["L0_attn_C"], f32)
    Dv = np.asarray(inputs["L0_attn_D"], f32)
    Wg = np.asarray(inputs["L0_ffn_W_gate"], f32)
    Wqkv = np.asarray(inputs["L0_ffn_W_qkv"], f32)
    WK = np.asarray(inputs["L1_attn_W_K"], f32)
    WQ = np.asarray(inputs["L1_attn_W_Q"], f32)
    WV = np.asarray(inputs["L1_attn_W_V"], f32)
    WO = np.asarray(inputs["L1_attn_W_O"], f32)
    w1 = np.asarray(inputs["L1_ffn_w1"], f32)
    g1 = np.asarray(inputs["L1_ffn_g1"], f32)
    w2 = np.asarray(inputs["L1_ffn_w2"], f32)
    g2 = np.asarray(inputs["L1_ffn_g2"], f32)

    C = np.ascontiguousarray
    shared = {
        "wg": C(Wg.T),
        "wqkv": C(Wqkv.T),
        "bt": C(Bm.T),
        "ct": C(Cm.T),
        "wkt": C(WK.reshape(D, D).T),
        "wqt": C(WQ.reshape(D, D).T),
        "wvt": C(WV.reshape(D, D).T),
        "wot": C(WO.T),
        "w1t": C(w1.T),
        "w2t": C(w2.T),
        "g1b": C(np.broadcast_to(g1, (128, D))),
        "g2b": C(np.broadcast_to(g2, (128, DFF))),
        "identf": np.eye(128, dtype=f32),
        "identb": np.eye(128).astype(ml_dtypes.bfloat16),
    }
    dd = np.zeros((4, 128, 4, 512), f32)
    idx = np.arange(128)
    for og in range(4):
        for ii in range(4):
            dd[og, idx, ii, ii * 128 + idx] = Dv[og * 512 + ii * 128 + idx]
    shared["ddiag"] = dd
    cstv = np.zeros((128, 8), f32)
    cstv[:, 0] = ISQ
    cstv[:, 1] = 1.0
    shared["cst"] = cstv

    tril = np.where(np.arange(128)[None, :] <= np.arange(128)[:, None], 0.0, -1e9).astype(f32)
    zeros = np.zeros((128, 128), f32)
    full = np.full((128, 128), -1e9, f32)

    in_maps = []
    for core in range(8):
        b, h = core // 2, core % 2
        xs = C(x[b].reshape(16, 128, D)[h::2].reshape(TOK, D))
        m = dict(shared)
        m["xin"] = xs
        m["maska"] = tril if h == 0 else zeros
        m["maskb"] = full if h == 0 else tril
        in_maps.append(m)
    return x.shape, in_maps


def kernel(**inputs):
    shape, in_maps = _prep(inputs)
    nc = build()
    res = run_bass_kernel_spmd(nc, in_maps, core_ids=list(range(8)))
    out = np.empty(shape, np.float32)
    for core in range(8):
        b, h = core // 2, core % 2
        out[b].reshape(16, 128, D)[h::2] = res.results[core]["out"].reshape(NT, 128, D)
    return out


# version 05
# speedup vs baseline: 1.1652x; 1.1652x over previous
import math
from collections import defaultdict
from contextlib import ExitStack

import numpy as np
import ml_dtypes

import concourse.bass as bass
import concourse.mybir as mybir
from concourse.bass_utils import run_bass_kernel_spmd

F32 = mybir.dt.float32
BF16 = mybir.dt.bfloat16
I8 = mybir.dt.int8
I32 = mybir.dt.int32
AF = mybir.ActivationFunctionType
ALU = mybir.AluOpType
AX = mybir.AxisListType

D = 2048
TOK = 1024
NT = 8
NI = D // 128
H = 16
DFF = 8192
NO = DFF // 128
ISQ = 1.0 / math.sqrt(128.0)
N_SUB = 7
L0F_CUT = 5
L0F_SKIP_AG = False
L0F_SKIP_KV = False
L0F_SKIP_Q = False
L0F_LOAD_ONLY = False
L0F_SKIP_GATE = False
LNT_MODE = 0
L1F_CUT = 3
L1FA_MODE = 0
SKIP_PRELOAD = False
SKIP_INIT = False

ENG = {"t": "tensor", "v": "vector", "s": "scalar", "g": "gpsimd", "y": "sync"}


class Q:

    def __init__(self, nc, stk):
        self.nc = nc
        self.buf = {k: [] for k in ENG}
        self.meta = []
        self.cnt = defaultdict(int)
        self.sems = {}
        self.waited = {k: defaultdict(int) for k in ENG}
        self.stk = stk

    def sem(self, name):
        self.sems[name] = self.stk.enter_context(self.nc.semaphore(name))

    def op(self, key, f, inc=None, wait=None, tag="", lock=False):
        if lock:
            wait = list(wait or []) + [(key, self.cnt[key])]
            if inc is None:
                inc = (key, 1)
            else:
                assert inc[0] == key
        waits = []
        for w in (wait or []):
            sname, val = w if isinstance(w, tuple) else (w, None)
            if val is None:
                val = self.cnt[sname]
            if val > self.waited[key][sname]:
                waits.append((sname, val))
                self.waited[key][sname] = val
        incs = None
        if inc:
            sname, k = inc
            self.cnt[sname] += k
            incs = (sname, k)
        self.meta.append((key, waits, incs, tag))

        def fn(eng, waits=waits, incs=incs, f=f):
            for sname, val in waits:
                eng.wait_ge(self.sems[sname], val)
            instr = f(eng)
            if incs is not None:
                instr.then_inc(self.sems[incs[0]], incs[1])
        self.buf[key].append(fn)

    def flush(self):
        bufs = {k: v for k, v in self.buf.items() if v}
        self.buf = {k: [] for k in ENG}
        if not bufs:
            return
        with self.nc.Block() as blk:
            for key, fns in bufs.items():
                getattr(blk, ENG[key])(lambda eng, fns=fns: [f(eng) for f in fns])

    def audit(self):
        qs = {k: [] for k in ENG}
        for key, waits, incs, tag in self.meta:
            qs[key].append((waits, incs, tag))
        pos = {k: 0 for k in ENG}
        cnt = defaultdict(int)
        progress = True
        while progress:
            progress = False
            for k in ENG:
                while pos[k] < len(qs[k]):
                    waits, incs, tag = qs[k][pos[k]]
                    if any(cnt[s] < v for s, v in waits):
                        break
                    if incs:
                        cnt[incs[0]] += incs[1]
                    pos[k] += 1
                    progress = True
        stuck = {k: (len(qs[k]) - pos[k], qs[k][pos[k]]) for k in ENG if pos[k] < len(qs[k])}
        if stuck:
            raise RuntimeError(f"deadlock audit failed: {stuck}")


def build(n_sub=None):
    n_sub = N_SUB if n_sub is None else n_sub
    nc = bass.Bass("TRN2", target_bir_lowering=False, detect_race_conditions=False)

    xin = nc.dram_tensor("xin", [TOK, D], F32, kind="ExternalInput")
    wg = nc.dram_tensor("wg", [D, 16], F32, kind="ExternalInput")
    wqkv = nc.dram_tensor("wqkv", [D, 3 * D], F32, kind="ExternalInput")
    bt = nc.dram_tensor("bt", [D, 1024], F32, kind="ExternalInput")
    ct = nc.dram_tensor("ct", [1024, D], F32, kind="ExternalInput")
    ddiag = nc.dram_tensor("ddiag", [4, 128, 4, 512], F32, kind="ExternalInput")
    wkt = nc.dram_tensor("wkt", [D, D], F32, kind="ExternalInput")
    wqt = nc.dram_tensor("wqt", [D, D], F32, kind="ExternalInput")
    wvt = nc.dram_tensor("wvt", [D, D], F32, kind="ExternalInput")
    wot = nc.dram_tensor("wot", [D, D], F32, kind="ExternalInput")
    w1t = nc.dram_tensor("w1t", [D, DFF], F32, kind="ExternalInput")
    w2t = nc.dram_tensor("w2t", [DFF, D], F32, kind="ExternalInput")
    g1b_h = nc.dram_tensor("g1b", [128, D], F32, kind="ExternalInput")
    g2b_h = nc.dram_tensor("g2b", [128, DFF], F32, kind="ExternalInput")
    id_h = nc.dram_tensor("identf", [128, 128], F32, kind="ExternalInput")
    idb_h = nc.dram_tensor("identb", [128, 128], BF16, kind="ExternalInput")
    mka_h = nc.dram_tensor("maska", [128, 128], F32, kind="ExternalInput")
    mkb_h = nc.dram_tensor("maskb", [128, 128], F32, kind="ExternalInput")
    cst_h = nc.dram_tensor("cst", [128, 8], F32, kind="ExternalInput")
    out_h = nc.dram_tensor("out", [TOK, D], F32, kind="ExternalOutput")
    dbg_h = nc.dram_tensor("dbg", [2048, 1024], F32, kind="ExternalOutput") \
        if globals().get("DEBUG_OUT", False) else None

    xres = nc.dram_tensor("xres", [TOK, D], F32)
    l0kv_in = [nc.dram_tensor(f"l0kvi{c}", [TOK, 256], F32) for c in range(2)]
    l0kv_out = [nc.dram_tensor(f"l0kvo{c}", [2 * TOK, 256], F32) for c in range(2)]
    kg_in = nc.dram_tensor("kgi", [D, TOK], F32)
    kg_out = nc.dram_tensor("kgo", [2 * D, TOK], F32)
    vg_in = nc.dram_tensor("vgi", [TOK, D], F32)
    vg_out = nc.dram_tensor("vgo", [2 * TOK, D], F32)
    w1q_h = nc.dram_tensor("w1qd", [D, DFF], I8)
    w2q_h = nc.dram_tensor("w2qd", [DFF, D], I8)
    hbuf = nc.dram_tensor("hbuf", [TOK, DFF], F32)

    stk = ExitStack()
    with stk:
        q = Q(nc, stk)
        for s in ["dy", "dg", "cc", "t", "v", "s", "g"]:
            q.sem(s)
        REPG = [[0, 1], [2, 3], [4, 5], [6, 7]]
        sbp = lambda name, shape, dt=F32: stk.enter_context(nc.sbuf_tensor(name, shape, dt))

        ident = sbp("ident_s", [128, 128])
        identb = sbp("identb_s", [128, 128], BF16)
        maska = sbp("maska_s", [128, 128])
        maskb = sbp("maskb_s", [128, 128])
        cst = sbp("cst_s", [128, 8])
        onesc = sbp("onesc", [128, 1])
        onesr = sbp("onesr", [1, 128])
        swcl = sbp("swcl", [128, 4])

        if not SKIP_INIT:
            for t_sb, t_h in [(ident, id_h), (identb, idb_h), (maska, mka_h),
                              (maskb, mkb_h), (cst, cst_h)]:
                q.op("y", lambda y, a=t_sb, b=t_h: y.dma_start(out=a[:], in_=b.ap()), inc=("dy", 16))
            q.op("v", lambda v: v.memset(onesc[:], 1.0), inc=("v", 1))
            q.op("v", lambda v: v.memset(onesr[:], 1.0), wait=[("dy", None)], inc=("v", 1))

        def ln_transpose(xsrc, x_lnT, sp):
            with ExitStack() as ps:
                e = ps.enter_context
                xt = [e(nc.sbuf_tensor(f"{sp}_xt{i}", [128, D], F32)) for i in range(2)]
                lnb = e(nc.sbuf_tensor(f"{sp}_lnb", [128, D], F32))
                sq = e(nc.sbuf_tensor(f"{sp}_sq", [128, D], F32))
                st = e(nc.sbuf_tensor(f"{sp}_st", [128, 12], F32))
                pst = [e(nc.psum_tensor(f"{sp}_pt{i}", [128, 4, 128], F32)) for i in range(2)]
                if not SKIP_PRELOAD:
                    q.op("s", lambda s: s.preload_act_table(AF.Sqrt), inc=("s", 1))
                for j in range(globals().get("LNT_NJ", NT)):
                    b = xt[j % 2]
                    q.op("y", lambda y, b=b, j=j: y.dma_start(
                        out=b[:], in_=xsrc.ap()[j * 128:(j + 1) * 128, :]),
                        inc=("dy", 16), wait=[("v", None), ("s", None), ("t", None)],
                        tag=f"{sp} lnload {j}")
                    dyv = q.cnt["dy"]
                    if LNT_MODE == 2:
                        sv = q.cnt["s"]
                        for i0 in range(0, NI, 4):
                            pp = pst[(i0 // 4) % 2]
                            for i in range(i0, i0 + 4):
                                q.op("t", lambda t, pp=pp, i=i, i0=i0, b=b: t.transpose(
                                    pp[:, i - i0, :], b[:, i * 128:(i + 1) * 128], ident[:]),
                                    wait=[("dy", dyv), ("v", None)] if i == i0 else None,
                                    inc=("t", 1) if i == i0 + 3 else None)
                            q.op("v", lambda v, pp=pp, i0=i0, j=j: v.tensor_copy(
                                x_lnT[:, i0:i0 + 4, j * 128:(j + 1) * 128], pp[:]),
                                wait=[("t", None)], inc=("v", 1))
                        continue
                    q.op("v", lambda v, b=b: v.reduce_sum(st[:, 0:1], b[:], AX.X),
                         wait=[("dy", dyv)], lock=True)
                    q.op("s", lambda s, b=b: s.activation(sq[:], b[:], AF.Square,
                                                          accum_out=st[:, 1:2]),
                         wait=[("dy", dyv)], inc=("s", 1))
                    q.op("v", lambda v: v.tensor_scalar(st[:, 2:3], st[:, 0:1], 1.0 / D, None, ALU.mult),
                         wait=[("s", None)], lock=True)
                    q.op("v", lambda v: v.tensor_scalar(st[:, 3:4], st[:, 1:2], 1.0 / D, None, ALU.mult),
                         lock=True)
                    q.op("v", lambda v: v.tensor_tensor(st[:, 4:5], st[:, 2:3], st[:, 2:3], ALU.mult),
                         lock=True)
                    q.op("v", lambda v: v.tensor_tensor(st[:, 5:6], st[:, 3:4], st[:, 4:5], ALU.subtract),
                         lock=True)
                    q.op("v", lambda v: v.tensor_scalar(st[:, 5:6], st[:, 5:6], 1e-5, None, ALU.add),
                         lock=True)
                    q.op("s", lambda s: s.activation(st[:, 6:7], st[:, 5:6], AF.Sqrt),
                         wait=[("v", None)], inc=("s", 1))
                    q.op("v", lambda v: v.reciprocal(st[:, 7:8], st[:, 6:7]), wait=[("s", None)],
                         lock=True)
                    q.op("v", lambda v: v.scalar_tensor_tensor(
                        st[:, 8:9], st[:, 2:3], -1.0, st[:, 7:8], ALU.mult, ALU.mult),
                        lock=True)
                    q.op("s", lambda s, b=b: s.activation(lnb[:], b[:], AF.Identity,
                                                          bias=st[:, 8:9], scale=st[:, 7:8]),
                         wait=[("v", None), ("t", None)], inc=("s", 1))
                    sv = q.cnt["s"]
                    if LNT_MODE == 1:
                        q.op("v", lambda v, j=j: v.memset(
                            x_lnT[:, :, j * 128:(j + 1) * 128], 0.01),
                            wait=[("s", sv)], inc=("v", 1))
                        continue
                    for i0 in range(0, NI, 4):
                        pp = pst[(i0 // 4) % 2]
                        for i in range(i0, i0 + 4):
                            q.op("t", lambda t, pp=pp, i=i, i0=i0: t.transpose(
                                pp[:, i - i0, :], lnb[:, i * 128:(i + 1) * 128], ident[:]),
                                wait=[("s", sv), ("v", None)] if i == i0 else None,
                                inc=("t", 1) if i == i0 + 3 else None)
                        q.op("v", lambda v, pp=pp, i0=i0, j=j: v.tensor_copy(
                            x_lnT[:, i0:i0 + 4, j * 128:(j + 1) * 128], pp[:]),
                            wait=[("t", None)], inc=("v", 1))
                q.flush()

        def make_attn_bufs(sp, e):
            return {
                "s0": e(nc.psum_tensor(f"{sp}_s0", [128, 1024], F32)),
                "s1": e(nc.psum_tensor(f"{sp}_s1", [128, 1024], F32)),
                "ptp": [e(nc.psum_tensor(f"{sp}_ptp{i}", [128, 4, 128], F32)) for i in range(2)],
                "psy": e(nc.psum_tensor(f"{sp}_psy", [128, 128], F32)),
                "psb": e(nc.sbuf_tensor(f"{sp}_Pp", [128, 2, 8, 128], F32)),
                "ptb": e(nc.sbuf_tensor(f"{sp}_PT", [128, 16, 128], F32)),
                "sta": e(nc.sbuf_tensor(f"{sp}_sta", [128, 10], F32)),
            }

        def attn_unit(bufs, q_ap, kfm, vtok, ns, y_emit, scale_p, xwaits=None):
            s0, s1 = bufs["s0"], bufs["s1"]
            ptp, psy = bufs["ptp"], bufs["psy"]
            psb, ptb, sta = bufs["psb"], bufs["ptb"], bufs["sta"]
            for r, sps in ((0, s0), (1, s1)):
                for i0 in range(0, ns, 4):
                    w = min(4, ns - i0)
                    wl = ([("s", None), ("v", None)] + (xwaits or [])) if (r == 0 and i0 == 0) else None
                    lastsc = (r == 1 and i0 + 4 >= ns)
                    q.op("t", lambda t, sps=sps, r=r, i0=i0, w=w: t.matmul(
                        sps[:, i0 * 128:(i0 + w) * 128], q_ap,
                        kfm[:, 8 * r + i0:8 * r + i0 + w, :]),
                        wait=wl, inc=("t", 1) if lastsc else None)
            tv = q.cnt["t"]
            q.op("v", lambda v: v.tensor_tensor(
                s0[:, (ns - 1) * 128:ns * 128], s0[:, (ns - 1) * 128:ns * 128],
                maska[:], ALU.add), wait=[("t", tv), ("s", None)], lock=True)
            q.op("v", lambda v: v.tensor_tensor(
                s1[:, (ns - 1) * 128:ns * 128], s1[:, (ns - 1) * 128:ns * 128],
                maskb[:], ALU.add), lock=True)
            q.op("v", lambda v: v.reduce_max(sta[:, 0:1], s0[:, 0:ns * 128], AX.X, negate=True),
                 lock=True)
            q.op("v", lambda v: v.reduce_max(sta[:, 1:2], s1[:, 0:ns * 128], AX.X, negate=True),
                 lock=True)
            q.op("v", lambda v: v.tensor_tensor(sta[:, 0:1], sta[:, 0:1], sta[:, 1:2], ALU.min),
                 lock=True)
            q.op("v", lambda v: v.tensor_scalar(sta[:, 2:3], sta[:, 0:1], ISQ, None, ALU.mult),
                 lock=True)
            vv = q.cnt["v"]
            pieces = []
            for r, sps in ((0, s0), (1, s1)):
                for i0 in range(0, ns, 4):
                    w = min(4, ns - i0)
                    col = 4 + len(pieces)
                    pieces.append(col)
                    q.op("s", lambda s, sps=sps, r=r, i0=i0, w=w, col=col: s.activation(
                        psb[:, r, i0:i0 + w, :],
                        sps[:, i0 * 128:(i0 + w) * 128].rearrange("p (a b) -> p a b", a=w),
                        AF.Exp, bias=sta[:, 2:3], scale=cst[:, 0:1],
                        accum_out=sta[:, col:col + 1]),
                        wait=[("v", vv), ("t", None)])
            q.op("s", lambda s: s.copy(sta[:, 3:4], sta[:, 4:5]), inc=("s", 1))
            sv = q.cnt["s"]
            for col in pieces[1:]:
                q.op("v", lambda v, col=col: v.tensor_tensor(
                    sta[:, 3:4], sta[:, 3:4], sta[:, col:col + 1], ALU.add),
                    wait=[("s", sv)], lock=True)
            q.op("v", lambda v: v.reciprocal(sta[:, 3:4], sta[:, 3:4]), lock=True)
            if scale_p:
                for r in range(2):
                    q.op("v", lambda v, r=r: v.tensor_scalar(
                        psb[:, r, 0:ns, :], psb[:, r, 0:ns, :], sta[:, 3:4], None, ALU.mult),
                        lock=True)
            pv = q.cnt["v"]
            nsl = 2 * ns
            for b0 in range(0, nsl, 4):
                pp = ptp[(b0 // 4) % 2]
                w = min(4, nsl - b0)
                for kk in range(w):
                    slot = b0 + kk
                    r, i = slot // ns, slot % ns
                    q.op("t", lambda t, pp=pp, kk=kk, r=r, i=i: t.transpose(
                        pp[:, kk, :], psb[:, r, i, :], ident[:]),
                        wait=([("v", pv)] if (b0 == 0 and kk == 0)
                              else ([("v", None)] if kk == 0 else None)),
                        inc=("t", 1) if kk == w - 1 else None)
                q.op("v", lambda v, pp=pp, b0=b0, w=w: v.tensor_copy(
                    ptb[:, b0:b0 + w, :], pp[:, 0:w, :]),
                    wait=[("t", None)], inc=("v", 1))
            ptv = q.cnt["v"]
            for slot in range(nsl):
                r, i = slot // ns, slot % ns
                vslot = 8 * r + i
                first, last = slot == 0, slot == nsl - 1
                if scale_p:
                    q.op("t", lambda t, vslot=vslot, slot=slot, first=first, last=last: t.matmul(
                        psy[:], vtok[:, vslot, :], ptb[:, slot, :], start=first, stop=last),
                        wait=[("v", ptv)] if first else None, inc=("t", 1) if last else None)
                else:
                    q.op("t", lambda t, vslot=vslot, slot=slot, first=first, last=last: t.matmul(
                        psy[:], ptb[:, slot, :], vtok[:, vslot, :], start=first, stop=last),
                        wait=[("v", ptv)] if first else None, inc=("t", 1) if last else None)
            y_emit(psy, sta[:, 3:4])

        def l0f(c, first):
            sp = f"l0f{c}"
            xsrc = xin if first else xres
            with ExitStack() as ps:
                e = ps.enter_context
                q_all = e(nc.sbuf_tensor(f"{sp}_qall", [128, H, NT, 128], F32))
                kfm = e(nc.sbuf_tensor(f"{sp}_kfm", [128, 16, 128], F32))
                vtok = e(nc.sbuf_tensor(f"{sp}_vtok", [128, 16, 128], F32))
                pa = ExitStack()
                e1 = pa.enter_context
                x_lnT = e1(nc.sbuf_tensor(f"{sp}_xlnT", [128, NI, TOK], F32))
                ln_transpose(xsrc, x_lnT, sp)
                q.op("s", lambda s: s.preload_act_table(AF.Exp), inc=("s", 1))
                if dbg_h is not None and c == 0 and globals().get("DBG_WHAT", "xlnT") == "xlnT":
                    q.op("y", lambda y: y.dma_start(
                        out=dbg_h.ap().rearrange("(c p) n -> p c n", p=128), in_=x_lnT[:]),
                        wait=[("v", None)], inc=("dy", 16), tag="dbg xlnT")
                if L0F_CUT < 2:
                    pa.close()
                    return

                kvsh = e1(nc.sbuf_tensor(f"{sp}_kvsh", [128, NT, 256], F32))
                zind = e1(nc.sbuf_tensor(f"{sp}_zind", [128, NT, 16], F32))
                zgt = e1(nc.sbuf_tensor(f"{sp}_zgt", [128, NT, 16], F32))
                zst = e1(nc.sbuf_tensor(f"{sp}_zst", [128, NT, 4], F32))
                if not globals().get("SKIP_KVSH_MS", False):
                    q.op("v", lambda v: v.memset(kvsh[:], 0.0), inc=("v", 1))

                with ExitStack() as p2:
                    e2 = p2.enter_context
                    wgt = e2(nc.sbuf_tensor(f"{sp}_wgt", [128, NI, 16], F32))
                    psz = e2(nc.psum_tensor(f"{sp}_psz", [128, 16], F32))
                    zsc = e2(nc.sbuf_tensor(f"{sp}_zsc", [128, 16], F32))
                    if not globals().get("SKIP_WGT", False):
                        q.op("y", lambda y: y.dma_start(
                            out=wgt[:], in_=wg.ap().rearrange("(c p) n -> p c n", p=128)),
                            inc=("dy", 16))
                    dyv = q.cnt["dy"]
                    for j in range(NT) if not L0F_SKIP_GATE else []:
                        for i in range(NI):
                            q.op("t", lambda t, i=i, j=j: t.matmul(
                                psz[:], x_lnT[:, i, j * 128:(j + 1) * 128], wgt[:, i, :],
                                start=(i == 0), stop=(i == NI - 1)),
                                wait=[("dy", dyv), ("v", None)] if i == 0 else None,
                                inc=("t", 1) if i == NI - 1 else None,
                                tag=f"{sp} z {j}")
                        q.op("v", lambda v, j=j: v.tensor_copy(zind[:, j, :], psz[:]),
                             wait=[("t", None)], lock=True)
                        q.op("v", lambda v, j=j: v.reduce_max(zst[:, j, 0:1], zind[:, j, :],
                                                              AX.X, negate=True), lock=True)
                        q.op("v", lambda v, j=j: v.tensor_scalar(
                            zst[:, j, 1:2], zst[:, j, 0:1], -1.0, None, ALU.mult), lock=True)
                        q.op("s", lambda s, j=j: s.activation(
                            zsc[:], zind[:, j, :], AF.Exp, bias=zst[:, j, 0:1],
                            scale=cst[:, 1:2],
                            accum_out=zst[:, j, 2:3]), wait=[("v", None)], inc=("s", 1))
                        q.op("v", lambda v, j=j: v.tensor_scalar(
                            zind[:, j, :], zind[:, j, :], zst[:, j, 1:2], None, ALU.is_ge),
                            wait=[("s", None)], lock=True)
                        q.op("v", lambda v, j=j: v.reciprocal(zst[:, j, 3:4], zst[:, j, 2:3]),
                            lock=True)
                        q.op("v", lambda v, j=j: v.tensor_scalar(
                            zgt[:, j, :], zind[:, j, :], zst[:, j, 3:4], None, ALU.mult),
                            lock=True)
                    q.flush()
                if L0F_CUT < 3:
                    pa.close()
                    return

                wsl = e1(nc.sbuf_tensor(f"{sp}_wsl", [128, NI, 512], F32))
                with ExitStack() as p2:
                    e2 = p2.enter_context
                    pskv = [e2(nc.psum_tensor(f"{sp}_pskv{i}", [128, 512], F32)) for i in range(2)]
                    oglist = ([] if L0F_SKIP_KV else list(range(4, 12))) + \
                             ([] if L0F_SKIP_Q else list(range(0, 4)))
                    oglist = oglist[:globals().get("L0F_NOG", 99)]
                    for ogi, og in enumerate(oglist):
                        tv = q.cnt["t"]
                        q.op("y", lambda y, og=og: y.dma_start(
                            out=wsl[:], in_=wqkv.ap()[:, og * 512:(og + 1) * 512]
                            .rearrange("(c p) n -> p c n", p=128)),
                            inc=("dy", 16), wait=[("t", tv)], tag=f"{sp} wsl {og}")
                        dyv = q.cnt["dy"]
                        if L0F_LOAD_ONLY:
                            continue
                        if og >= 4:
                            sel = zind if og < 8 else zgt
                            dst0 = 0 if og < 8 else 128
                            for j in range(NT):
                                pp = pskv[j % 2]
                                for i in range(NI):
                                    q.op("t", lambda t, pp=pp, i=i, j=j: t.matmul(
                                        pp[:], x_lnT[:, i, j * 128:(j + 1) * 128], wsl[:, i, :],
                                        start=(i == 0), stop=(i == NI - 1)),
                                        wait=[("dy", dyv), ("v", None)] if i == 0 else None,
                                        inc=("t", 1) if i == NI - 1 else None,
                                        tag=f"{sp} kv {og}.{j}")
                                tv2 = q.cnt["t"]
                                for hh in range(4):
                                    head = (og - (4 if og < 8 else 8)) * 4 + hh
                                    q.op("v", lambda v, pp=pp, hh=hh, head=head, j=j, dst0=dst0,
                                         sel=sel: v.scalar_tensor_tensor(
                                             kvsh[:, j, dst0:dst0 + 128],
                                             pp[:, hh * 128:(hh + 1) * 128],
                                             sel[:, j, head:head + 1],
                                             kvsh[:, j, dst0:dst0 + 128],
                                             ALU.mult, ALU.add),
                                         wait=[("t", tv2)] if hh == 0 else None,
                                         lock=True)
                        else:
                            ngrp = globals().get("L0F_NGRP", 8)
                            for oc in range(4):
                                head = og * 4 + oc
                                for jh in range(2):
                                    if oc * 2 + jh >= ngrp:
                                        continue
                                    pp = pskv[(oc * 2 + jh) % 2]
                                    for i in range(NI):
                                        q.op("t", lambda t, pp=pp, i=i, oc=oc, jh=jh: t.matmul(
                                            pp[:], wsl[:, i, oc * 128:(oc + 1) * 128],
                                            x_lnT[:, i, jh * 512:(jh + 1) * 512],
                                            start=(i == 0), stop=(i == NI - 1)),
                                            wait=[("dy", dyv), ("v", None)] if i == 0 else None,
                                            inc=("t", 1) if i == NI - 1 else None,
                                            tag=f"{sp} q {head}.{jh}")
                                    if not globals().get("SKIP_QCOPY", False):
                                        q.op("v", lambda v, pp=pp, head=head, jh=jh: v.tensor_copy(
                                            q_all[:, head, jh * 4:(jh + 1) * 4, :],
                                            pp[:].rearrange("p (a b) -> p a b", a=4)),
                                            wait=[("t", None)], inc=("v", 1))
                        if og == 4 and dbg_h is not None and c == 0 and \
                                globals().get("DBG_WHAT") == "gate":
                            q.op("y", lambda y: y.dma_start(
                                out=dbg_h.ap()[0:128, 0:128].rearrange("p (a b) -> p a b", a=8),
                                in_=zind[:]), wait=[("v", None)], inc=("dy", 16))
                            q.op("y", lambda y: y.dma_start(
                                out=dbg_h.ap()[0:128, 128:256].rearrange("p (a b) -> p a b", a=8),
                                in_=zgt[:]), inc=("dy", 16))
                            q.op("y", lambda y: y.dma_start(
                                out=dbg_h.ap()[0:128, 256:288].rearrange("p (a b) -> p a b", a=8),
                                in_=zst[:]), inc=("dy", 16))
                        if og == 11 and dbg_h is not None and c == 0 and \
                                globals().get("DBG_WHAT") == "kvsh":
                            q.op("y", lambda y: y.dma_start(
                                out=dbg_h.ap()[0:1024, 0:256].rearrange("(a p) n -> p a n", p=128),
                                in_=kvsh[:]), wait=[("v", None)], inc=("dy", 16), tag="dbg kvsh")
                        if og == 11 and not L0F_SKIP_AG:
                            q.op("y", lambda y: y.dma_start(
                                out=l0kv_in[c].ap().rearrange("(a p) n -> p a n", p=128),
                                in_=kvsh[:]), inc=("dy", 16), wait=[("v", None)])
                            dyb = q.cnt["dy"]
                            q.op("g", lambda g: g.collective_compute(
                                "AllGather", ALU.bypass, replica_groups=REPG,
                                ins=[l0kv_in[c].ap().opt()], outs=[l0kv_out[c].ap().opt()]),
                                inc=("cc", 1), wait=[("dy", dyb)], tag=f"{sp} AG")
                    q.flush()

                pa.close()
                if L0F_CUT < 4:
                    return
                with ExitStack() as p2:
                    e2 = p2.enter_context
                    ktok = e2(nc.sbuf_tensor(f"{sp}_ktok", [128, 16, 128], F32))
                    ptk = [e2(nc.psum_tensor(f"{sp}_ptk{i}", [128, 4, 128], F32)) for i in range(2)]
                    ccv = q.cnt["cc"]
                    q.op("y", lambda y: y.dma_start(
                        out=ktok[:], in_=l0kv_out[c].ap()[:, 0:128]
                        .rearrange("(a p) n -> p a n", p=128)), inc=("dy", 16),
                        wait=[("cc", ccv), ("t", None), ("v", None), ("s", None)])
                    q.op("y", lambda y: y.dma_start(
                        out=vtok[:], in_=l0kv_out[c].ap()[:, 128:256]
                        .rearrange("(a p) n -> p a n", p=128)), inc=("dy", 16))
                    dyv = q.cnt["dy"]
                    for b0 in range(0, 16, 4):
                        pp = ptk[(b0 // 4) % 2]
                        for kk in range(4):
                            q.op("t", lambda t, pp=pp, kk=kk, b0=b0: t.transpose(
                                pp[:, kk, :], ktok[:, b0 + kk, :], ident[:]),
                                wait=[("dy", dyv), ("v", None)] if kk == 0 else None,
                                inc=("t", 1) if kk == 3 else None)
                        q.op("v", lambda v, pp=pp, b0=b0: v.tensor_copy(
                            kfm[:, b0:b0 + 4, :], pp[:]), wait=[("t", None)], inc=("v", 1))
                    q.flush()

                if L0F_CUT < 5:
                    return
                ytile = e(nc.sbuf_tensor(f"{sp}_yt", [128, D], F32))
                xt = e(nc.sbuf_tensor(f"{sp}_xtr", [128, D], F32))
                with ExitStack() as p2:
                    bufs = make_attn_bufs(sp, p2.enter_context)
                    for j in range(NT):
                        for h in range(H):
                            def y_emit(psy, rme, h=h, j=j):
                                q.op("v", lambda v, rme=rme: v.tensor_scalar(
                                    swcl[:, 2:3], rme, 0.5, None, ALU.mult),
                                    wait=[("t", None)], lock=True)
                                q.op("v", lambda v, h=h, psy=psy: v.tensor_scalar(
                                    ytile[:, h * 128:(h + 1) * 128], psy[:], swcl[:, 2:3],
                                    None, ALU.mult), lock=True)
                                if h == H - 1:
                                    q.op("y", lambda y, j=j: y.dma_start(
                                        out=xt[:], in_=xsrc.ap()[j * 128:(j + 1) * 128, :]),
                                        inc=("dy", 16))
                                    q.op("v", lambda v: v.tensor_tensor(
                                        xt[:], xt[:], ytile[:], ALU.add),
                                        wait=[("dy", None)], inc=("v", 1))
                                    q.op("y", lambda y, j=j: y.dma_start(
                                        out=xres.ap()[j * 128:(j + 1) * 128, :], in_=xt[:]),
                                        wait=[("v", None)], inc=("dy", 16), tag=f"{sp} xst {j}")
                            attn_unit(bufs, q_all[:, h, j, :], kfm, vtok, j + 1,
                                      y_emit, scale_p=False)
                    q.flush()

        def l0a():
            sp = "l0a"
            with ExitStack() as ps:
                e = ps.enter_context
                x_lnT = e(nc.sbuf_tensor(f"{sp}_xlnT", [128, NI, TOK], F32))
                ln_transpose(xres, x_lnT, sp)
                q.op("s", lambda s: s.preload_act_table(AF.Tanh), inc=("s", 1))
                stfm = e(nc.sbuf_tensor(f"{sp}_stfm", [128, 8, TOK], F32))
                with ExitStack() as p2:
                    e2 = p2.enter_context
                    wsl = e2(nc.sbuf_tensor(f"{sp}_wsl", [128, NI, 512], F32))
                    pp2 = [e2(nc.psum_tensor(f"{sp}_ps{i}", [128, 512], F32)) for i in range(2)]
                    for og in range(2):
                        tv = q.cnt["t"]
                        q.op("y", lambda y, og=og: y.dma_start(
                            out=wsl[:], in_=bt.ap()[:, og * 512:(og + 1) * 512]
                            .rearrange("(c p) n -> p c n", p=128)),
                            inc=("dy", 16), wait=[("t", tv)])
                        dyv = q.cnt["dy"]
                        for nn in range(4):
                            for jh in range(2):
                                pp = pp2[(nn * 2 + jh) % 2]
                                for i in range(NI):
                                    q.op("t", lambda t, pp=pp, i=i, nn=nn, jh=jh: t.matmul(
                                        pp[:], wsl[:, i, nn * 128:(nn + 1) * 128],
                                        x_lnT[:, i, jh * 512:(jh + 1) * 512],
                                        start=(i == 0), stop=(i == NI - 1)),
                                        wait=[("dy", dyv), ("s", None)] if i == 0 else None,
                                        inc=("t", 1) if i == NI - 1 else None,
                                        tag=f"{sp} B {og}.{nn}.{jh}")
                                q.op("s", lambda s, pp=pp, og=og, nn=nn, jh=jh: s.activation(
                                    stfm[:, og * 4 + nn, jh * 512:(jh + 1) * 512], pp[:],
                                    AF.Tanh), wait=[("t", None)], inc=("s", 1))
                    q.flush()
                with ExitStack() as p2:
                    e2 = p2.enter_context
                    csl = e2(nc.sbuf_tensor(f"{sp}_csl", [128, 8, 512], F32))
                    dsl = e2(nc.sbuf_tensor(f"{sp}_dsl", [128, 4, 512], F32))
                    xts = [e2(nc.sbuf_tensor(f"{sp}_xts{j}", [128, D], F32)) for j in range(NT)]
                    pp2 = [e2(nc.psum_tensor(f"{sp}_pc{i}", [128, 512], F32)) for i in range(2)]
                    for j in range(NT):
                        q.op("y", lambda y, j=j: y.dma_start(
                            out=xts[j][:], in_=xres.ap()[j * 128:(j + 1) * 128, :]),
                            inc=("dy", 16), wait=[("s", None), ("t", None), ("v", None)],
                            tag=f"{sp} xload {j}")
                    for og in range(4):
                        tv = q.cnt["t"]
                        q.op("y", lambda y, og=og: y.dma_start(
                            out=csl[:], in_=ct.ap()[:, og * 512:(og + 1) * 512]
                            .rearrange("(c p) n -> p c n", p=128)),
                            inc=("dy", 16), wait=[("t", tv)])
                        q.op("y", lambda y, og=og: y.dma_start(
                            out=dsl[:], in_=ddiag.ap()[og]), inc=("dy", 16))
                        dyv = q.cnt["dy"]
                        for j in range(NT):
                            pp = pp2[j % 2]
                            for n8 in range(8):
                                q.op("t", lambda t, pp=pp, n8=n8, j=j: t.matmul(
                                    pp[:], stfm[:, n8, j * 128:(j + 1) * 128], csl[:, n8, :],
                                    start=(n8 == 0), stop=False),
                                    wait=[("dy", dyv), ("v", None)] if n8 == 0 else None,
                                    tag=f"{sp} C {j}.{og}")
                            for ii in range(4):
                                i = og * 4 + ii
                                q.op("t", lambda t, pp=pp, i=i, ii=ii, j=j: t.matmul(
                                    pp[:], x_lnT[:, i, j * 128:(j + 1) * 128], dsl[:, ii, :],
                                    start=False, stop=(ii == 3)),
                                    inc=("t", 1) if ii == 3 else None)
                            q.op("v", lambda v, pp=pp, og=og, j=j: v.tensor_tensor(
                                xts[j][:, og * 512:(og + 1) * 512],
                                xts[j][:, og * 512:(og + 1) * 512], pp[:], ALU.add),
                                wait=[("t", None)], inc=("v", 1))
                    for j in range(NT):
                        q.op("y", lambda y, j=j: y.dma_start(
                            out=xres.ap()[j * 128:(j + 1) * 128, :], in_=xts[j][:]),
                            wait=[("v", None)], inc=("dy", 16), tag=f"{sp} xst {j}")
                    q.flush()

        def wq():
            sp = "wq"
            with ExitStack() as ps:
                e = ps.enter_context
                wbuf = [e(nc.sbuf_tensor(f"{sp}_wb{i}", [128, DFF], F32)) for i in range(2)]
                acc = e(nc.sbuf_tensor(f"{sp}_acc", [128, 4], F32))
                ssc = e(nc.sbuf_tensor(f"{sp}_ssc", [1, 4], F32))
                psm = e(nc.psum_tensor(f"{sp}_psm", [1, 2], F32))
                psb = e(nc.psum_tensor(f"{sp}_psb", [128, 2], F32))
                u32 = e(nc.sbuf_tensor(f"{sp}_u32", [128, 2048], I32))
                uf = e(nc.sbuf_tensor(f"{sp}_uf", [128, 2048], F32))
                ti8 = e(nc.sbuf_tensor(f"{sp}_i8", [128, DFF], I8))
                q.op("v", lambda v: v.memset(acc[:], 0.0), inc=("v", 1))
                lastuse = [q.cnt["v"], q.cnt["v"]]
                for wi, (wsrc, nch, w_fd) in enumerate([(w1t, NI, DFF), (w2t, NO, D)]):
                    for chk in range(nch):
                        bi = chk % 2
                        b = wbuf[bi]
                        q.op("y", lambda y, b=b, wsrc=wsrc, chk=chk, w_fd=w_fd: y.dma_start(
                            out=b[:, 0:w_fd], in_=wsrc.ap()[chk * 128:(chk + 1) * 128, :]),
                            inc=("dy", 16),
                            wait=[("v", lastuse[bi]), ("t", None), ("s", None)],
                            tag=f"{sp} A {wi}.{chk}")
                        q.op("v", lambda v, b=b, w_fd=w_fd: v.tensor_reduce(
                            acc[:, 2:3], b[:, 0:w_fd], AX.X, ALU.add, apply_absolute_value=True),
                            wait=[("dy", None)], lock=True)
                        q.op("v", lambda v, wi=wi: v.tensor_tensor(
                            acc[:, wi:wi + 1], acc[:, wi:wi + 1], acc[:, 2:3], ALU.add),
                            lock=True)
                        lastuse[bi] = q.cnt["v"]
                q.op("t", lambda t: t.matmul(psm[:], onesc[:], acc[:, 0:2]),
                     wait=[("v", None)], inc=("t", 1))
                q.op("v", lambda v: v.tensor_scalar(ssc[:, 0:2], psm[:], 1.0 / (D * DFF), None,
                                                    ALU.mult), wait=[("t", None)], lock=True)
                q.op("v", lambda v: v.tensor_scalar(ssc[:, 0:2], ssc[:, 0:2], 1e-5, None, ALU.max),
                     lock=True)
                q.op("v", lambda v: v.reciprocal(ssc[:, 2:4], ssc[:, 0:2]), lock=True)
                q.op("t", lambda t: t.matmul(psb[:], onesr[:], ssc[:, 0:2]),
                     wait=[("v", None)], inc=("t", 1))
                q.op("v", lambda v: v.tensor_copy(swcl[:, 0:2], psb[:]),
                     wait=[("t", None)], inc=("v", 1))
                q.op("t", lambda t: t.matmul(psb[:], onesr[:], ssc[:, 2:4]),
                     wait=[("v", None)], inc=("t", 1))
                q.op("v", lambda v: v.tensor_copy(acc[:, 0:2], psb[:]),
                     wait=[("t", None)], inc=("v", 1))
                lastv = [q.cnt["v"], q.cnt["v"]]
                lastg = [q.cnt["g"], q.cnt["g"]]
                for wi, (wsrc, wdst, nch, w_fd) in enumerate(
                        [(w1t, w1q_h, NI, DFF), (w2t, w2q_h, NO, D)]):
                    for chk in range(nch):
                        bi = chk % 2
                        b = wbuf[bi]
                        eng = "v"
                        q.op("y", lambda y, b=b, wsrc=wsrc, chk=chk, w_fd=w_fd: y.dma_start(
                            out=b[:, 0:w_fd], in_=wsrc.ap()[chk * 128:(chk + 1) * 128, :]),
                            inc=("dy", 16),
                            wait=[("v", lastv[bi]), ("g", lastg[bi])], tag=f"{sp} B {wi}.{chk}")
                        dyv = q.cnt["dy"]
                        for pc in range(0, w_fd, 2048):
                            pw = min(2048, w_fd - pc)
                            q.op(eng, lambda x, b=b, pc=pc, pw=pw, wi=wi: x.tensor_scalar(
                                uf[:, 0:pw], b[:, pc:pc + pw], acc[:, wi:wi + 1], None, ALU.mult),
                                wait=[("dy", dyv)] + ([("g", None)] if eng == "v" else [("v", None)]))
                            q.op(eng, lambda x, pw=pw: x.tensor_copy(u32[:, 0:pw], uf[:, 0:pw]))
                            q.op(eng, lambda x, pw=pw: x.tensor_scalar(
                                u32[:, 0:pw], u32[:, 0:pw], -1, 1, ALU.max, ALU.min))
                            q.op(eng, lambda x, pc=pc, pw=pw: x.tensor_copy(
                                ti8[:, pc:pc + pw], u32[:, 0:pw]), inc=(eng, 1))
                        if eng == "v":
                            lastv[bi] = q.cnt["v"]
                        else:
                            lastg[bi] = q.cnt["g"]
                        q.op("y", lambda y, wdst=wdst, chk=chk, w_fd=w_fd: y.dma_start(
                            out=wdst.ap()[chk * 128:(chk + 1) * 128, :], in_=ti8[:, 0:w_fd]),
                            wait=[(eng, None)], inc=("dy", 16))
                q.flush()

        def l1f(c):
            sp = f"l1f{c}"
            with ExitStack() as ps:
                e = ps.enter_context
                combs = e(nc.sbuf_tensor(f"{sp}_cmb", [128, NT, 2], F32))
                msp = e(nc.sbuf_tensor(f"{sp}_msp", [128, NT, 16], F32))
                pab = ExitStack()
                xqT = pab.enter_context(nc.sbuf_tensor(f"{sp}_xqT", [128, NI, TOK], BF16))
                with ExitStack() as p2:
                    e2 = p2.enter_context
                    g1s = e2(nc.sbuf_tensor(f"{sp}_g1", [128, D], F32))
                    xt = e2(nc.sbuf_tensor(f"{sp}_xta", [128, D], F32))
                    lnb = e2(nc.sbuf_tensor(f"{sp}_lnb", [128, D], F32))
                    sq = e2(nc.sbuf_tensor(f"{sp}_sq", [128, D], F32))
                    kb = e2(nc.sbuf_tensor(f"{sp}_kb", [128, D], BF16))
                    k32 = e2(nc.sbuf_tensor(f"{sp}_k32", [128, D], I32))
                    st = e2(nc.sbuf_tensor(f"{sp}_sta", [128, 16], F32))
                    ptb = [e2(nc.psum_tensor(f"{sp}_ptb{i}", [128, 4, 128], F32)) for i in range(2)]
                    q.op("y", lambda y: y.dma_start(out=g1s[:], in_=g1b_h.ap()), inc=("dy", 16))
                    q.op("s", lambda s: s.preload_act_table(AF.Sqrt), inc=("s", 1))
                    for j in range(NT):
                        q.op("y", lambda y, j=j: y.dma_start(
                            out=xt[:], in_=xres.ap()[j * 128:(j + 1) * 128, :]),
                            inc=("dy", 16), wait=[("v", None), ("s", None), ("t", None)],
                            tag=f"{sp} a {j}")
                        dyv = q.cnt["dy"]
                        q.op("v", lambda v: v.reduce_sum(st[:, 0:1], xt[:], AX.X),
                             wait=[("dy", dyv)], lock=True)
                        q.op("s", lambda s: s.activation(sq[:], xt[:], AF.Square,
                                                         accum_out=st[:, 1:2]),
                             wait=[("dy", dyv)], inc=("s", 1))
                        q.op("v", lambda v: v.tensor_scalar(st[:, 2:3], st[:, 0:1], 1.0 / D, None, ALU.mult),
                             wait=[("s", None)], lock=True)
                        q.op("v", lambda v: v.tensor_scalar(st[:, 3:4], st[:, 1:2], 1.0 / D, None, ALU.mult),
                             lock=True)
                        q.op("v", lambda v: v.tensor_tensor(st[:, 4:5], st[:, 2:3], st[:, 2:3], ALU.mult),
                             lock=True)
                        q.op("v", lambda v: v.tensor_tensor(st[:, 5:6], st[:, 3:4], st[:, 4:5], ALU.subtract),
                             lock=True)
                        q.op("v", lambda v: v.tensor_scalar(st[:, 5:6], st[:, 5:6], 1e-5, None, ALU.add),
                             lock=True)
                        q.op("s", lambda s: s.activation(st[:, 6:7], st[:, 5:6], AF.Sqrt),
                             wait=[("v", None)], inc=("s", 1))
                        q.op("v", lambda v: v.reciprocal(st[:, 7:8], st[:, 6:7]), wait=[("s", None)],
                             lock=True)
                        q.op("v", lambda v: v.scalar_tensor_tensor(
                            st[:, 8:9], st[:, 2:3], -1.0, st[:, 7:8], ALU.mult, ALU.mult),
                            lock=True)
                        q.op("s", lambda s: s.activation(lnb[:], xt[:], AF.Identity,
                                                         bias=st[:, 8:9], scale=st[:, 7:8]),
                             wait=[("v", None)], inc=("s", 1))
                        q.op("s", lambda s: s.activation(sq[:], lnb[:], AF.Square,
                                                         accum_out=st[:, 9:10]), inc=("s", 1))
                        q.op("v", lambda v: v.tensor_scalar(st[:, 10:11], st[:, 9:10], 1.0 / D,
                                                            None, ALU.mult), wait=[("s", None)],
                             lock=True)
                        q.op("v", lambda v: v.tensor_scalar(st[:, 10:11], st[:, 10:11], 1e-6,
                                                            None, ALU.add), lock=True)
                        q.op("s", lambda s: s.activation(st[:, 11:12], st[:, 10:11], AF.Sqrt),
                             wait=[("v", None)], inc=("s", 1))
                        q.op("v", lambda v: v.reciprocal(st[:, 12:13], st[:, 11:12]),
                             wait=[("s", None)], lock=True)
                        q.op("v", lambda v: v.scalar_tensor_tensor(
                            lnb[:], lnb[:], st[:, 12:13], g1s[:], ALU.mult, ALU.mult), lock=True)
                        q.op("v", lambda v: v.tensor_reduce(st[:, 13:14], lnb[:], AX.X, ALU.max,
                                                            apply_absolute_value=True), lock=True)
                        q.op("v", lambda v: v.tensor_scalar(st[:, 13:14], st[:, 13:14], 1e-5,
                                                            None, ALU.max), lock=True)
                        q.op("v", lambda v: v.reciprocal(st[:, 14:15], st[:, 13:14]), lock=True)
                        q.op("v", lambda v: v.tensor_scalar(st[:, 14:15], st[:, 14:15], 127.0,
                                                            None, ALU.mult), lock=True)
                        q.op("v", lambda v, j=j: v.tensor_scalar(
                            combs[:, j, 0:1], st[:, 13:14], 1.0 / 127.0, None, ALU.mult), lock=True)
                        q.op("v", lambda v, j=j: v.tensor_tensor(
                            combs[:, j, 0:1], combs[:, j, 0:1], swcl[:, 0:1], ALU.mult), lock=True)
                        if L1FA_MODE >= 2:
                            q.op("v", lambda v, j=j: v.memset(
                                xqT[:, :, j * 128:(j + 1) * 128], 1.0), inc=("v", 1))
                            continue
                        q.op("v", lambda v: v.tensor_scalar(lnb[:], lnb[:], st[:, 14:15], None,
                                                            ALU.mult), lock=True)
                        q.op("v", lambda v: v.tensor_copy(k32[:], lnb[:]))
                        q.op("v", lambda v: v.tensor_copy(kb[:], k32[:]), inc=("v", 1))
                        vv = q.cnt["v"]
                        if L1FA_MODE >= 1:
                            q.op("v", lambda v, j=j: v.memset(
                                xqT[:, :, j * 128:(j + 1) * 128], 1.0),
                                wait=[("v", vv)], inc=("v", 1))
                            continue
                        for i0 in range(0, NI, 4):
                            pp = ptb[(i0 // 4) % 2]
                            for i in range(i0, i0 + 4):
                                q.op("t", lambda t, pp=pp, i=i, i0=i0: t.matmul(
                                    pp[:, i - i0, :], kb[:, i * 128:(i + 1) * 128], identb[:],
                                    start=True, stop=True),
                                    wait=([("v", None)] if i == i0 else None),
                                    inc=("t", 1) if i == i0 + 3 else None)
                            q.op("v", lambda v, pp=pp, i0=i0, j=j: v.tensor_copy(
                                xqT[:, i0:i0 + 4, j * 128:(j + 1) * 128], pp[:]),
                                wait=[("t", None)], inc=("v", 1))
                    q.flush()
                if L1F_CUT < 2:
                    pab.close()
                    return

                with ExitStack() as p2:
                    e2 = p2.enter_context
                    w1s = [e2(nc.sbuf_tensor(f"{sp}_w1s{i}", [128, NI, 512], BF16)) for i in range(2)]
                    hsb = [e2(nc.sbuf_tensor(f"{sp}_hsb{i}", [128, 512], F32)) for i in range(2)]
                    hscr = e2(nc.sbuf_tensor(f"{sp}_hscr", [128, 512], F32))
                    g2s = e2(nc.sbuf_tensor(f"{sp}_g2", [128, DFF], F32))
                    pph = [e2(nc.psum_tensor(f"{sp}_pph{i}", [128, 512], F32)) for i in range(2)]
                    q.op("y", lambda y: y.dma_start(out=g2s[:], in_=g2b_h.ap()),
                         inc=("dy", 16), wait=[("t", None), ("v", None)])
                    wlast = [q.cnt["t"], q.cnt["t"]]
                    hlast = [q.cnt["dy"], q.cnt["dy"]]
                    for og in range(16):
                        wb = w1s[og % 2]
                        for hf in range(2):
                            q.op("g", lambda g, og=og, wb=wb, hf=hf: g.dma_start(
                                out=wb[:, hf * 8:(hf + 1) * 8, :],
                                in_=w1q_h.ap()[hf * 1024:(hf + 1) * 1024,
                                               og * 512:(og + 1) * 512]
                                .rearrange("(c p) n -> p c n", p=128)),
                                inc=("dg", 16), wait=[("t", wlast[og % 2])],
                                tag=f"{sp} w1 {og}.{hf}")
                        dgv = q.cnt["dg"]
                        for j in range(NT):
                            pp = pph[j % 2]
                            hb = hsb[j % 2]
                            for i in range(NI):
                                q.op("t", lambda t, pp=pp, wb=wb, i=i, j=j: t.matmul(
                                    pp[:], xqT[:, i, j * 128:(j + 1) * 128], wb[:, i, :],
                                    start=(i == 0), stop=(i == NI - 1)),
                                    wait=[("dg", dgv), ("s", None)] if i == 0 else None,
                                    inc=("t", 1) if i == NI - 1 else None,
                                    tag=f"{sp} h {og}.{j}")
                            q.op("s", lambda s, pp=pp, hb=hb, j=j: s.activation(
                                hb[:], pp[:], AF.Relu, scale=combs[:, j, 0:1]),
                                wait=[("t", None), ("dy", hlast[j % 2])], inc=("s", 1))
                            q.op("s", lambda s, hb=hb: s.activation(hb[:], hb[:], AF.Square),
                                 inc=("s", 1))
                            q.op("s", lambda s, hb=hb, j=j, og=og: s.activation(
                                hscr[:], hb[:], AF.Square, accum_out=msp[:, j, og:og + 1]),
                                inc=("s", 1))
                            q.op("v", lambda v, hb=hb, og=og: v.tensor_tensor(
                                hb[:], hb[:], g2s[:, og * 512:(og + 1) * 512], ALU.mult),
                                wait=[("s", None)], inc=("v", 1))
                            q.op("y", lambda y, hb=hb, j=j, og=og: y.dma_start(
                                out=hbuf.ap()[j * 128:(j + 1) * 128, og * 512:(og + 1) * 512],
                                in_=hb[:]), wait=[("v", None)], inc=("dy", 16))
                            hlast[j % 2] = q.cnt["dy"]
                        wlast[og % 2] = q.cnt["t"]
                    q.flush()
                pab.close()
                if L1F_CUT < 3:
                    return

                with ExitStack() as p2:
                    e2 = p2.enter_context
                    hqT = e2(nc.sbuf_tensor(f"{sp}_hqT", [128, NO, 4, 128], BF16))
                    hti = e2(nc.sbuf_tensor(f"{sp}_ht", [128, DFF], F32))
                    kb2 = e2(nc.sbuf_tensor(f"{sp}_kb2", [128, 2048], BF16))
                    k322 = e2(nc.sbuf_tensor(f"{sp}_k322", [128, 2048], I32))
                    st2 = e2(nc.sbuf_tensor(f"{sp}_st2", [128, 8], F32))
                    w2s = e2(nc.sbuf_tensor(f"{sp}_w2s", [128, NO, 512], BF16))
                    xtl = [e2(nc.sbuf_tensor(f"{sp}_xtl{i}", [128, D], F32)) for i in range(4)]
                    ptb2 = [e2(nc.psum_tensor(f"{sp}_pt2{i}", [128, 4, 128], F32)) for i in range(2)]
                    ppo = [e2(nc.psum_tensor(f"{sp}_ppo{i}", [128, 512], F32)) for i in range(2)]
                    for half in range(2):
                        jl = [half * 4 + jj for jj in range(4)]
                        for jj, j in enumerate(jl):
                            q.op("y", lambda y, j=j: y.dma_start(
                                out=hti[:], in_=xres.ap()[0:0 + 128, :]) if False else y.dma_start(
                                out=hti[:], in_=hbuf.ap()[j * 128:(j + 1) * 128, :]),
                                inc=("dy", 16), wait=[("v", None), ("t", None)],
                                tag=f"{sp} hload {j}")
                            dyv = q.cnt["dy"]
                            q.op("v", lambda v, j=j: v.reduce_sum(st2[:, 0:1], msp[:, j, :], AX.X),
                                 wait=[("dy", dyv)], lock=True)
                            q.op("v", lambda v: v.tensor_scalar(st2[:, 0:1], st2[:, 0:1],
                                                                1.0 / DFF, None, ALU.mult), lock=True)
                            q.op("v", lambda v: v.tensor_scalar(st2[:, 0:1], st2[:, 0:1], 1e-6,
                                                                None, ALU.add), lock=True)
                            q.op("s", lambda s: s.activation(st2[:, 1:2], st2[:, 0:1], AF.Sqrt),
                                 wait=[("v", None)], inc=("s", 1))
                            q.op("v", lambda v: v.reciprocal(st2[:, 2:3], st2[:, 1:2]),
                                 wait=[("s", None)], lock=True)
                            q.op("v", lambda v: v.tensor_scalar(
                                hti[:], hti[:], st2[:, 2:3], None, ALU.mult), lock=True)
                            q.op("v", lambda v: v.tensor_reduce(st2[:, 3:4], hti[:], AX.X,
                                                                ALU.max, apply_absolute_value=True),
                                lock=True)
                            q.op("v", lambda v: v.tensor_scalar(st2[:, 3:4], st2[:, 3:4], 1e-5,
                                                                None, ALU.max), lock=True)
                            q.op("v", lambda v: v.reciprocal(st2[:, 4:5], st2[:, 3:4]), lock=True)
                            q.op("v", lambda v: v.tensor_scalar(st2[:, 4:5], st2[:, 4:5], 127.0,
                                                                None, ALU.mult), lock=True)
                            q.op("v", lambda v, j=j: v.tensor_scalar(
                                combs[:, j, 1:2], st2[:, 3:4], 0.5 / 127.0, None, ALU.mult), lock=True)
                            q.op("v", lambda v, j=j: v.tensor_tensor(
                                combs[:, j, 1:2], combs[:, j, 1:2], swcl[:, 1:2], ALU.mult), lock=True)
                            q.op("v", lambda v: v.tensor_scalar(hti[:], hti[:], st2[:, 4:5],
                                                                None, ALU.mult), lock=True)
                            for pc in range(0, DFF, 2048):
                                q.op("v", lambda v, pc=pc: v.tensor_copy(k322[:], hti[:, pc:pc + 2048]))
                                q.op("v", lambda v, pc=pc: v.tensor_copy(kb2[:], k322[:]),
                                     inc=("v", 1))
                                vv = q.cnt["v"]
                                for o0 in range(0, 16, 4):
                                    pp = ptb2[(o0 // 4) % 2]
                                    for oc in range(o0, o0 + 4):
                                        q.op("t", lambda t, pp=pp, oc=oc, o0=o0: t.matmul(
                                            pp[:, oc - o0, :], kb2[:, oc * 128:(oc + 1) * 128],
                                            identb[:], start=True, stop=True),
                                            wait=([("v", None)] if oc == o0 else None),
                                            inc=("t", 1) if oc == o0 + 3 else None)
                                    q.op("v", lambda v, pp=pp, pc=pc, o0=o0, jj=jj: v.tensor_copy(
                                        hqT[:, pc // 128 + o0:pc // 128 + o0 + 4, jj, :], pp[:]),
                                        wait=[("t", None)], inc=("v", 1))
                            q.op("y", lambda y, jj=jj, j=j: y.dma_start(
                                out=xtl[jj][:], in_=xres.ap()[j * 128:(j + 1) * 128, :]),
                                inc=("dy", 16))
                        wlast = q.cnt["t"]
                        for dmc in range(4):
                            for qt in range(4):
                                q.op("g", lambda g, dmc=dmc, qt=qt: g.dma_start(
                                    out=w2s[:, qt * 16:(qt + 1) * 16, :],
                                    in_=w2q_h.ap()[qt * 2048:(qt + 1) * 2048,
                                                   dmc * 512:(dmc + 1) * 512]
                                    .rearrange("(c p) n -> p c n", p=128)),
                                    inc=("dg", 16), wait=[("t", None)],
                                    tag=f"{sp} w2 {half}.{dmc}.{qt}")
                            dgv = q.cnt["dg"]
                            for jj, j in enumerate(jl):
                                pp = ppo[jj % 2]
                                for oc in range(NO):
                                    q.op("t", lambda t, pp=pp, oc=oc, jj=jj: t.matmul(
                                        pp[:], hqT[:, oc, jj, :], w2s[:, oc, :],
                                        start=(oc == 0), stop=(oc == NO - 1)),
                                        wait=[("dg", dgv), ("v", None)] if oc == 0 else None,
                                        inc=("t", 1) if oc == NO - 1 else None,
                                        tag=f"{sp} y2 {half}.{dmc}.{jj}")
                                q.op("v", lambda v, pp=pp, jj=jj, j=j, dmc=dmc: v.scalar_tensor_tensor(
                                    xtl[jj][:, dmc * 512:(dmc + 1) * 512], pp[:],
                                    combs[:, j, 1:2], xtl[jj][:, dmc * 512:(dmc + 1) * 512],
                                    ALU.mult, ALU.add),
                                    wait=[("t", None), ("dy", None)], inc=("v", 1))
                        for jj, j in enumerate(jl):
                            q.op("y", lambda y, jj=jj, j=j: y.dma_start(
                                out=xres.ap()[j * 128:(j + 1) * 128, :], in_=xtl[jj][:]),
                                wait=[("v", None)], inc=("dy", 16), tag=f"{sp} xst {j}")
                    q.flush()

        def l1a():
            sp = "l1a"
            with ExitStack() as ps:
                e = ps.enter_context
                x_lnT = e(nc.sbuf_tensor(f"{sp}_xlnT", [128, NI, TOK], F32))
                ln_transpose(xres, x_lnT, sp)
                q.op("s", lambda s: s.preload_act_table(AF.Exp), inc=("s", 1))
                with ExitStack() as p2:
                    e2 = p2.enter_context
                    wsl = e2(nc.sbuf_tensor(f"{sp}_wsl", [128, NI, 512], F32))
                    kvb = e2(nc.sbuf_tensor(f"{sp}_kvb", [128, 1024], F32))
                    pp2 = [e2(nc.psum_tensor(f"{sp}_pk{i}", [128, 512], F32)) for i in range(2)]
                    for og in range(4):
                        tv = q.cnt["t"]
                        q.op("y", lambda y, og=og: y.dma_start(
                            out=wsl[:], in_=wkt.ap()[:, og * 512:(og + 1) * 512]
                            .rearrange("(c p) n -> p c n", p=128)),
                            inc=("dy", 16), wait=[("t", tv)], tag=f"{sp} wk {og}")
                        dyv = q.cnt["dy"]
                        for hc in range(4):
                            for jh in range(2):
                                pp = pp2[jh]
                                for i in range(NI):
                                    q.op("t", lambda t, pp=pp, i=i, hc=hc, jh=jh: t.matmul(
                                        pp[:], wsl[:, i, hc * 128:(hc + 1) * 128],
                                        x_lnT[:, i, jh * 512:(jh + 1) * 512],
                                        start=(i == 0), stop=(i == NI - 1)),
                                        wait=[("dy", dyv), ("v", None)] if i == 0 else None,
                                        inc=("t", 1) if i == NI - 1 else None,
                                        tag=f"{sp} K {og}.{hc}.{jh}")
                                q.op("v", lambda v, pp=pp, jh=jh: v.tensor_copy(
                                    kvb[:, jh * 512:(jh + 1) * 512], pp[:]),
                                    wait=[("t", None)], inc=("v", 1))
                            gh = og * 4 + hc
                            q.op("y", lambda y, gh=gh: y.dma_start(
                                out=kg_in.ap()[gh * 128:(gh + 1) * 128, :], in_=kvb[:]),
                                wait=[("v", None)], inc=("dy", 16))
                    for og in range(4):
                        tv = q.cnt["t"]
                        q.op("y", lambda y, og=og: y.dma_start(
                            out=wsl[:], in_=wvt.ap()[:, og * 512:(og + 1) * 512]
                            .rearrange("(c p) n -> p c n", p=128)),
                            inc=("dy", 16), wait=[("t", tv), ("dy", None)], tag=f"{sp} wv {og}")
                        dyv = q.cnt["dy"]
                        for j in range(NT):
                            pp = pp2[j % 2]
                            for i in range(NI):
                                q.op("t", lambda t, pp=pp, i=i, j=j: t.matmul(
                                    pp[:], x_lnT[:, i, j * 128:(j + 1) * 128], wsl[:, i, :],
                                    start=(i == 0), stop=(i == NI - 1)),
                                    wait=[("dy", dyv), ("v", None)] if i == 0 else None,
                                    inc=("t", 1) if i == NI - 1 else None,
                                    tag=f"{sp} V {og}.{j}")
                            q.op("v", lambda v, pp=pp: v.tensor_copy(kvb[:, 0:512], pp[:]),
                                 wait=[("t", None)], inc=("v", 1))
                            q.op("y", lambda y, j=j, og=og: y.dma_start(
                                out=vg_in.ap()[j * 128:(j + 1) * 128, og * 512:(og + 1) * 512],
                                in_=kvb[:, 0:512]), wait=[("v", None)], inc=("dy", 16))
                    dyb = q.cnt["dy"]
                    q.op("g", lambda g: g.collective_compute(
                        "AllGather", ALU.bypass, replica_groups=REPG,
                        ins=[kg_in.ap().opt()], outs=[kg_out.ap().opt()]),
                        inc=("cc", 1), wait=[("dy", dyb)], tag=f"{sp} AGK")
                    q.op("g", lambda g: g.collective_compute(
                        "AllGather", ALU.bypass, replica_groups=REPG,
                        ins=[vg_in.ap().opt()], outs=[vg_out.ap().opt()]),
                        inc=("cc", 1), tag=f"{sp} AGV")
                    q.flush()
                ccv = q.cnt["cc"]

                qh = e(nc.sbuf_tensor(f"{sp}_qh", [128, H, 4, 128], F32))
                yfm = e(nc.sbuf_tensor(f"{sp}_yfm", [128, H, 4, 128], F32))
                for half in range(2):
                    jl = [half * 4 + jj for jj in range(4)]
                    with ExitStack() as p2:
                        e2 = p2.enter_context
                        wslq = e2(nc.sbuf_tensor(f"{sp}_wsq{half}", [128, NI, 512], F32))
                        ppq = [e2(nc.psum_tensor(f"{sp}_pq{half}{i}", [128, 512], F32))
                               for i in range(2)]
                        for og in range(4):
                            tv = q.cnt["t"]
                            q.op("y", lambda y, og=og: y.dma_start(
                                out=wslq[:], in_=wqt.ap()[:, og * 512:(og + 1) * 512]
                                .rearrange("(c p) n -> p c n", p=128)),
                                inc=("dy", 16), wait=[("t", tv)], tag=f"{sp} wq{half} {og}")
                            dyv = q.cnt["dy"]
                            for hc in range(4):
                                h = og * 4 + hc
                                pp = ppq[hc % 2]
                                for i in range(NI):
                                    q.op("t", lambda t, pp=pp, i=i, hc=hc, half=half: t.matmul(
                                        pp[:], wslq[:, i, hc * 128:(hc + 1) * 128],
                                        x_lnT[:, i, half * 512:(half + 1) * 512],
                                        start=(i == 0), stop=(i == NI - 1)),
                                        wait=[("dy", dyv), ("v", None)] if i == 0 else None,
                                        inc=("t", 1) if i == NI - 1 else None,
                                        tag=f"{sp} Q {half}.{h}")
                                q.op("v", lambda v, pp=pp, h=h: v.tensor_copy(
                                    qh[:, h, :, :], pp[:].rearrange("p (a b) -> p a b", a=4)),
                                    wait=[("t", None)], inc=("v", 1))
                        q.flush()

                    with ExitStack() as p2:
                        e2 = p2.enter_context
                        kfh = [e2(nc.sbuf_tensor(f"{sp}_kfh{half}{i}", [128, 16, 128], F32))
                               for i in range(2)]
                        vth = [e2(nc.sbuf_tensor(f"{sp}_vth{half}{i}", [128, 16, 128], F32))
                               for i in range(2)]
                        bufs = make_attn_bufs(f"{sp}h{half}", e2)
                        kvt_last = [0, 0]
                        dy_kv = [0] * (H + 1)

                        def loadkv(h, w):
                            buf = h % 2
                            q.op("y", lambda y, h=h, buf=buf: y.dma_start(
                                out=kfh[buf][:, 0:8, :],
                                in_=kg_out.ap()[h * 128:(h + 1) * 128, :]
                                .rearrange("p (a b) -> p a b", a=8)),
                                inc=("dy", 16), wait=w, tag=f"{sp} kv {half}.{h}")
                            q.op("y", lambda y, h=h, buf=buf: y.dma_start(
                                out=kfh[buf][:, 8:16, :],
                                in_=kg_out.ap()[D + h * 128:D + (h + 1) * 128, :]
                                .rearrange("p (a b) -> p a b", a=8)), inc=("dy", 16))
                            q.op("y", lambda y, h=h, buf=buf: y.dma_start(
                                out=vth[buf][:], in_=vg_out.ap()[:, h * 128:(h + 1) * 128]
                                .rearrange("(a p) n -> p a n", p=128)), inc=("dy", 16))
                            dy_kv[h] = q.cnt["dy"]

                        loadkv(0, [("cc", ccv), ("t", None), ("v", None)])
                        for h in range(H):
                            if h + 1 < H:
                                loadkv(h + 1, [("t", kvt_last[(h + 1) % 2])])
                            for jj, j in enumerate(jl):
                                def y_emit(psy, rme, h=h, jj=jj):
                                    q.op("v", lambda v, h=h, jj=jj, psy=psy: v.tensor_copy(
                                        yfm[:, h, jj, :], psy[:]),
                                        wait=[("t", None)], inc=("v", 1))
                                attn_unit(bufs, qh[:, h, jj, :], kfh[h % 2], vth[h % 2],
                                          j + 1, y_emit, scale_p=True,
                                          xwaits=[("dy", dy_kv[h])])
                            kvt_last[h % 2] = q.cnt["t"]
                        q.flush()

                    with ExitStack() as p2:
                        e2 = p2.enter_context
                        wslo = e2(nc.sbuf_tensor(f"{sp}_wso{half}", [128, NI, 512], F32))
                        xtl = [e2(nc.sbuf_tensor(f"{sp}_xo{half}{i}", [128, D], F32))
                               for i in range(4)]
                        ppo = [e2(nc.psum_tensor(f"{sp}_po{half}{i}", [128, 512], F32))
                               for i in range(2)]
                        for jj, j in enumerate(jl):
                            q.op("y", lambda y, jj=jj, j=j: y.dma_start(
                                out=xtl[jj][:], in_=xres.ap()[j * 128:(j + 1) * 128, :]),
                                inc=("dy", 16), wait=[("t", None), ("v", None)])
                        for og in range(4):
                            tv = q.cnt["t"]
                            q.op("y", lambda y, og=og: y.dma_start(
                                out=wslo[:], in_=wot.ap()[:, og * 512:(og + 1) * 512]
                                .rearrange("(c p) n -> p c n", p=128)),
                                inc=("dy", 16), wait=[("t", tv)], tag=f"{sp} wo{half} {og}")
                            dyv = q.cnt["dy"]
                            for jj, j in enumerate(jl):
                                pp = ppo[jj % 2]
                                for hh in range(H):
                                    q.op("t", lambda t, pp=pp, hh=hh, jj=jj: t.matmul(
                                        pp[:], yfm[:, hh, jj, :], wslo[:, hh, :],
                                        start=(hh == 0), stop=(hh == H - 1)),
                                        wait=[("dy", dyv), ("v", None)] if hh == 0 else None,
                                        inc=("t", 1) if hh == H - 1 else None,
                                        tag=f"{sp} WO {half}.{og}.{jj}")
                                q.op("v", lambda v, pp=pp, jj=jj, og=og: v.tensor_tensor(
                                    xtl[jj][:, og * 512:(og + 1) * 512],
                                    xtl[jj][:, og * 512:(og + 1) * 512], pp[:], ALU.add),
                                    wait=[("t", None), ("dy", None)], inc=("v", 1))
                        for jj, j in enumerate(jl):
                            q.op("y", lambda y, jj=jj, j=j: y.dma_start(
                                out=xres.ap()[j * 128:(j + 1) * 128, :], in_=xtl[jj][:]),
                                wait=[("v", None)], inc=("dy", 16), tag=f"{sp} xst {j}")
                        q.flush()

        if n_sub >= 1:
            l0f(0, first=True)
        if n_sub >= 2:
            l0a()
        if n_sub >= 3:
            l0f(1, first=False)
        if n_sub >= 4:
            wq()
        if n_sub >= 5:
            l1f(0)
        if n_sub >= 6:
            l1a()
        if n_sub >= 7:
            l1f(1)

        src = xres if (n_sub >= 1 and L0F_CUT >= 5) else xin
        q.op("y", lambda y: y.dma_start(out=out_h.ap(), in_=src.ap()),
             wait=[("v", None), ("s", None), ("t", None), ("dy", None), ("dg", None),
                   ("cc", None), ("g", None)], inc=("dy", 16), tag="final out")
        fin = q.cnt["dy"]
        q.op("y", lambda y: y.nop(), wait=[("dy", fin)], tag="final wait")
        q.flush()
        q.audit()
    return nc


def _prep(inputs):
    f32 = np.float32
    x = np.asarray(inputs["L0_attn_x"], f32)
    Bm = np.asarray(inputs["L0_attn_B"], f32)
    Cm = np.asarray(inputs["L0_attn_C"], f32)
    Dv = np.asarray(inputs["L0_attn_D"], f32)
    Wg = np.asarray(inputs["L0_ffn_W_gate"], f32)
    Wqkv = np.asarray(inputs["L0_ffn_W_qkv"], f32)
    WK = np.asarray(inputs["L1_attn_W_K"], f32)
    WQ = np.asarray(inputs["L1_attn_W_Q"], f32)
    WV = np.asarray(inputs["L1_attn_W_V"], f32)
    WO = np.asarray(inputs["L1_attn_W_O"], f32)
    w1 = np.asarray(inputs["L1_ffn_w1"], f32)
    g1 = np.asarray(inputs["L1_ffn_g1"], f32)
    w2 = np.asarray(inputs["L1_ffn_w2"], f32)
    g2 = np.asarray(inputs["L1_ffn_g2"], f32)

    C = np.ascontiguousarray
    shared = {
        "wg": C(Wg.T),
        "wqkv": C(Wqkv.T),
        "bt": C(Bm.T),
        "ct": C(Cm.T),
        "wkt": C(WK.reshape(D, D).T),
        "wqt": C(WQ.reshape(D, D).T),
        "wvt": C(WV.reshape(D, D).T),
        "wot": C(WO.T),
        "w1t": C(w1.T),
        "w2t": C(w2.T),
        "g1b": C(np.broadcast_to(g1, (128, D))),
        "g2b": C(np.broadcast_to(g2, (128, DFF))),
        "identf": np.eye(128, dtype=f32),
        "identb": np.eye(128).astype(ml_dtypes.bfloat16),
    }
    dd = np.zeros((4, 128, 4, 512), f32)
    idx = np.arange(128)
    for og in range(4):
        for ii in range(4):
            dd[og, idx, ii, ii * 128 + idx] = Dv[og * 512 + ii * 128 + idx]
    shared["ddiag"] = dd
    cstv = np.zeros((128, 8), f32)
    cstv[:, 0] = ISQ
    cstv[:, 1] = 1.0
    shared["cst"] = cstv

    tril = np.where(np.arange(128)[None, :] <= np.arange(128)[:, None], 0.0, -1e9).astype(f32)
    zeros = np.zeros((128, 128), f32)
    full = np.full((128, 128), -1e9, f32)

    in_maps = []
    for core in range(8):
        b, h = core // 2, core % 2
        xs = C(x[b].reshape(16, 128, D)[h::2].reshape(TOK, D))
        m = dict(shared)
        m["xin"] = xs
        m["maska"] = tril if h == 0 else zeros
        m["maskb"] = full if h == 0 else tril
        in_maps.append(m)
    return x.shape, in_maps


def kernel(**inputs):
    shape, in_maps = _prep(inputs)
    nc = build()
    res = run_bass_kernel_spmd(nc, in_maps, core_ids=list(range(8)))
    out = np.empty(shape, np.float32)
    for core in range(8):
        b, h = core // 2, core % 2
        out[b].reshape(16, 128, D)[h::2] = res.results[core]["out"].reshape(NT, 128, D)
    return out


# version 06
# speedup vs baseline: 1.1926x; 1.0235x over previous
import math
from collections import defaultdict
from contextlib import ExitStack

import numpy as np
import ml_dtypes

import concourse.bass as bass
import concourse.mybir as mybir
from concourse.bass_utils import run_bass_kernel_spmd

F32 = mybir.dt.float32
BF16 = mybir.dt.bfloat16
I8 = mybir.dt.int8
I32 = mybir.dt.int32
AF = mybir.ActivationFunctionType
ALU = mybir.AluOpType
AX = mybir.AxisListType

D = 2048
TOK = 1024
NT = 8
NI = D // 128
H = 16
DFF = 8192
NO = DFF // 128
ISQ = 1.0 / math.sqrt(128.0)
N_SUB = 7
L0F_CUT = 5
L0F_SKIP_AG = False
L0F_SKIP_KV = False
L0F_SKIP_Q = False
L0F_LOAD_ONLY = False
L0F_SKIP_GATE = False
LNT_MODE = 0
L1F_CUT = 3
L1FA_MODE = 0
SKIP_PRELOAD = False
SKIP_INIT = False

ENG = {"t": "tensor", "v": "vector", "s": "scalar", "g": "gpsimd", "y": "sync"}


class Q:

    def __init__(self, nc, stk):
        self.nc = nc
        self.buf = {k: [] for k in ENG}
        self.meta = []
        self.cnt = defaultdict(int)
        self.sems = {}
        self.waited = {k: defaultdict(int) for k in ENG}
        self.stk = stk

    def sem(self, name):
        self.sems[name] = self.stk.enter_context(self.nc.semaphore(name))

    def op(self, key, f, inc=None, wait=None, tag="", lock=False):
        if lock:
            wait = list(wait or []) + [(key, self.cnt[key])]
            if inc is None:
                inc = (key, 1)
            else:
                assert inc[0] == key
        waits = []
        for w in (wait or []):
            sname, val = w if isinstance(w, tuple) else (w, None)
            if val is None:
                val = self.cnt[sname]
            if val > self.waited[key][sname]:
                waits.append((sname, val))
                self.waited[key][sname] = val
        incs = None
        if inc:
            sname, k = inc
            self.cnt[sname] += k
            incs = (sname, k)
        self.meta.append((key, waits, incs, tag))

        def fn(eng, waits=waits, incs=incs, f=f):
            for sname, val in waits:
                eng.wait_ge(self.sems[sname], val)
            instr = f(eng)
            if incs is not None:
                instr.then_inc(self.sems[incs[0]], incs[1])
        self.buf[key].append(fn)

    def flush(self):
        bufs = {k: v for k, v in self.buf.items() if v}
        self.buf = {k: [] for k in ENG}
        if not bufs:
            return
        with self.nc.Block() as blk:
            for key, fns in bufs.items():
                getattr(blk, ENG[key])(lambda eng, fns=fns: [f(eng) for f in fns])

    def audit(self):
        qs = {k: [] for k in ENG}
        for key, waits, incs, tag in self.meta:
            qs[key].append((waits, incs, tag))
        pos = {k: 0 for k in ENG}
        cnt = defaultdict(int)
        progress = True
        while progress:
            progress = False
            for k in ENG:
                while pos[k] < len(qs[k]):
                    waits, incs, tag = qs[k][pos[k]]
                    if any(cnt[s] < v for s, v in waits):
                        break
                    if incs:
                        cnt[incs[0]] += incs[1]
                    pos[k] += 1
                    progress = True
        stuck = {k: (len(qs[k]) - pos[k], qs[k][pos[k]]) for k in ENG if pos[k] < len(qs[k])}
        if stuck:
            raise RuntimeError(f"deadlock audit failed: {stuck}")


def build(n_sub=None):
    n_sub = N_SUB if n_sub is None else n_sub
    nc = bass.Bass("TRN2", target_bir_lowering=False, detect_race_conditions=False)

    xin = nc.dram_tensor("xin", [TOK, D], F32, kind="ExternalInput")
    wg = nc.dram_tensor("wg", [D, 16], F32, kind="ExternalInput")
    wqkv = nc.dram_tensor("wqkv", [D, 3 * D], F32, kind="ExternalInput")
    bt = nc.dram_tensor("bt", [D, 1024], F32, kind="ExternalInput")
    ct = nc.dram_tensor("ct", [1024, D], F32, kind="ExternalInput")
    ddiag = nc.dram_tensor("ddiag", [4, 128, 4, 512], F32, kind="ExternalInput")
    wkt = nc.dram_tensor("wkt", [D, D], F32, kind="ExternalInput")
    wqt = nc.dram_tensor("wqt", [D, D], F32, kind="ExternalInput")
    wvt = nc.dram_tensor("wvt", [D, D], F32, kind="ExternalInput")
    wot = nc.dram_tensor("wot", [D, D], F32, kind="ExternalInput")
    w1t = nc.dram_tensor("w1t", [D, DFF], F32, kind="ExternalInput")
    w2t = nc.dram_tensor("w2t", [DFF, D], F32, kind="ExternalInput")
    g1b_h = nc.dram_tensor("g1b", [128, D], F32, kind="ExternalInput")
    g2b_h = nc.dram_tensor("g2b", [128, DFF], F32, kind="ExternalInput")
    id_h = nc.dram_tensor("identf", [128, 128], F32, kind="ExternalInput")
    idb_h = nc.dram_tensor("identb", [128, 128], BF16, kind="ExternalInput")
    mka_h = nc.dram_tensor("maska", [128, 128], F32, kind="ExternalInput")
    mkb_h = nc.dram_tensor("maskb", [128, 128], F32, kind="ExternalInput")
    cst_h = nc.dram_tensor("cst", [128, 8], F32, kind="ExternalInput")
    out_h = nc.dram_tensor("out", [TOK, D], F32, kind="ExternalOutput")
    dbg_h = nc.dram_tensor("dbg", [2048, 1024], F32, kind="ExternalOutput") \
        if globals().get("DEBUG_OUT", False) else None

    xres = nc.dram_tensor("xres", [TOK, D], F32)
    l0kv_in = [nc.dram_tensor(f"l0kvi{c}", [TOK, 256], F32) for c in range(2)]
    l0kv_out = [nc.dram_tensor(f"l0kvo{c}", [2 * TOK, 256], F32) for c in range(2)]
    kg_in = nc.dram_tensor("kgi", [D, TOK], F32)
    kg_out = nc.dram_tensor("kgo", [2 * D, TOK], F32)
    vg_in = nc.dram_tensor("vgi", [TOK, D], F32)
    vg_out = nc.dram_tensor("vgo", [2 * TOK, D], F32)
    w1q_h = nc.dram_tensor("w1qd", [D, DFF], I8)
    w2q_h = nc.dram_tensor("w2qd", [DFF, D], I8)
    hbuf = nc.dram_tensor("hbuf", [TOK, DFF], F32)

    stk = ExitStack()
    with stk:
        q = Q(nc, stk)
        for s in ["dy", "dg", "cc", "t", "v", "s", "g"]:
            q.sem(s)
        REPG = [[0, 1], [2, 3], [4, 5], [6, 7]]
        sbp = lambda name, shape, dt=F32: stk.enter_context(nc.sbuf_tensor(name, shape, dt))

        ident = sbp("ident_s", [128, 128])
        identb = sbp("identb_s", [128, 128], BF16)
        maska = sbp("maska_s", [128, 128])
        maskb = sbp("maskb_s", [128, 128])
        cst = sbp("cst_s", [128, 8])
        onesc = sbp("onesc", [128, 1])
        onesr = sbp("onesr", [1, 128])
        swcl = sbp("swcl", [128, 4])

        if not SKIP_INIT:
            for t_sb, t_h in [(ident, id_h), (identb, idb_h), (maska, mka_h),
                              (maskb, mkb_h), (cst, cst_h)]:
                q.op("y", lambda y, a=t_sb, b=t_h: y.dma_start(out=a[:], in_=b.ap()), inc=("dy", 16))
            q.op("v", lambda v: v.memset(onesc[:], 1.0), inc=("v", 1))
            q.op("v", lambda v: v.memset(onesr[:], 1.0), wait=[("dy", None)], inc=("v", 1))

        def ln_transpose(xsrc, x_lnT, sp):
            with ExitStack() as ps:
                e = ps.enter_context
                xt = [e(nc.sbuf_tensor(f"{sp}_xt{i}", [128, D], F32)) for i in range(2)]
                lnb = e(nc.sbuf_tensor(f"{sp}_lnb", [128, D], F32))
                sq = e(nc.sbuf_tensor(f"{sp}_sq", [128, D], F32))
                st = e(nc.sbuf_tensor(f"{sp}_st", [128, 12], F32))
                pst = [e(nc.psum_tensor(f"{sp}_pt{i}", [128, 4, 128], F32)) for i in range(2)]
                if not SKIP_PRELOAD:
                    q.op("s", lambda s: s.preload_act_table(AF.Sqrt), inc=("s", 1))
                for j in range(globals().get("LNT_NJ", NT)):
                    b = xt[j % 2]
                    q.op("y", lambda y, b=b, j=j: y.dma_start(
                        out=b[:], in_=xsrc.ap()[j * 128:(j + 1) * 128, :]),
                        inc=("dy", 16), wait=[("v", None), ("s", None), ("t", None)],
                        tag=f"{sp} lnload {j}")
                    dyv = q.cnt["dy"]
                    if LNT_MODE == 2:
                        sv = q.cnt["s"]
                        for i0 in range(0, NI, 4):
                            pp = pst[(i0 // 4) % 2]
                            for i in range(i0, i0 + 4):
                                q.op("t", lambda t, pp=pp, i=i, i0=i0, b=b: t.transpose(
                                    pp[:, i - i0, :], b[:, i * 128:(i + 1) * 128], ident[:]),
                                    wait=[("dy", dyv), ("v", None)] if i == i0 else None,
                                    inc=("t", 1) if i == i0 + 3 else None)
                            q.op("v", lambda v, pp=pp, i0=i0, j=j: v.tensor_copy(
                                x_lnT[:, i0:i0 + 4, j * 128:(j + 1) * 128], pp[:]),
                                wait=[("t", None)], inc=("v", 1))
                        continue
                    q.op("v", lambda v, b=b: v.reduce_sum(st[:, 0:1], b[:], AX.X),
                         wait=[("dy", dyv)], lock=True)
                    q.op("s", lambda s, b=b: s.activation(sq[:], b[:], AF.Square,
                                                          accum_out=st[:, 1:2]),
                         wait=[("dy", dyv)], inc=("s", 1))
                    q.op("v", lambda v: v.tensor_scalar(st[:, 2:3], st[:, 0:1], 1.0 / D, None, ALU.mult),
                         wait=[("s", None)], lock=True)
                    q.op("v", lambda v: v.tensor_scalar(st[:, 3:4], st[:, 1:2], 1.0 / D, None, ALU.mult),
                         lock=True)
                    q.op("v", lambda v: v.tensor_tensor(st[:, 4:5], st[:, 2:3], st[:, 2:3], ALU.mult),
                         lock=True)
                    q.op("v", lambda v: v.tensor_tensor(st[:, 5:6], st[:, 3:4], st[:, 4:5], ALU.subtract),
                         lock=True)
                    q.op("v", lambda v: v.tensor_scalar(st[:, 5:6], st[:, 5:6], 1e-5, None, ALU.add),
                         lock=True)
                    q.op("s", lambda s: s.activation(st[:, 6:7], st[:, 5:6], AF.Sqrt),
                         wait=[("v", None)], inc=("s", 1))
                    q.op("v", lambda v: v.reciprocal(st[:, 7:8], st[:, 6:7]), wait=[("s", None)],
                         lock=True)
                    q.op("v", lambda v: v.scalar_tensor_tensor(
                        st[:, 8:9], st[:, 2:3], -1.0, st[:, 7:8], ALU.mult, ALU.mult),
                        lock=True)
                    q.op("s", lambda s, b=b: s.activation(lnb[:], b[:], AF.Identity,
                                                          bias=st[:, 8:9], scale=st[:, 7:8]),
                         wait=[("v", None), ("t", None)], inc=("s", 1))
                    sv = q.cnt["s"]
                    if LNT_MODE == 1:
                        q.op("v", lambda v, j=j: v.memset(
                            x_lnT[:, :, j * 128:(j + 1) * 128], 0.01),
                            wait=[("s", sv)], inc=("v", 1))
                        continue
                    for i0 in range(0, NI, 4):
                        pp = pst[(i0 // 4) % 2]
                        for i in range(i0, i0 + 4):
                            q.op("t", lambda t, pp=pp, i=i, i0=i0: t.transpose(
                                pp[:, i - i0, :], lnb[:, i * 128:(i + 1) * 128], ident[:]),
                                wait=[("s", sv), ("v", None)] if i == i0 else None,
                                inc=("t", 1) if i == i0 + 3 else None)
                        q.op("v", lambda v, pp=pp, i0=i0, j=j: v.tensor_copy(
                            x_lnT[:, i0:i0 + 4, j * 128:(j + 1) * 128], pp[:]),
                            wait=[("t", None)], inc=("v", 1))
                q.flush()

        def make_attn_bufs(sp, e):
            return {
                "s0": e(nc.psum_tensor(f"{sp}_s0", [128, 1024], F32)),
                "s1": e(nc.psum_tensor(f"{sp}_s1", [128, 1024], F32)),
                "ptp": [e(nc.psum_tensor(f"{sp}_ptp{i}", [128, 4, 128], F32)) for i in range(2)],
                "psy": e(nc.psum_tensor(f"{sp}_psy", [128, 128], F32)),
                "psb": e(nc.sbuf_tensor(f"{sp}_Pp", [128, 2, 8, 128], F32)),
                "ptb": e(nc.sbuf_tensor(f"{sp}_PT", [128, 16, 128], F32)),
                "sta": e(nc.sbuf_tensor(f"{sp}_sta", [128, 10], F32)),
            }

        def attn_unit(bufs, q_ap, kfm, vtok, ns, y_emit, scale_p, xwaits=None):
            s0, s1 = bufs["s0"], bufs["s1"]
            ptp, psy = bufs["ptp"], bufs["psy"]
            psb, ptb, sta = bufs["psb"], bufs["ptb"], bufs["sta"]
            for r, sps in ((0, s0), (1, s1)):
                for i0 in range(0, ns, 4):
                    w = min(4, ns - i0)
                    wl = ([("s", None), ("v", None)] + (xwaits or [])) if (r == 0 and i0 == 0) else None
                    lastsc = (r == 1 and i0 + 4 >= ns)
                    q.op("t", lambda t, sps=sps, r=r, i0=i0, w=w: t.matmul(
                        sps[:, i0 * 128:(i0 + w) * 128], q_ap,
                        kfm[:, 8 * r + i0:8 * r + i0 + w, :]),
                        wait=wl, inc=("t", 1) if lastsc else None)
            tv = q.cnt["t"]
            q.op("v", lambda v: v.tensor_tensor(
                s0[:, (ns - 1) * 128:ns * 128], s0[:, (ns - 1) * 128:ns * 128],
                maska[:], ALU.add), wait=[("t", tv), ("s", None)], lock=True)
            q.op("v", lambda v: v.tensor_tensor(
                s1[:, (ns - 1) * 128:ns * 128], s1[:, (ns - 1) * 128:ns * 128],
                maskb[:], ALU.add), lock=True)
            q.op("v", lambda v: v.reduce_max(sta[:, 0:1], s0[:, 0:ns * 128], AX.X, negate=True),
                 lock=True)
            q.op("v", lambda v: v.reduce_max(sta[:, 1:2], s1[:, 0:ns * 128], AX.X, negate=True),
                 lock=True)
            q.op("v", lambda v: v.tensor_tensor(sta[:, 0:1], sta[:, 0:1], sta[:, 1:2], ALU.min),
                 lock=True)
            q.op("v", lambda v: v.tensor_scalar(sta[:, 2:3], sta[:, 0:1], ISQ, None, ALU.mult),
                 lock=True)
            vv = q.cnt["v"]
            pieces = []
            for r, sps in ((0, s0), (1, s1)):
                for i0 in range(0, ns, 4):
                    w = min(4, ns - i0)
                    col = 4 + len(pieces)
                    pieces.append(col)
                    q.op("s", lambda s, sps=sps, r=r, i0=i0, w=w, col=col: s.activation(
                        psb[:, r, i0:i0 + w, :],
                        sps[:, i0 * 128:(i0 + w) * 128].rearrange("p (a b) -> p a b", a=w),
                        AF.Exp, bias=sta[:, 2:3], scale=cst[:, 0:1],
                        accum_out=sta[:, col:col + 1]),
                        wait=[("v", vv), ("t", None)])
            q.op("s", lambda s: s.copy(sta[:, 3:4], sta[:, 4:5]), inc=("s", 1))
            sv = q.cnt["s"]
            for col in pieces[1:]:
                q.op("v", lambda v, col=col: v.tensor_tensor(
                    sta[:, 3:4], sta[:, 3:4], sta[:, col:col + 1], ALU.add),
                    wait=[("s", sv)], lock=True)
            q.op("v", lambda v: v.reciprocal(sta[:, 3:4], sta[:, 3:4]), lock=True)
            if scale_p:
                for r in range(2):
                    q.op("v", lambda v, r=r: v.tensor_scalar(
                        psb[:, r, 0:ns, :], psb[:, r, 0:ns, :], sta[:, 3:4], None, ALU.mult),
                        lock=True)
            pv = q.cnt["v"]
            nsl = 2 * ns
            for b0 in range(0, nsl, 4):
                pp = ptp[(b0 // 4) % 2]
                w = min(4, nsl - b0)
                for kk in range(w):
                    slot = b0 + kk
                    r, i = slot // ns, slot % ns
                    q.op("t", lambda t, pp=pp, kk=kk, r=r, i=i: t.transpose(
                        pp[:, kk, :], psb[:, r, i, :], ident[:]),
                        wait=([("v", pv)] if (b0 == 0 and kk == 0)
                              else ([("v", None)] if kk == 0 else None)),
                        inc=("t", 1) if kk == w - 1 else None)
                q.op("v", lambda v, pp=pp, b0=b0, w=w: v.tensor_copy(
                    ptb[:, b0:b0 + w, :], pp[:, 0:w, :]),
                    wait=[("t", None)], inc=("v", 1))
            ptv = q.cnt["v"]
            for slot in range(nsl):
                r, i = slot // ns, slot % ns
                vslot = 8 * r + i
                first, last = slot == 0, slot == nsl - 1
                if scale_p:
                    q.op("t", lambda t, vslot=vslot, slot=slot, first=first, last=last: t.matmul(
                        psy[:], vtok[:, vslot, :], ptb[:, slot, :], start=first, stop=last),
                        wait=[("v", ptv)] if first else None, inc=("t", 1) if last else None)
                else:
                    q.op("t", lambda t, vslot=vslot, slot=slot, first=first, last=last: t.matmul(
                        psy[:], ptb[:, slot, :], vtok[:, vslot, :], start=first, stop=last),
                        wait=[("v", ptv)] if first else None, inc=("t", 1) if last else None)
            y_emit(psy, sta[:, 3:4])

        def l0f(c, first):
            sp = f"l0f{c}"
            xsrc = xin if first else xres
            with ExitStack() as ps:
                e = ps.enter_context
                q_all = e(nc.sbuf_tensor(f"{sp}_qall", [128, H, NT, 128], F32))
                kfm = e(nc.sbuf_tensor(f"{sp}_kfm", [128, 16, 128], F32))
                vtok = e(nc.sbuf_tensor(f"{sp}_vtok", [128, 16, 128], F32))
                pa = ExitStack()
                e1 = pa.enter_context
                x_lnT = e1(nc.sbuf_tensor(f"{sp}_xlnT", [128, NI, TOK], F32))
                ln_transpose(xsrc, x_lnT, sp)
                q.op("s", lambda s: s.preload_act_table(AF.Exp), inc=("s", 1))
                if dbg_h is not None and c == 0 and globals().get("DBG_WHAT", "xlnT") == "xlnT":
                    q.op("y", lambda y: y.dma_start(
                        out=dbg_h.ap().rearrange("(c p) n -> p c n", p=128), in_=x_lnT[:]),
                        wait=[("v", None)], inc=("dy", 16), tag="dbg xlnT")
                if L0F_CUT < 2:
                    pa.close()
                    return

                kvsh = e1(nc.sbuf_tensor(f"{sp}_kvsh", [128, NT, 256], F32))
                zind = e1(nc.sbuf_tensor(f"{sp}_zind", [128, NT, 16], F32))
                zgt = e1(nc.sbuf_tensor(f"{sp}_zgt", [128, NT, 16], F32))
                zst = e1(nc.sbuf_tensor(f"{sp}_zst", [128, NT, 4], F32))
                if not globals().get("SKIP_KVSH_MS", False):
                    q.op("v", lambda v: v.memset(kvsh[:], 0.0), inc=("v", 1))

                with ExitStack() as p2:
                    e2 = p2.enter_context
                    wgt = e2(nc.sbuf_tensor(f"{sp}_wgt", [128, NI, 16], F32))
                    psz = e2(nc.psum_tensor(f"{sp}_psz", [128, 16], F32))
                    zsc = e2(nc.sbuf_tensor(f"{sp}_zsc", [128, 16], F32))
                    if not globals().get("SKIP_WGT", False):
                        q.op("y", lambda y: y.dma_start(
                            out=wgt[:], in_=wg.ap().rearrange("(c p) n -> p c n", p=128)),
                            inc=("dy", 16))
                    dyv = q.cnt["dy"]
                    for j in range(NT) if not L0F_SKIP_GATE else []:
                        for i in range(NI):
                            q.op("t", lambda t, i=i, j=j: t.matmul(
                                psz[:], x_lnT[:, i, j * 128:(j + 1) * 128], wgt[:, i, :],
                                start=(i == 0), stop=(i == NI - 1)),
                                wait=[("dy", dyv), ("v", None)] if i == 0 else None,
                                inc=("t", 1) if i == NI - 1 else None,
                                tag=f"{sp} z {j}")
                        q.op("v", lambda v, j=j: v.tensor_copy(zind[:, j, :], psz[:]),
                             wait=[("t", None)], lock=True)
                        q.op("v", lambda v, j=j: v.reduce_max(zst[:, j, 0:1], zind[:, j, :],
                                                              AX.X, negate=True), lock=True)
                        q.op("v", lambda v, j=j: v.tensor_scalar(
                            zst[:, j, 1:2], zst[:, j, 0:1], -1.0, None, ALU.mult), lock=True)
                        q.op("s", lambda s, j=j: s.activation(
                            zsc[:], zind[:, j, :], AF.Exp, bias=zst[:, j, 0:1],
                            scale=cst[:, 1:2],
                            accum_out=zst[:, j, 2:3]), wait=[("v", None)], inc=("s", 1))
                        q.op("v", lambda v, j=j: v.tensor_scalar(
                            zind[:, j, :], zind[:, j, :], zst[:, j, 1:2], None, ALU.is_ge),
                            wait=[("s", None)], lock=True)
                        q.op("v", lambda v, j=j: v.reciprocal(zst[:, j, 3:4], zst[:, j, 2:3]),
                            lock=True)
                        q.op("v", lambda v, j=j: v.tensor_scalar(
                            zgt[:, j, :], zind[:, j, :], zst[:, j, 3:4], None, ALU.mult),
                            lock=True)
                    q.flush()
                if L0F_CUT < 3:
                    pa.close()
                    return

                wsl = e1(nc.sbuf_tensor(f"{sp}_wsl", [128, NI, 512], F32))
                with ExitStack() as p2:
                    e2 = p2.enter_context
                    pskv = [e2(nc.psum_tensor(f"{sp}_pskv{i}", [128, 512], F32)) for i in range(2)]
                    oglist = ([] if L0F_SKIP_KV else list(range(4, 12))) + \
                             ([] if L0F_SKIP_Q else list(range(0, 4)))
                    oglist = oglist[:globals().get("L0F_NOG", 99)]
                    for ogi, og in enumerate(oglist):
                        tv = q.cnt["t"]
                        for hf in range(2):
                            q.op("y", lambda y, og=og, hf=hf: y.dma_start(
                                out=wsl[:, hf * 8:(hf + 1) * 8, :],
                                in_=wqkv.ap()[hf * 1024:(hf + 1) * 1024,
                                              og * 512:(og + 1) * 512]
                                .rearrange("(c p) n -> p c n", p=128)),
                                inc=("dy", 16), wait=[("t", tv)], tag=f"{sp} wsl {og}.{hf}")
                        dyv = q.cnt["dy"]
                        if L0F_LOAD_ONLY:
                            continue
                        if og >= 4:
                            sel = zind if og < 8 else zgt
                            dst0 = 0 if og < 8 else 128
                            for j in range(NT):
                                pp = pskv[j % 2]
                                for i in range(NI):
                                    q.op("t", lambda t, pp=pp, i=i, j=j: t.matmul(
                                        pp[:], x_lnT[:, i, j * 128:(j + 1) * 128], wsl[:, i, :],
                                        start=(i == 0), stop=(i == NI - 1)),
                                        wait=[("dy", dyv), ("v", None)] if i == 0 else None,
                                        inc=("t", 1) if i == NI - 1 else None,
                                        tag=f"{sp} kv {og}.{j}")
                                tv2 = q.cnt["t"]
                                for hh in range(4):
                                    head = (og - (4 if og < 8 else 8)) * 4 + hh
                                    q.op("v", lambda v, pp=pp, hh=hh, head=head, j=j, dst0=dst0,
                                         sel=sel: v.scalar_tensor_tensor(
                                             kvsh[:, j, dst0:dst0 + 128],
                                             pp[:, hh * 128:(hh + 1) * 128],
                                             sel[:, j, head:head + 1],
                                             kvsh[:, j, dst0:dst0 + 128],
                                             ALU.mult, ALU.add),
                                         wait=[("t", tv2)] if hh == 0 else None,
                                         lock=True)
                        else:
                            ngrp = globals().get("L0F_NGRP", 8)
                            for oc in range(4):
                                head = og * 4 + oc
                                for jh in range(2):
                                    if oc * 2 + jh >= ngrp:
                                        continue
                                    pp = pskv[(oc * 2 + jh) % 2]
                                    for i in range(NI):
                                        q.op("t", lambda t, pp=pp, i=i, oc=oc, jh=jh: t.matmul(
                                            pp[:], wsl[:, i, oc * 128:(oc + 1) * 128],
                                            x_lnT[:, i, jh * 512:(jh + 1) * 512],
                                            start=(i == 0), stop=(i == NI - 1)),
                                            wait=[("dy", dyv), ("v", None)] if i == 0 else None,
                                            inc=("t", 1) if i == NI - 1 else None,
                                            tag=f"{sp} q {head}.{jh}")
                                    if not globals().get("SKIP_QCOPY", False):
                                        q.op("v", lambda v, pp=pp, head=head, jh=jh: v.tensor_copy(
                                            q_all[:, head, jh * 4:(jh + 1) * 4, :],
                                            pp[:].rearrange("p (a b) -> p a b", a=4)),
                                            wait=[("t", None)], inc=("v", 1))
                        if og == 4 and dbg_h is not None and c == 0 and \
                                globals().get("DBG_WHAT") == "gate":
                            q.op("y", lambda y: y.dma_start(
                                out=dbg_h.ap()[0:128, 0:128].rearrange("p (a b) -> p a b", a=8),
                                in_=zind[:]), wait=[("v", None)], inc=("dy", 16))
                            q.op("y", lambda y: y.dma_start(
                                out=dbg_h.ap()[0:128, 128:256].rearrange("p (a b) -> p a b", a=8),
                                in_=zgt[:]), inc=("dy", 16))
                            q.op("y", lambda y: y.dma_start(
                                out=dbg_h.ap()[0:128, 256:288].rearrange("p (a b) -> p a b", a=8),
                                in_=zst[:]), inc=("dy", 16))
                        if og == 11 and dbg_h is not None and c == 0 and \
                                globals().get("DBG_WHAT") == "kvsh":
                            q.op("y", lambda y: y.dma_start(
                                out=dbg_h.ap()[0:1024, 0:256].rearrange("(a p) n -> p a n", p=128),
                                in_=kvsh[:]), wait=[("v", None)], inc=("dy", 16), tag="dbg kvsh")
                        if og == 11 and not L0F_SKIP_AG:
                            q.op("y", lambda y: y.dma_start(
                                out=l0kv_in[c].ap().rearrange("(a p) n -> p a n", p=128),
                                in_=kvsh[:]), inc=("dy", 16), wait=[("v", None)])
                            dyb = q.cnt["dy"]
                            q.op("g", lambda g: g.collective_compute(
                                "AllGather", ALU.bypass, replica_groups=REPG,
                                ins=[l0kv_in[c].ap().opt()], outs=[l0kv_out[c].ap().opt()]),
                                inc=("cc", 1), wait=[("dy", dyb)], tag=f"{sp} AG")
                    q.flush()

                pa.close()
                if L0F_CUT < 4:
                    return
                with ExitStack() as p2:
                    e2 = p2.enter_context
                    ktok = e2(nc.sbuf_tensor(f"{sp}_ktok", [128, 16, 128], F32))
                    ptk = [e2(nc.psum_tensor(f"{sp}_ptk{i}", [128, 4, 128], F32)) for i in range(2)]
                    ccv = q.cnt["cc"]
                    q.op("y", lambda y: y.dma_start(
                        out=ktok[:], in_=l0kv_out[c].ap()[:, 0:128]
                        .rearrange("(a p) n -> p a n", p=128)), inc=("dy", 16),
                        wait=[("cc", ccv), ("t", None), ("v", None), ("s", None)])
                    q.op("y", lambda y: y.dma_start(
                        out=vtok[:], in_=l0kv_out[c].ap()[:, 128:256]
                        .rearrange("(a p) n -> p a n", p=128)), inc=("dy", 16))
                    dyv = q.cnt["dy"]
                    for b0 in range(0, 16, 4):
                        pp = ptk[(b0 // 4) % 2]
                        for kk in range(4):
                            q.op("t", lambda t, pp=pp, kk=kk, b0=b0: t.transpose(
                                pp[:, kk, :], ktok[:, b0 + kk, :], ident[:]),
                                wait=[("dy", dyv), ("v", None)] if kk == 0 else None,
                                inc=("t", 1) if kk == 3 else None)
                        q.op("v", lambda v, pp=pp, b0=b0: v.tensor_copy(
                            kfm[:, b0:b0 + 4, :], pp[:]), wait=[("t", None)], inc=("v", 1))
                    q.flush()

                if L0F_CUT < 5:
                    return
                ytile = e(nc.sbuf_tensor(f"{sp}_yt", [128, D], F32))
                xt = e(nc.sbuf_tensor(f"{sp}_xtr", [128, D], F32))
                with ExitStack() as p2:
                    bufs = make_attn_bufs(sp, p2.enter_context)
                    for j in range(NT):
                        for h in range(H):
                            def y_emit(psy, rme, h=h, j=j):
                                q.op("v", lambda v, rme=rme: v.tensor_scalar(
                                    swcl[:, 2:3], rme, 0.5, None, ALU.mult),
                                    wait=[("t", None)], lock=True)
                                q.op("v", lambda v, h=h, psy=psy: v.tensor_scalar(
                                    ytile[:, h * 128:(h + 1) * 128], psy[:], swcl[:, 2:3],
                                    None, ALU.mult), lock=True)
                                if h == H - 1:
                                    q.op("y", lambda y, j=j: y.dma_start(
                                        out=xt[:], in_=xsrc.ap()[j * 128:(j + 1) * 128, :]),
                                        inc=("dy", 16))
                                    q.op("v", lambda v: v.tensor_tensor(
                                        xt[:], xt[:], ytile[:], ALU.add),
                                        wait=[("dy", None)], inc=("v", 1))
                                    q.op("y", lambda y, j=j: y.dma_start(
                                        out=xres.ap()[j * 128:(j + 1) * 128, :], in_=xt[:]),
                                        wait=[("v", None)], inc=("dy", 16), tag=f"{sp} xst {j}")
                            attn_unit(bufs, q_all[:, h, j, :], kfm, vtok, j + 1,
                                      y_emit, scale_p=False)
                    q.flush()

        def l0a():
            sp = "l0a"
            with ExitStack() as ps:
                e = ps.enter_context
                x_lnT = e(nc.sbuf_tensor(f"{sp}_xlnT", [128, NI, TOK], F32))
                ln_transpose(xres, x_lnT, sp)
                q.op("s", lambda s: s.preload_act_table(AF.Tanh), inc=("s", 1))
                stfm = e(nc.sbuf_tensor(f"{sp}_stfm", [128, 8, TOK], F32))
                with ExitStack() as p2:
                    e2 = p2.enter_context
                    wsl = e2(nc.sbuf_tensor(f"{sp}_wsl", [128, NI, 512], F32))
                    pp2 = [e2(nc.psum_tensor(f"{sp}_ps{i}", [128, 512], F32)) for i in range(2)]
                    for og in range(2):
                        tv = q.cnt["t"]
                        for hf in range(2):
                            q.op("y", lambda y, og=og, hf=hf: y.dma_start(
                                out=wsl[:, hf * 8:(hf + 1) * 8, :],
                                in_=bt.ap()[hf * 1024:(hf + 1) * 1024,
                                            og * 512:(og + 1) * 512]
                                .rearrange("(c p) n -> p c n", p=128)),
                                inc=("dy", 16), wait=[("t", tv)])
                        dyv = q.cnt["dy"]
                        for nn in range(4):
                            for jh in range(2):
                                pp = pp2[(nn * 2 + jh) % 2]
                                for i in range(NI):
                                    q.op("t", lambda t, pp=pp, i=i, nn=nn, jh=jh: t.matmul(
                                        pp[:], wsl[:, i, nn * 128:(nn + 1) * 128],
                                        x_lnT[:, i, jh * 512:(jh + 1) * 512],
                                        start=(i == 0), stop=(i == NI - 1)),
                                        wait=[("dy", dyv), ("s", None)] if i == 0 else None,
                                        inc=("t", 1) if i == NI - 1 else None,
                                        tag=f"{sp} B {og}.{nn}.{jh}")
                                q.op("s", lambda s, pp=pp, og=og, nn=nn, jh=jh: s.activation(
                                    stfm[:, og * 4 + nn, jh * 512:(jh + 1) * 512], pp[:],
                                    AF.Tanh), wait=[("t", None)], inc=("s", 1))
                    q.flush()
                with ExitStack() as p2:
                    e2 = p2.enter_context
                    csl = e2(nc.sbuf_tensor(f"{sp}_csl", [128, 8, 512], F32))
                    dsl = e2(nc.sbuf_tensor(f"{sp}_dsl", [128, 4, 512], F32))
                    xts = [e2(nc.sbuf_tensor(f"{sp}_xts{j}", [128, D], F32)) for j in range(NT)]
                    pp2 = [e2(nc.psum_tensor(f"{sp}_pc{i}", [128, 512], F32)) for i in range(2)]
                    for j in range(NT):
                        q.op("y", lambda y, j=j: y.dma_start(
                            out=xts[j][:], in_=xres.ap()[j * 128:(j + 1) * 128, :]),
                            inc=("dy", 16), wait=[("s", None), ("t", None), ("v", None)],
                            tag=f"{sp} xload {j}")
                    for og in range(4):
                        tv = q.cnt["t"]
                        for hf in range(2):
                            q.op("y", lambda y, og=og, hf=hf: y.dma_start(
                                out=csl[:, hf * 4:(hf + 1) * 4, :],
                                in_=ct.ap()[hf * 512:(hf + 1) * 512,
                                            og * 512:(og + 1) * 512]
                                .rearrange("(c p) n -> p c n", p=128)),
                                inc=("dy", 16), wait=[("t", tv)])
                        q.op("y", lambda y, og=og: y.dma_start(
                            out=dsl[:], in_=ddiag.ap()[og]), inc=("dy", 16))
                        dyv = q.cnt["dy"]
                        for j in range(NT):
                            pp = pp2[j % 2]
                            for n8 in range(8):
                                q.op("t", lambda t, pp=pp, n8=n8, j=j: t.matmul(
                                    pp[:], stfm[:, n8, j * 128:(j + 1) * 128], csl[:, n8, :],
                                    start=(n8 == 0), stop=False),
                                    wait=[("dy", dyv), ("v", None)] if n8 == 0 else None,
                                    tag=f"{sp} C {j}.{og}")
                            for ii in range(4):
                                i = og * 4 + ii
                                q.op("t", lambda t, pp=pp, i=i, ii=ii, j=j: t.matmul(
                                    pp[:], x_lnT[:, i, j * 128:(j + 1) * 128], dsl[:, ii, :],
                                    start=False, stop=(ii == 3)),
                                    inc=("t", 1) if ii == 3 else None)
                            q.op("v", lambda v, pp=pp, og=og, j=j: v.tensor_tensor(
                                xts[j][:, og * 512:(og + 1) * 512],
                                xts[j][:, og * 512:(og + 1) * 512], pp[:], ALU.add),
                                wait=[("t", None)], inc=("v", 1))
                    for j in range(NT):
                        q.op("y", lambda y, j=j: y.dma_start(
                            out=xres.ap()[j * 128:(j + 1) * 128, :], in_=xts[j][:]),
                            wait=[("v", None)], inc=("dy", 16), tag=f"{sp} xst {j}")
                    q.flush()

        def wq():
            sp = "wq"
            with ExitStack() as ps:
                e = ps.enter_context
                wbuf = [e(nc.sbuf_tensor(f"{sp}_wb{i}", [128, DFF], F32)) for i in range(2)]
                acc = e(nc.sbuf_tensor(f"{sp}_acc", [128, 4], F32))
                ssc = e(nc.sbuf_tensor(f"{sp}_ssc", [1, 4], F32))
                psm = e(nc.psum_tensor(f"{sp}_psm", [1, 2], F32))
                psb = e(nc.psum_tensor(f"{sp}_psb", [128, 2], F32))
                u32 = e(nc.sbuf_tensor(f"{sp}_u32", [128, 2048], I32))
                uf = e(nc.sbuf_tensor(f"{sp}_uf", [128, 2048], F32))
                ti8 = e(nc.sbuf_tensor(f"{sp}_i8", [128, DFF], I8))
                q.op("v", lambda v: v.memset(acc[:], 0.0), inc=("v", 1))
                lastuse = [q.cnt["v"], q.cnt["v"]]
                for wi, (wsrc, nch, w_fd) in enumerate([(w1t, NI, DFF), (w2t, NO, D)]):
                    for chk in range(nch):
                        bi = chk % 2
                        b = wbuf[bi]
                        q.op("y", lambda y, b=b, wsrc=wsrc, chk=chk, w_fd=w_fd: y.dma_start(
                            out=b[:, 0:w_fd], in_=wsrc.ap()[chk * 128:(chk + 1) * 128, :]),
                            inc=("dy", 16),
                            wait=[("v", lastuse[bi]), ("t", None), ("s", None)],
                            tag=f"{sp} A {wi}.{chk}")
                        q.op("v", lambda v, b=b, w_fd=w_fd: v.tensor_reduce(
                            acc[:, 2:3], b[:, 0:w_fd], AX.X, ALU.add, apply_absolute_value=True),
                            wait=[("dy", None)], lock=True)
                        q.op("v", lambda v, wi=wi: v.tensor_tensor(
                            acc[:, wi:wi + 1], acc[:, wi:wi + 1], acc[:, 2:3], ALU.add),
                            lock=True)
                        lastuse[bi] = q.cnt["v"]
                q.op("t", lambda t: t.matmul(psm[:], onesc[:], acc[:, 0:2]),
                     wait=[("v", None)], inc=("t", 1))
                q.op("v", lambda v: v.tensor_scalar(ssc[:, 0:2], psm[:], 1.0 / (D * DFF), None,
                                                    ALU.mult), wait=[("t", None)], lock=True)
                q.op("v", lambda v: v.tensor_scalar(ssc[:, 0:2], ssc[:, 0:2], 1e-5, None, ALU.max),
                     lock=True)
                q.op("v", lambda v: v.reciprocal(ssc[:, 2:4], ssc[:, 0:2]), lock=True)
                q.op("t", lambda t: t.matmul(psb[:], onesr[:], ssc[:, 0:2]),
                     wait=[("v", None)], inc=("t", 1))
                q.op("v", lambda v: v.tensor_copy(swcl[:, 0:2], psb[:]),
                     wait=[("t", None)], inc=("v", 1))
                q.op("t", lambda t: t.matmul(psb[:], onesr[:], ssc[:, 2:4]),
                     wait=[("v", None)], inc=("t", 1))
                q.op("v", lambda v: v.tensor_copy(acc[:, 0:2], psb[:]),
                     wait=[("t", None)], inc=("v", 1))
                lastv = [q.cnt["v"], q.cnt["v"]]
                lastg = [q.cnt["g"], q.cnt["g"]]
                for wi, (wsrc, wdst, nch, w_fd) in enumerate(
                        [(w1t, w1q_h, NI, DFF), (w2t, w2q_h, NO, D)]):
                    for chk in range(nch):
                        bi = chk % 2
                        b = wbuf[bi]
                        eng = "v"
                        q.op("y", lambda y, b=b, wsrc=wsrc, chk=chk, w_fd=w_fd: y.dma_start(
                            out=b[:, 0:w_fd], in_=wsrc.ap()[chk * 128:(chk + 1) * 128, :]),
                            inc=("dy", 16),
                            wait=[("v", lastv[bi]), ("g", lastg[bi])], tag=f"{sp} B {wi}.{chk}")
                        dyv = q.cnt["dy"]
                        for pc in range(0, w_fd, 2048):
                            pw = min(2048, w_fd - pc)
                            q.op(eng, lambda x, b=b, pc=pc, pw=pw, wi=wi: x.tensor_scalar(
                                uf[:, 0:pw], b[:, pc:pc + pw], acc[:, wi:wi + 1], None, ALU.mult),
                                wait=[("dy", dyv)] + ([("g", None)] if eng == "v" else [("v", None)]))
                            q.op(eng, lambda x, pw=pw: x.tensor_copy(u32[:, 0:pw], uf[:, 0:pw]))
                            q.op(eng, lambda x, pw=pw: x.tensor_scalar(
                                u32[:, 0:pw], u32[:, 0:pw], -1, 1, ALU.max, ALU.min))
                            q.op(eng, lambda x, pc=pc, pw=pw: x.tensor_copy(
                                ti8[:, pc:pc + pw], u32[:, 0:pw]), inc=(eng, 1))
                        if eng == "v":
                            lastv[bi] = q.cnt["v"]
                        else:
                            lastg[bi] = q.cnt["g"]
                        q.op("y", lambda y, wdst=wdst, chk=chk, w_fd=w_fd: y.dma_start(
                            out=wdst.ap()[chk * 128:(chk + 1) * 128, :], in_=ti8[:, 0:w_fd]),
                            wait=[(eng, None)], inc=("dy", 16))
                q.flush()

        def l1f(c):
            sp = f"l1f{c}"
            with ExitStack() as ps:
                e = ps.enter_context
                combs = e(nc.sbuf_tensor(f"{sp}_cmb", [128, NT, 2], F32))
                msp = e(nc.sbuf_tensor(f"{sp}_msp", [128, NT, 16], F32))
                pab = ExitStack()
                xqT = pab.enter_context(nc.sbuf_tensor(f"{sp}_xqT", [128, NI, TOK], BF16))
                with ExitStack() as p2:
                    e2 = p2.enter_context
                    g1s = e2(nc.sbuf_tensor(f"{sp}_g1", [128, D], F32))
                    xt = e2(nc.sbuf_tensor(f"{sp}_xta", [128, D], F32))
                    lnb = e2(nc.sbuf_tensor(f"{sp}_lnb", [128, D], F32))
                    sq = e2(nc.sbuf_tensor(f"{sp}_sq", [128, D], F32))
                    kb = e2(nc.sbuf_tensor(f"{sp}_kb", [128, D], BF16))
                    k32 = e2(nc.sbuf_tensor(f"{sp}_k32", [128, D], I32))
                    st = e2(nc.sbuf_tensor(f"{sp}_sta", [128, 16], F32))
                    ptb = [e2(nc.psum_tensor(f"{sp}_ptb{i}", [128, 4, 128], F32)) for i in range(2)]
                    q.op("y", lambda y: y.dma_start(out=g1s[:], in_=g1b_h.ap()), inc=("dy", 16))
                    q.op("s", lambda s: s.preload_act_table(AF.Sqrt), inc=("s", 1))
                    for j in range(NT):
                        q.op("y", lambda y, j=j: y.dma_start(
                            out=xt[:], in_=xres.ap()[j * 128:(j + 1) * 128, :]),
                            inc=("dy", 16), wait=[("v", None), ("s", None), ("t", None)],
                            tag=f"{sp} a {j}")
                        dyv = q.cnt["dy"]
                        q.op("v", lambda v: v.reduce_sum(st[:, 0:1], xt[:], AX.X),
                             wait=[("dy", dyv)], lock=True)
                        q.op("s", lambda s: s.activation(sq[:], xt[:], AF.Square,
                                                         accum_out=st[:, 1:2]),
                             wait=[("dy", dyv)], inc=("s", 1))
                        q.op("v", lambda v: v.tensor_scalar(st[:, 2:3], st[:, 0:1], 1.0 / D, None, ALU.mult),
                             wait=[("s", None)], lock=True)
                        q.op("v", lambda v: v.tensor_scalar(st[:, 3:4], st[:, 1:2], 1.0 / D, None, ALU.mult),
                             lock=True)
                        q.op("v", lambda v: v.tensor_tensor(st[:, 4:5], st[:, 2:3], st[:, 2:3], ALU.mult),
                             lock=True)
                        q.op("v", lambda v: v.tensor_tensor(st[:, 5:6], st[:, 3:4], st[:, 4:5], ALU.subtract),
                             lock=True)
                        q.op("v", lambda v: v.tensor_scalar(st[:, 5:6], st[:, 5:6], 1e-5, None, ALU.add),
                             lock=True)
                        q.op("s", lambda s: s.activation(st[:, 6:7], st[:, 5:6], AF.Sqrt),
                             wait=[("v", None)], inc=("s", 1))
                        q.op("v", lambda v: v.reciprocal(st[:, 7:8], st[:, 6:7]), wait=[("s", None)],
                             lock=True)
                        q.op("v", lambda v: v.scalar_tensor_tensor(
                            st[:, 8:9], st[:, 2:3], -1.0, st[:, 7:8], ALU.mult, ALU.mult),
                            lock=True)
                        q.op("s", lambda s: s.activation(lnb[:], xt[:], AF.Identity,
                                                         bias=st[:, 8:9], scale=st[:, 7:8]),
                             wait=[("v", None)], inc=("s", 1))
                        q.op("s", lambda s: s.activation(sq[:], lnb[:], AF.Square,
                                                         accum_out=st[:, 9:10]), inc=("s", 1))
                        q.op("v", lambda v: v.tensor_scalar(st[:, 10:11], st[:, 9:10], 1.0 / D,
                                                            None, ALU.mult), wait=[("s", None)],
                             lock=True)
                        q.op("v", lambda v: v.tensor_scalar(st[:, 10:11], st[:, 10:11], 1e-6,
                                                            None, ALU.add), lock=True)
                        q.op("s", lambda s: s.activation(st[:, 11:12], st[:, 10:11], AF.Sqrt),
                             wait=[("v", None)], inc=("s", 1))
                        q.op("v", lambda v: v.reciprocal(st[:, 12:13], st[:, 11:12]),
                             wait=[("s", None)], lock=True)
                        q.op("v", lambda v: v.scalar_tensor_tensor(
                            lnb[:], lnb[:], st[:, 12:13], g1s[:], ALU.mult, ALU.mult), lock=True)
                        q.op("v", lambda v: v.tensor_reduce(st[:, 13:14], lnb[:], AX.X, ALU.max,
                                                            apply_absolute_value=True), lock=True)
                        q.op("v", lambda v: v.tensor_scalar(st[:, 13:14], st[:, 13:14], 1e-5,
                                                            None, ALU.max), lock=True)
                        q.op("v", lambda v: v.reciprocal(st[:, 14:15], st[:, 13:14]), lock=True)
                        q.op("v", lambda v: v.tensor_scalar(st[:, 14:15], st[:, 14:15], 127.0,
                                                            None, ALU.mult), lock=True)
                        q.op("v", lambda v, j=j: v.tensor_scalar(
                            combs[:, j, 0:1], st[:, 13:14], 1.0 / 127.0, None, ALU.mult), lock=True)
                        q.op("v", lambda v, j=j: v.tensor_tensor(
                            combs[:, j, 0:1], combs[:, j, 0:1], swcl[:, 0:1], ALU.mult), lock=True)
                        if L1FA_MODE >= 2:
                            q.op("v", lambda v, j=j: v.memset(
                                xqT[:, :, j * 128:(j + 1) * 128], 1.0), inc=("v", 1))
                            continue
                        q.op("v", lambda v: v.tensor_scalar(lnb[:], lnb[:], st[:, 14:15], None,
                                                            ALU.mult), lock=True)
                        q.op("v", lambda v: v.tensor_copy(k32[:], lnb[:]))
                        q.op("v", lambda v: v.tensor_copy(kb[:], k32[:]), inc=("v", 1))
                        vv = q.cnt["v"]
                        if L1FA_MODE >= 1:
                            q.op("v", lambda v, j=j: v.memset(
                                xqT[:, :, j * 128:(j + 1) * 128], 1.0),
                                wait=[("v", vv)], inc=("v", 1))
                            continue
                        for i0 in range(0, NI, 4):
                            pp = ptb[(i0 // 4) % 2]
                            for i in range(i0, i0 + 4):
                                q.op("t", lambda t, pp=pp, i=i, i0=i0: t.matmul(
                                    pp[:, i - i0, :], kb[:, i * 128:(i + 1) * 128], identb[:],
                                    start=True, stop=True),
                                    wait=([("v", None)] if i == i0 else None),
                                    inc=("t", 1) if i == i0 + 3 else None)
                            q.op("v", lambda v, pp=pp, i0=i0, j=j: v.tensor_copy(
                                xqT[:, i0:i0 + 4, j * 128:(j + 1) * 128], pp[:]),
                                wait=[("t", None)], inc=("v", 1))
                    q.flush()
                if L1F_CUT < 2:
                    pab.close()
                    return

                with ExitStack() as p2:
                    e2 = p2.enter_context
                    w1s = [e2(nc.sbuf_tensor(f"{sp}_w1s{i}", [128, NI, 512], BF16)) for i in range(2)]
                    hsb = [e2(nc.sbuf_tensor(f"{sp}_hsb{i}", [128, 512], F32)) for i in range(2)]
                    hscr = e2(nc.sbuf_tensor(f"{sp}_hscr", [128, 512], F32))
                    g2s = e2(nc.sbuf_tensor(f"{sp}_g2", [128, DFF], F32))
                    pph = [e2(nc.psum_tensor(f"{sp}_pph{i}", [128, 512], F32)) for i in range(2)]
                    q.op("y", lambda y: y.dma_start(out=g2s[:], in_=g2b_h.ap()),
                         inc=("dy", 16), wait=[("t", None), ("v", None)])
                    wlast = [q.cnt["t"], q.cnt["t"]]
                    hlast = [q.cnt["dy"], q.cnt["dy"]]
                    for og in range(16):
                        wb = w1s[og % 2]
                        for hf in range(2):
                            q.op("g", lambda g, og=og, wb=wb, hf=hf: g.dma_start(
                                out=wb[:, hf * 8:(hf + 1) * 8, :],
                                in_=w1q_h.ap()[hf * 1024:(hf + 1) * 1024,
                                               og * 512:(og + 1) * 512]
                                .rearrange("(c p) n -> p c n", p=128)),
                                inc=("dg", 16), wait=[("t", wlast[og % 2])],
                                tag=f"{sp} w1 {og}.{hf}")
                        dgv = q.cnt["dg"]
                        for j in range(NT):
                            pp = pph[j % 2]
                            hb = hsb[j % 2]
                            for i in range(NI):
                                q.op("t", lambda t, pp=pp, wb=wb, i=i, j=j: t.matmul(
                                    pp[:], xqT[:, i, j * 128:(j + 1) * 128], wb[:, i, :],
                                    start=(i == 0), stop=(i == NI - 1)),
                                    wait=[("dg", dgv), ("s", None)] if i == 0 else None,
                                    inc=("t", 1) if i == NI - 1 else None,
                                    tag=f"{sp} h {og}.{j}")
                            q.op("s", lambda s, pp=pp, hb=hb, j=j: s.activation(
                                hb[:], pp[:], AF.Relu, scale=combs[:, j, 0:1]),
                                wait=[("t", None), ("dy", hlast[j % 2])], inc=("s", 1))
                            q.op("s", lambda s, hb=hb: s.activation(hb[:], hb[:], AF.Square),
                                 inc=("s", 1))
                            q.op("s", lambda s, hb=hb, j=j, og=og: s.activation(
                                hscr[:], hb[:], AF.Square, accum_out=msp[:, j, og:og + 1]),
                                inc=("s", 1))
                            q.op("v", lambda v, hb=hb, og=og: v.tensor_tensor(
                                hb[:], hb[:], g2s[:, og * 512:(og + 1) * 512], ALU.mult),
                                wait=[("s", None)], inc=("v", 1))
                            q.op("y", lambda y, hb=hb, j=j, og=og: y.dma_start(
                                out=hbuf.ap()[j * 128:(j + 1) * 128, og * 512:(og + 1) * 512],
                                in_=hb[:]), wait=[("v", None)], inc=("dy", 16))
                            hlast[j % 2] = q.cnt["dy"]
                        wlast[og % 2] = q.cnt["t"]
                    q.flush()
                pab.close()
                if L1F_CUT < 3:
                    return

                with ExitStack() as p2:
                    e2 = p2.enter_context
                    hqT = e2(nc.sbuf_tensor(f"{sp}_hqT", [128, NO, 4, 128], BF16))
                    hti = e2(nc.sbuf_tensor(f"{sp}_ht", [128, DFF], F32))
                    kb2 = e2(nc.sbuf_tensor(f"{sp}_kb2", [128, 2048], BF16))
                    k322 = e2(nc.sbuf_tensor(f"{sp}_k322", [128, 2048], I32))
                    st2 = e2(nc.sbuf_tensor(f"{sp}_st2", [128, 8], F32))
                    w2s = e2(nc.sbuf_tensor(f"{sp}_w2s", [128, NO, 512], BF16))
                    xtl = [e2(nc.sbuf_tensor(f"{sp}_xtl{i}", [128, D], F32)) for i in range(4)]
                    ptb2 = [e2(nc.psum_tensor(f"{sp}_pt2{i}", [128, 4, 128], F32)) for i in range(2)]
                    ppo = [e2(nc.psum_tensor(f"{sp}_ppo{i}", [128, 512], F32)) for i in range(2)]
                    for half in range(2):
                        jl = [half * 4 + jj for jj in range(4)]
                        for jj, j in enumerate(jl):
                            q.op("y", lambda y, j=j: y.dma_start(
                                out=hti[:], in_=xres.ap()[0:0 + 128, :]) if False else y.dma_start(
                                out=hti[:], in_=hbuf.ap()[j * 128:(j + 1) * 128, :]),
                                inc=("dy", 16), wait=[("v", None), ("t", None)],
                                tag=f"{sp} hload {j}")
                            dyv = q.cnt["dy"]
                            q.op("v", lambda v, j=j: v.reduce_sum(st2[:, 0:1], msp[:, j, :], AX.X),
                                 wait=[("dy", dyv)], lock=True)
                            q.op("v", lambda v: v.tensor_scalar(st2[:, 0:1], st2[:, 0:1],
                                                                1.0 / DFF, None, ALU.mult), lock=True)
                            q.op("v", lambda v: v.tensor_scalar(st2[:, 0:1], st2[:, 0:1], 1e-6,
                                                                None, ALU.add), lock=True)
                            q.op("s", lambda s: s.activation(st2[:, 1:2], st2[:, 0:1], AF.Sqrt),
                                 wait=[("v", None)], inc=("s", 1))
                            q.op("v", lambda v: v.reciprocal(st2[:, 2:3], st2[:, 1:2]),
                                 wait=[("s", None)], lock=True)
                            q.op("v", lambda v: v.tensor_scalar(
                                hti[:], hti[:], st2[:, 2:3], None, ALU.mult), lock=True)
                            q.op("v", lambda v: v.tensor_reduce(st2[:, 3:4], hti[:], AX.X,
                                                                ALU.max, apply_absolute_value=True),
                                lock=True)
                            q.op("v", lambda v: v.tensor_scalar(st2[:, 3:4], st2[:, 3:4], 1e-5,
                                                                None, ALU.max), lock=True)
                            q.op("v", lambda v: v.reciprocal(st2[:, 4:5], st2[:, 3:4]), lock=True)
                            q.op("v", lambda v: v.tensor_scalar(st2[:, 4:5], st2[:, 4:5], 127.0,
                                                                None, ALU.mult), lock=True)
                            q.op("v", lambda v, j=j: v.tensor_scalar(
                                combs[:, j, 1:2], st2[:, 3:4], 0.5 / 127.0, None, ALU.mult), lock=True)
                            q.op("v", lambda v, j=j: v.tensor_tensor(
                                combs[:, j, 1:2], combs[:, j, 1:2], swcl[:, 1:2], ALU.mult), lock=True)
                            q.op("v", lambda v: v.tensor_scalar(hti[:], hti[:], st2[:, 4:5],
                                                                None, ALU.mult), lock=True)
                            for pc in range(0, DFF, 2048):
                                q.op("v", lambda v, pc=pc: v.tensor_copy(k322[:], hti[:, pc:pc + 2048]))
                                q.op("v", lambda v, pc=pc: v.tensor_copy(kb2[:], k322[:]),
                                     inc=("v", 1))
                                vv = q.cnt["v"]
                                for o0 in range(0, 16, 4):
                                    pp = ptb2[(o0 // 4) % 2]
                                    for oc in range(o0, o0 + 4):
                                        q.op("t", lambda t, pp=pp, oc=oc, o0=o0: t.matmul(
                                            pp[:, oc - o0, :], kb2[:, oc * 128:(oc + 1) * 128],
                                            identb[:], start=True, stop=True),
                                            wait=([("v", None)] if oc == o0 else None),
                                            inc=("t", 1) if oc == o0 + 3 else None)
                                    q.op("v", lambda v, pp=pp, pc=pc, o0=o0, jj=jj: v.tensor_copy(
                                        hqT[:, pc // 128 + o0:pc // 128 + o0 + 4, jj, :], pp[:]),
                                        wait=[("t", None)], inc=("v", 1))
                            q.op("y", lambda y, jj=jj, j=j: y.dma_start(
                                out=xtl[jj][:], in_=xres.ap()[j * 128:(j + 1) * 128, :]),
                                inc=("dy", 16))
                        wlast = q.cnt["t"]
                        for dmc in range(4):
                            for qt in range(4):
                                q.op("g", lambda g, dmc=dmc, qt=qt: g.dma_start(
                                    out=w2s[:, qt * 16:(qt + 1) * 16, :],
                                    in_=w2q_h.ap()[qt * 2048:(qt + 1) * 2048,
                                                   dmc * 512:(dmc + 1) * 512]
                                    .rearrange("(c p) n -> p c n", p=128)),
                                    inc=("dg", 16), wait=[("t", None)],
                                    tag=f"{sp} w2 {half}.{dmc}.{qt}")
                            dgv = q.cnt["dg"]
                            for jj, j in enumerate(jl):
                                pp = ppo[jj % 2]
                                for oc in range(NO):
                                    q.op("t", lambda t, pp=pp, oc=oc, jj=jj: t.matmul(
                                        pp[:], hqT[:, oc, jj, :], w2s[:, oc, :],
                                        start=(oc == 0), stop=(oc == NO - 1)),
                                        wait=[("dg", dgv), ("v", None)] if oc == 0 else None,
                                        inc=("t", 1) if oc == NO - 1 else None,
                                        tag=f"{sp} y2 {half}.{dmc}.{jj}")
                                q.op("v", lambda v, pp=pp, jj=jj, j=j, dmc=dmc: v.scalar_tensor_tensor(
                                    xtl[jj][:, dmc * 512:(dmc + 1) * 512], pp[:],
                                    combs[:, j, 1:2], xtl[jj][:, dmc * 512:(dmc + 1) * 512],
                                    ALU.mult, ALU.add),
                                    wait=[("t", None), ("dy", None)], inc=("v", 1))
                        for jj, j in enumerate(jl):
                            q.op("y", lambda y, jj=jj, j=j: y.dma_start(
                                out=xres.ap()[j * 128:(j + 1) * 128, :], in_=xtl[jj][:]),
                                wait=[("v", None)], inc=("dy", 16), tag=f"{sp} xst {j}")
                    q.flush()

        def l1a():
            sp = "l1a"
            with ExitStack() as ps:
                e = ps.enter_context
                x_lnT = e(nc.sbuf_tensor(f"{sp}_xlnT", [128, NI, TOK], F32))
                ln_transpose(xres, x_lnT, sp)
                q.op("s", lambda s: s.preload_act_table(AF.Exp), inc=("s", 1))
                with ExitStack() as p2:
                    e2 = p2.enter_context
                    wsl = e2(nc.sbuf_tensor(f"{sp}_wsl", [128, NI, 512], F32))
                    kvb = e2(nc.sbuf_tensor(f"{sp}_kvb", [128, 1024], F32))
                    pp2 = [e2(nc.psum_tensor(f"{sp}_pk{i}", [128, 512], F32)) for i in range(2)]
                    for og in range(4):
                        tv = q.cnt["t"]
                        for hf in range(2):
                            q.op("y", lambda y, og=og, hf=hf: y.dma_start(
                                out=wsl[:, hf * 8:(hf + 1) * 8, :],
                                in_=wkt.ap()[hf * 1024:(hf + 1) * 1024,
                                             og * 512:(og + 1) * 512]
                                .rearrange("(c p) n -> p c n", p=128)),
                                inc=("dy", 16), wait=[("t", tv)], tag=f"{sp} wk {og}.{hf}")
                        dyv = q.cnt["dy"]
                        for hc in range(4):
                            for jh in range(2):
                                pp = pp2[jh]
                                for i in range(NI):
                                    q.op("t", lambda t, pp=pp, i=i, hc=hc, jh=jh: t.matmul(
                                        pp[:], wsl[:, i, hc * 128:(hc + 1) * 128],
                                        x_lnT[:, i, jh * 512:(jh + 1) * 512],
                                        start=(i == 0), stop=(i == NI - 1)),
                                        wait=[("dy", dyv), ("v", None)] if i == 0 else None,
                                        inc=("t", 1) if i == NI - 1 else None,
                                        tag=f"{sp} K {og}.{hc}.{jh}")
                                q.op("v", lambda v, pp=pp, jh=jh: v.tensor_copy(
                                    kvb[:, jh * 512:(jh + 1) * 512], pp[:]),
                                    wait=[("t", None)], inc=("v", 1))
                            gh = og * 4 + hc
                            q.op("y", lambda y, gh=gh: y.dma_start(
                                out=kg_in.ap()[gh * 128:(gh + 1) * 128, :], in_=kvb[:]),
                                wait=[("v", None)], inc=("dy", 16))
                    for og in range(4):
                        tv = q.cnt["t"]
                        for hf in range(2):
                            q.op("y", lambda y, og=og, hf=hf: y.dma_start(
                                out=wsl[:, hf * 8:(hf + 1) * 8, :],
                                in_=wvt.ap()[hf * 1024:(hf + 1) * 1024,
                                             og * 512:(og + 1) * 512]
                                .rearrange("(c p) n -> p c n", p=128)),
                                inc=("dy", 16), wait=[("t", tv), ("dy", None)],
                                tag=f"{sp} wv {og}.{hf}")
                        dyv = q.cnt["dy"]
                        for j in range(NT):
                            pp = pp2[j % 2]
                            for i in range(NI):
                                q.op("t", lambda t, pp=pp, i=i, j=j: t.matmul(
                                    pp[:], x_lnT[:, i, j * 128:(j + 1) * 128], wsl[:, i, :],
                                    start=(i == 0), stop=(i == NI - 1)),
                                    wait=[("dy", dyv), ("v", None)] if i == 0 else None,
                                    inc=("t", 1) if i == NI - 1 else None,
                                    tag=f"{sp} V {og}.{j}")
                            q.op("v", lambda v, pp=pp: v.tensor_copy(kvb[:, 0:512], pp[:]),
                                 wait=[("t", None)], inc=("v", 1))
                            q.op("y", lambda y, j=j, og=og: y.dma_start(
                                out=vg_in.ap()[j * 128:(j + 1) * 128, og * 512:(og + 1) * 512],
                                in_=kvb[:, 0:512]), wait=[("v", None)], inc=("dy", 16))
                    dyb = q.cnt["dy"]
                    q.op("g", lambda g: g.collective_compute(
                        "AllGather", ALU.bypass, replica_groups=REPG,
                        ins=[kg_in.ap().opt()], outs=[kg_out.ap().opt()]),
                        inc=("cc", 1), wait=[("dy", dyb)], tag=f"{sp} AGK")
                    q.op("g", lambda g: g.collective_compute(
                        "AllGather", ALU.bypass, replica_groups=REPG,
                        ins=[vg_in.ap().opt()], outs=[vg_out.ap().opt()]),
                        inc=("cc", 1), tag=f"{sp} AGV")
                    q.flush()
                ccv = q.cnt["cc"]

                qh = e(nc.sbuf_tensor(f"{sp}_qh", [128, H, 4, 128], F32))
                yfm = e(nc.sbuf_tensor(f"{sp}_yfm", [128, H, 4, 128], F32))
                for half in range(2):
                    jl = [half * 4 + jj for jj in range(4)]
                    with ExitStack() as p2:
                        e2 = p2.enter_context
                        wslq = e2(nc.sbuf_tensor(f"{sp}_wsq{half}", [128, NI, 512], F32))
                        ppq = [e2(nc.psum_tensor(f"{sp}_pq{half}{i}", [128, 512], F32))
                               for i in range(2)]
                        for og in range(4):
                            tv = q.cnt["t"]
                            for hf in range(2):
                                q.op("y", lambda y, og=og, hf=hf: y.dma_start(
                                    out=wslq[:, hf * 8:(hf + 1) * 8, :],
                                    in_=wqt.ap()[hf * 1024:(hf + 1) * 1024,
                                                 og * 512:(og + 1) * 512]
                                    .rearrange("(c p) n -> p c n", p=128)),
                                    inc=("dy", 16), wait=[("t", tv)],
                                    tag=f"{sp} wq{half} {og}.{hf}")
                            dyv = q.cnt["dy"]
                            for hc in range(4):
                                h = og * 4 + hc
                                pp = ppq[hc % 2]
                                for i in range(NI):
                                    q.op("t", lambda t, pp=pp, i=i, hc=hc, half=half: t.matmul(
                                        pp[:], wslq[:, i, hc * 128:(hc + 1) * 128],
                                        x_lnT[:, i, half * 512:(half + 1) * 512],
                                        start=(i == 0), stop=(i == NI - 1)),
                                        wait=[("dy", dyv), ("v", None)] if i == 0 else None,
                                        inc=("t", 1) if i == NI - 1 else None,
                                        tag=f"{sp} Q {half}.{h}")
                                q.op("v", lambda v, pp=pp, h=h: v.tensor_copy(
                                    qh[:, h, :, :], pp[:].rearrange("p (a b) -> p a b", a=4)),
                                    wait=[("t", None)], inc=("v", 1))
                        q.flush()

                    with ExitStack() as p2:
                        e2 = p2.enter_context
                        kfh = [e2(nc.sbuf_tensor(f"{sp}_kfh{half}{i}", [128, 16, 128], F32))
                               for i in range(2)]
                        vth = [e2(nc.sbuf_tensor(f"{sp}_vth{half}{i}", [128, 16, 128], F32))
                               for i in range(2)]
                        bufs = make_attn_bufs(f"{sp}h{half}", e2)
                        kvt_last = [0, 0]
                        dy_kv = [0] * (H + 1)

                        def loadkv(h, w):
                            buf = h % 2
                            q.op("y", lambda y, h=h, buf=buf: y.dma_start(
                                out=kfh[buf][:, 0:8, :],
                                in_=kg_out.ap()[h * 128:(h + 1) * 128, :]
                                .rearrange("p (a b) -> p a b", a=8)),
                                inc=("dy", 16), wait=w, tag=f"{sp} kv {half}.{h}")
                            q.op("y", lambda y, h=h, buf=buf: y.dma_start(
                                out=kfh[buf][:, 8:16, :],
                                in_=kg_out.ap()[D + h * 128:D + (h + 1) * 128, :]
                                .rearrange("p (a b) -> p a b", a=8)), inc=("dy", 16))
                            q.op("y", lambda y, h=h, buf=buf: y.dma_start(
                                out=vth[buf][:], in_=vg_out.ap()[:, h * 128:(h + 1) * 128]
                                .rearrange("(a p) n -> p a n", p=128)), inc=("dy", 16))
                            dy_kv[h] = q.cnt["dy"]

                        loadkv(0, [("cc", ccv), ("t", None), ("v", None)])
                        for h in range(H):
                            if h + 1 < H:
                                loadkv(h + 1, [("t", kvt_last[(h + 1) % 2])])
                            for jj, j in enumerate(jl):
                                def y_emit(psy, rme, h=h, jj=jj):
                                    q.op("v", lambda v, h=h, jj=jj, psy=psy: v.tensor_copy(
                                        yfm[:, h, jj, :], psy[:]),
                                        wait=[("t", None)], inc=("v", 1))
                                attn_unit(bufs, qh[:, h, jj, :], kfh[h % 2], vth[h % 2],
                                          j + 1, y_emit, scale_p=True,
                                          xwaits=[("dy", dy_kv[h])])
                            kvt_last[h % 2] = q.cnt["t"]
                        q.flush()

                    with ExitStack() as p2:
                        e2 = p2.enter_context
                        wslo = e2(nc.sbuf_tensor(f"{sp}_wso{half}", [128, NI, 512], F32))
                        xtl = [e2(nc.sbuf_tensor(f"{sp}_xo{half}{i}", [128, D], F32))
                               for i in range(4)]
                        ppo = [e2(nc.psum_tensor(f"{sp}_po{half}{i}", [128, 512], F32))
                               for i in range(2)]
                        for jj, j in enumerate(jl):
                            q.op("y", lambda y, jj=jj, j=j: y.dma_start(
                                out=xtl[jj][:], in_=xres.ap()[j * 128:(j + 1) * 128, :]),
                                inc=("dy", 16), wait=[("t", None), ("v", None)])
                        for og in range(4):
                            tv = q.cnt["t"]
                            for hf in range(2):
                                q.op("y", lambda y, og=og, hf=hf: y.dma_start(
                                    out=wslo[:, hf * 8:(hf + 1) * 8, :],
                                    in_=wot.ap()[hf * 1024:(hf + 1) * 1024,
                                                 og * 512:(og + 1) * 512]
                                    .rearrange("(c p) n -> p c n", p=128)),
                                    inc=("dy", 16), wait=[("t", tv)],
                                    tag=f"{sp} wo{half} {og}.{hf}")
                            dyv = q.cnt["dy"]
                            for jj, j in enumerate(jl):
                                pp = ppo[jj % 2]
                                for hh in range(H):
                                    q.op("t", lambda t, pp=pp, hh=hh, jj=jj: t.matmul(
                                        pp[:], yfm[:, hh, jj, :], wslo[:, hh, :],
                                        start=(hh == 0), stop=(hh == H - 1)),
                                        wait=[("dy", dyv), ("v", None)] if hh == 0 else None,
                                        inc=("t", 1) if hh == H - 1 else None,
                                        tag=f"{sp} WO {half}.{og}.{jj}")
                                q.op("v", lambda v, pp=pp, jj=jj, og=og: v.tensor_tensor(
                                    xtl[jj][:, og * 512:(og + 1) * 512],
                                    xtl[jj][:, og * 512:(og + 1) * 512], pp[:], ALU.add),
                                    wait=[("t", None), ("dy", None)], inc=("v", 1))
                        for jj, j in enumerate(jl):
                            q.op("y", lambda y, jj=jj, j=j: y.dma_start(
                                out=xres.ap()[j * 128:(j + 1) * 128, :], in_=xtl[jj][:]),
                                wait=[("v", None)], inc=("dy", 16), tag=f"{sp} xst {j}")
                        q.flush()

        if n_sub >= 1:
            l0f(0, first=True)
        if n_sub >= 2:
            l0a()
        if n_sub >= 3:
            l0f(1, first=False)
        if n_sub >= 4:
            wq()
        if n_sub >= 5:
            l1f(0)
        if n_sub >= 6:
            l1a()
        if n_sub >= 7:
            l1f(1)

        src = xres if (n_sub >= 1 and L0F_CUT >= 5) else xin
        q.op("y", lambda y: y.dma_start(out=out_h.ap(), in_=src.ap()),
             wait=[("v", None), ("s", None), ("t", None), ("dy", None), ("dg", None),
                   ("cc", None), ("g", None)], inc=("dy", 16), tag="final out")
        fin = q.cnt["dy"]
        q.op("y", lambda y: y.nop(), wait=[("dy", fin)], tag="final wait")
        q.flush()
        q.audit()
    return nc


def _prep(inputs):
    f32 = np.float32
    x = np.asarray(inputs["L0_attn_x"], f32)
    Bm = np.asarray(inputs["L0_attn_B"], f32)
    Cm = np.asarray(inputs["L0_attn_C"], f32)
    Dv = np.asarray(inputs["L0_attn_D"], f32)
    Wg = np.asarray(inputs["L0_ffn_W_gate"], f32)
    Wqkv = np.asarray(inputs["L0_ffn_W_qkv"], f32)
    WK = np.asarray(inputs["L1_attn_W_K"], f32)
    WQ = np.asarray(inputs["L1_attn_W_Q"], f32)
    WV = np.asarray(inputs["L1_attn_W_V"], f32)
    WO = np.asarray(inputs["L1_attn_W_O"], f32)
    w1 = np.asarray(inputs["L1_ffn_w1"], f32)
    g1 = np.asarray(inputs["L1_ffn_g1"], f32)
    w2 = np.asarray(inputs["L1_ffn_w2"], f32)
    g2 = np.asarray(inputs["L1_ffn_g2"], f32)

    C = np.ascontiguousarray
    shared = {
        "wg": C(Wg.T),
        "wqkv": C(Wqkv.T),
        "bt": C(Bm.T),
        "ct": C(Cm.T),
        "wkt": C(WK.reshape(D, D).T),
        "wqt": C(WQ.reshape(D, D).T),
        "wvt": C(WV.reshape(D, D).T),
        "wot": C(WO.T),
        "w1t": C(w1.T),
        "w2t": C(w2.T),
        "g1b": C(np.broadcast_to(g1, (128, D))),
        "g2b": C(np.broadcast_to(g2, (128, DFF))),
        "identf": np.eye(128, dtype=f32),
        "identb": np.eye(128).astype(ml_dtypes.bfloat16),
    }
    dd = np.zeros((4, 128, 4, 512), f32)
    idx = np.arange(128)
    for og in range(4):
        for ii in range(4):
            dd[og, idx, ii, ii * 128 + idx] = Dv[og * 512 + ii * 128 + idx]
    shared["ddiag"] = dd
    cstv = np.zeros((128, 8), f32)
    cstv[:, 0] = ISQ
    cstv[:, 1] = 1.0
    shared["cst"] = cstv

    tril = np.where(np.arange(128)[None, :] <= np.arange(128)[:, None], 0.0, -1e9).astype(f32)
    zeros = np.zeros((128, 128), f32)
    full = np.full((128, 128), -1e9, f32)

    in_maps = []
    for core in range(8):
        b, h = core // 2, core % 2
        xs = C(x[b].reshape(16, 128, D)[h::2].reshape(TOK, D))
        m = dict(shared)
        m["xin"] = xs
        m["maska"] = tril if h == 0 else zeros
        m["maskb"] = full if h == 0 else tril
        in_maps.append(m)
    return x.shape, in_maps


def kernel(**inputs):
    shape, in_maps = _prep(inputs)
    nc = build()
    res = run_bass_kernel_spmd(nc, in_maps, core_ids=list(range(8)))
    out = np.empty(shape, np.float32)
    for core in range(8):
        b, h = core // 2, core % 2
        out[b].reshape(16, 128, D)[h::2] = res.results[core]["out"].reshape(NT, 128, D)
    return out
